# version 08
# baseline (speedup 1.0000x reference)
import numpy as np
import ml_dtypes
import concourse.bass as bass
import concourse.mybir as mybir
from concourse.bass_utils import run_bass_kernel_spmd

F32 = mybir.dt.float32
BF16 = mybir.dt.bfloat16
AF = mybir.ActivationFunctionType
ALU = mybir.AluOpType
AX = mybir.AxisListType

BASE_LR_INV = float(np.log(np.expm1(0.01)))
NS_A, NS_B, NS_C = 3.4445, -4.775, 2.0315

B, L, D = 4, 2048, 2048
SERIAL = 0
SCAN_LIMIT = 99
QP_LIMIT = 99
QH = 99
CH, NCH, KD, TOK = 512, 4, 16, 1024
BF = ml_dtypes.bfloat16


class Sched:
    ENGS = ("t", "v", "s", "g", "y")

    def __init__(self):
        self.q = {e: [] for e in self.ENGS}
        self.cnt = {"t": 0, "v": 0, "s": 0, "g": 0,
                    "dy": 0, "ds": 0, "dg": 0, "cc": 0}

    PAIR = {2: (("t", "v"), ("t", "s"), ("t", "g"), ("v", "t"), ("s", "t"), ("g", "t")),
            3: (("t", "v"), ("v", "t")), 4: (("t", "s"), ("s", "t")),
            5: (("t", "g"), ("g", "t")), 6: (("v", "s"), ("s", "v"))}

    def op(self, eng, fn, deps=()):
        if SERIAL == 1 and getattr(self, "last", None):
            deps = tuple(deps) + (self.last,)
        elif SERIAL >= 2:
            lasts = getattr(self, "lasts", {})
            for (a, b) in self.PAIR[SERIAL]:
                if eng == a and b in lasts:
                    deps = tuple(deps) + (lasts[b],)
        self.cnt[eng] += 1
        self.q[eng].append((fn, tuple(deps), (eng, 1)))
        self.last = (eng, self.cnt[eng])
        if not hasattr(self, "lasts"):
            self.lasts = {}
        self.lasts[eng] = (eng, self.cnt[eng])
        return (eng, self.cnt[eng])

    def dma(self, queue, fn, deps=()):
        sem = {"y": "dy", "s": "ds", "g": "dg"}[queue]
        if SERIAL and getattr(self, "last", None):
            deps = tuple(deps) + (self.last,)
        deps = tuple(deps) + ((sem, self.cnt[sem]),)
        self.cnt[sem] += 16
        self.q[queue].append((fn, deps, (sem, 16)))
        if SERIAL:
            self.last = (sem, self.cnt[sem])
        return (sem, self.cnt[sem])

    def coll(self, fn, deps=()):
        self.cnt["cc"] += 1
        self.q["g"].append((fn, tuple(deps), ("cc", 1)))
        return ("cc", self.cnt["cc"])

    W = 3584

    def finalize(self):
        dep_vals = {k: set() for k in self.cnt}
        for qk in self.q:
            for _, deps, _ in self.q[qk]:
                for (dn, dv) in deps:
                    if dv > 0:
                        dep_vals[dn].add(dv)
        self.rank = {}
        self.rtot = {}
        for name, vals in dep_vals.items():
            r = {}
            step = 16 if name in ("dy", "ds", "dg") else 1
            for i, v in enumerate(sorted(vals)):
                r[v] = (i + 1) * step
            self.rank[name] = r
            self.rtot[name] = len(vals) * step

    def replay(self, engine, eng_key, sems):
        waited = {}
        cur = {k: 0 for k in self.cnt}
        for fn, deps, (sem_name, inc) in self.q[eng_key]:
            for (dn, dv) in deps:
                if dv <= 0:
                    continue
                rv = self.rank[dn][dv]
                k = (rv - 1) // self.W
                v = rv - k * self.W
                if waited.get((dn, k), -1) < v:
                    engine.wait_ge(sems[(dn, k)], v)
                    waited[(dn, k)] = v
            cur[sem_name] += inc
            n = cur[sem_name]
            ins = fn(engine)
            if n in self.rank[sem_name]:
                rv = self.rank[sem_name][n]
                k = (rv - 1) // self.W
                ins.then_inc(sems[(sem_name, k)], inc)

    def nsems(self, name):
        return max(1, (self.rtot.get(name, 0) + self.W - 1) // self.W)


def _prep(inp):
    x = np.ascontiguousarray(np.asarray(inp["L0_attn_x"], np.float32))
    w0 = np.asarray(inp["L0_attn_w0"], np.float32)
    w1 = np.asarray(inp["L0_attn_w1"], np.float32)
    w2 = np.asarray(inp["L0_attn_w2"], np.float32)
    Wqkv = np.asarray(inp["L0_attn_W_qkv"], np.float32)
    Wlr = np.asarray(inp["L0_attn_W_lr"], np.float32)
    blr = np.asarray(inp["L0_attn_b_lr"], np.float32)
    Wproj = np.asarray(inp["L0_attn_W_proj"], np.float32)
    onorm = np.asarray(inp["L0_attn_onorm_w"], np.float32)
    A = np.asarray(inp["L0_ffn_A"], np.float32)
    Bw = np.asarray(inp["L0_ffn_Bw"], np.float32)
    Wg = np.asarray(inp["L0_ffn_Wg"], np.float32)

    wproj_r = np.ascontiguousarray(Wproj.reshape(16, 128, 16, 128).transpose(2, 1, 0, 3))
    AT = A.transpose(2, 0, 1).reshape(D, 512)
    wa_r = np.ascontiguousarray(AT.reshape(16, 128, 4, 128).transpose(2, 1, 0, 3))
    Bw_l = Bw.transpose(0, 2, 1).reshape(512, D)
    wbw_r = np.ascontiguousarray(Bw_l.reshape(4, 128, 16, 128).transpose(2, 1, 0, 3))
    wgt_r = np.ascontiguousarray(Wg.T.reshape(16, 128, 8).transpose(1, 0, 2))
    onorm_r = np.ascontiguousarray(onorm.reshape(128, 1))

    ones_b = np.ones((128, 128), dtype=BF)
    ones_f = np.ones((128, 128), dtype=np.float32)
    ident_b = np.eye(128, dtype=np.float32).astype(BF)
    ident_f = np.eye(128, dtype=np.float32)
    e64 = np.zeros((128, 512), np.float32)
    for col in range(512):
        e64[col // 64, col] = 1.0
    e64_b = e64.astype(BF)

    xT = [np.ascontiguousarray(x[b].T) for b in range(B)]

    per_core = []
    for c in range(8):
        b, half = c // 2, c % 2
        hs = slice(half * 8, half * 8 + 8)
        Wcols = np.ascontiguousarray(Wqkv.reshape(D, 3, 16, 128)[:, :, hs, :].reshape(D, 24 * 128))
        wq_r = np.ascontiguousarray(Wcols.reshape(16, 128, 24, 128).transpose(2, 1, 0, 3))
        Wlrc = Wlr.reshape(D, 3, 16)[:, :, hs].reshape(D, 24)
        wlr_r = np.ascontiguousarray(Wlrc.reshape(16, 128, 24).transpose(1, 0, 2))
        blr_r = np.ascontiguousarray(blr.reshape(3, 16)[:, hs].reshape(24, 1))
        w0h, w1h, w2h = w0[hs], w1[hs], w2[hs]
        pc = {
            "xT": xT[b], "wq": wq_r, "wlr": wlr_r, "blr": blr_r,
            "w0t": np.ascontiguousarray(w0h.transpose(2, 0, 1)),
            "w1t": np.ascontiguousarray(w1h.transpose(2, 0, 1)),
            "w2t": np.ascontiguousarray(w2h.transpose(2, 0, 1)),
            "w0o": np.ascontiguousarray(w0h.transpose(1, 0, 2)),
            "w1o": np.ascontiguousarray(w1h.transpose(1, 0, 2)),
            "w2o": np.ascontiguousarray(w2h.transpose(1, 0, 2)),
            "wproj": wproj_r, "wa": wa_r, "wbw": wbw_r, "wgt": wgt_r,
            "onorm": onorm_r, "ones_b": ones_b, "ones_f": ones_f,
            "ident_b": ident_b, "ident_f": ident_f, "e64_b": e64_b,
        }
        per_core.append(pc)
    return per_core


def build_program(upto=99, dbg_names=()):
    nc = bass.Bass("TRN2", target_bir_lowering=False)
    S = Sched()
    dbg_outs = {}

    dr = {}
    def di(name, shape, dt=F32):
        dr[name] = nc.dram_tensor(name, list(shape), dt, kind="ExternalInput").ap()
    di("xT", (D, L))
    di("wq", (24, 128, 16, 128))
    di("wlr", (128, 16, 24))
    di("blr", (24, 1))
    for n in ("w0t", "w1t", "w2t", "w0o", "w1o", "w2o"):
        di(n, (128, 8, 128))
    di("wproj", (16, 128, 16, 128))
    di("wa", (4, 128, 16, 128))
    di("wbw", (16, 128, 4, 128))
    di("wgt", (128, 16, 8))
    di("onorm", (128, 1))
    di("ones_b", (128, 128), BF16); di("ones_f", (128, 128))
    di("ident_b", (128, 128), BF16); di("ident_f", (128, 128))
    di("e64_b", (128, 512), BF16)
    out = nc.dram_tensor("out", [D, TOK], F32, kind="ExternalOutput").ap()
    cc_in = nc.dram_tensor("cc_in", [4, 1024, 512], BF16, kind="Internal").ap()
    cc_out = nc.dram_tensor("cc_out", [4, 2048, 512], BF16, kind="Internal").ap()

    BASE = (nc.sbuf_base + 31) // 32 * 32
    OFF_Q, OFF_A, OFF_S = BASE, BASE + 98304, BASE + 163840
    SCR = 23040
    OFF_FIX = OFF_S + SCR
    ARENA = 211680
    nc.alloc_sbuf_tensor("arena", [128, ARENA], mybir.dt.int8)

    sb = {}
    def sba(name, shape, dt, off):
        assert off % 32 == 0, (name, off)
        assert off >= BASE
        sb[name] = nc.alloc_sbuf_tensor_at(name, list(shape), dt, offset=off)
        return sb[name]

    qkv = sba("qkv", [128, 24 * 2048], BF16, OFF_Q)
    x1 = sba("x1", [128, 16 * 1024], F32, OFF_A)
    x2p = sba("x2p", [128, 16 * 1024], F32, OFF_A)
    x1nb = sba("x1nb", [128, 16 * 1024], BF16, OFF_Q + 65536)
    xT_sb = sba("xT_sb", [128, 16 * 2048], BF16, OFF_A)
    oiN = sba("oiN", [128, 8 * 2048], BF16, OFF_A)
    attn = sba("attn", [128, 16 * 1024], BF16, OFF_A + 32768)
    x1n = sba("x1n", [128, 16 * 1024], F32, OFF_Q)

    w0T = sba("w0T", [128, 8 * 128], F32, OFF_FIX)
    w1T = sba("w1T", [128, 8 * 128], F32, OFF_FIX + 4096)
    w2T = sba("w2T", [128, 8 * 128], F32, OFF_FIX + 8192)
    w0fw = sba("w0fw", [128, 8 * 128], BF16, OFF_FIX + 12288)
    w1fw = sba("w1fw", [128, 8 * 128], BF16, OFF_FIX + 14336)
    w1Tfw = sba("w1Tfw", [128, 8 * 128], BF16, OFF_FIX + 16384)
    w2fw = sba("w2fw", [128, 8 * 128], BF16, OFF_FIX + 18432)
    CON = OFF_FIX + 20480
    ones_b = sba("ones_b_s", [128, 128], BF16, CON)
    ident_b = sba("ident_b_s", [128, 128], BF16, CON + 256)
    ones_f = sba("ones_f_s", [128, 128], F32, CON + 512)
    ident_f = sba("ident_f_s", [128, 128], F32, CON + 1024)
    e64_s = sba("e64_s", [128, 512], BF16, CON + 1536)
    onorm_s = sba("onorm_s", [128, 1], F32, CON + 2560)
    blr_s = sba("blr_s", [24, 1], F32, CON + 2592)
    lr_tok = sba("lr_tok", [128, 16 * 24], F32, CON + 2624)
    wn_s = sba("wn_s", [128, 24], F32, CON + 4160)

    wq_b = [sba(f"wq_b{i}", [128, 16 * 128], BF16, OFF_S + i * 4096) for i in range(2)]
    lrp = sba("lrp", [24, 2048], F32, OFF_S + 8192)
    sp_r = sba("sp_r", [24, 512], F32, OFF_S + 16384)
    sp_a = sba("sp_a", [24, 512], F32, OFF_S + 18432)
    wlr_b = sba("wlr_b", [128, 16 * 24], BF16, OFF_S + 20480)

    pb = [nc.alloc_psum_tensor(f"pb{i}", [128, 512], F32) for i in range(6)]
    pt0 = nc.alloc_psum_tensor("pt0", [128, 1024], BF16)
    pt1 = nc.alloc_psum_tensor("pt1", [128, 1024], BF16)

    sems = None

    def load(qu, dst, src, deps=()):
        return S.dma(qu, lambda e, d=dst, s=src: e.dma_start(out=d, in_=s), deps)

    load("y", ones_b[:, :], dr["ones_b"])
    load("y", ident_b[:, :], dr["ident_b"])
    load("y", ones_f[:, :], dr["ones_f"])
    load("y", ident_f[:, :], dr["ident_f"])
    load("y", e64_s[:, :], dr["e64_b"])
    load("y", onorm_s[:, :], dr["onorm"])
    d_blr = load("y", blr_s[:, :], dr["blr"])
    load("y", w0T[:, :], dr["w0t"].rearrange("p j d -> p (j d)"))
    load("y", w1T[:, :], dr["w1t"].rearrange("p j d -> p (j d)"))
    d_wstate = load("y", w2T[:, :], dr["w2t"].rearrange("p j d -> p (j d)"))
    load("g", w0fw[:, :], dr["w0o"].rearrange("p j d -> p (j d)"))
    load("g", w1fw[:, :], dr["w1o"].rearrange("p j d -> p (j d)"))
    load("g", w1Tfw[:, :], dr["w1t"].rearrange("p j d -> p (j d)"))
    d_wfw = load("g", w2fw[:, :], dr["w2o"].rearrange("p j d -> p (j d)"))
    d_consts = ("dy", S.cnt["dy"])

    xT_dst = xT_sb[:, :].rearrange("p (k t) -> p k t", k=16)
    xT_src = dr["xT"].rearrange("(k p) t -> p k t", p=128)
    d_xt0 = load("g", xT_dst[:, :, 0:1024], xT_src[:, :, 0:1024])
    d_xt1 = load("g", xT_dst[:, :, 1024:2048], xT_src[:, :, 1024:2048])
    d_wlr = load("g", wlr_b[:, :], dr["wlr"].rearrange("p k c -> p (k c)"))

    t_blradj = S.op("v", lambda e: e.tensor_scalar_add(blr_s[:, :], blr_s[:, :], BASE_LR_INV),
                    deps=[d_blr])
    ev_tok = {}
    last_evac = {tt: None for tt in range(4)}
    last_mm_of_cc = {}
    for cc in range(24):
        par = cc % 2
        wdeps = [] if cc < 2 else [last_mm_of_cc[cc - 2]]
        d_wq = load("g", wq_b[par][:, :], dr["wq"][cc].rearrange("p k c -> p (k c)"), wdeps)
        for tt in range(4):
            deps = [d_wq, d_xt0 if tt < 2 else d_xt1]
            if last_evac[tt] is not None:
                deps.append(last_evac[tt])
            mm_tok = None
            for k in range(16):
                def mk(e, par=par, tt=tt, k=k):
                    return e.matmul(pb[tt][:, :], wq_b[par][:, k * 128:(k + 1) * 128],
                                    xT_sb[:, k * 2048 + tt * 512: k * 2048 + (tt + 1) * 512],
                                    start=(k == 0), stop=(k == 15))
                mm_tok = S.op("t", mk, deps=deps if k == 0 else ())
            last_mm_of_cc[cc] = mm_tok
            def ev(e, cc=cc, tt=tt):
                return e.activation(qkv[:, cc * 2048 + tt * 512: cc * 2048 + (tt + 1) * 512],
                                    pb[tt][:, :], AF.Silu)
            tok = S.op("s", ev, deps=[mm_tok])
            ev_tok[(cc, tt)] = tok
            last_evac[tt] = tok

    lr_ev = []
    for tt in range(4):
        mm_tok = None
        for k in range(16):
            def mk(e, tt=tt, k=k):
                return e.matmul(pb[4][0:24, :], wlr_b[:, k * 24:(k + 1) * 24],
                                xT_sb[:, k * 2048 + tt * 512: k * 2048 + (tt + 1) * 512],
                                start=(k == 0), stop=(k == 15))
            deps = [d_wlr, d_xt0 if tt < 2 else d_xt1] + ([lr_ev[-1]] if lr_ev else [])
            mm_tok = S.op("t", mk, deps=deps if k == 0 else ())
        def ev(e, tt=tt):
            return e.tensor_scalar(lrp[:, tt * 512:(tt + 1) * 512], pb[4][0:24, :],
                                   blr_s[:, 0:1], None, ALU.add)
        lr_ev.append(S.op("v", ev, deps=[mm_tok, t_blradj]))
    sp_done = []
    for hh in range(4):
        sl = slice(hh * 512, (hh + 1) * 512)
        war = list(lr_ev) + ([sp_done[-1]] if sp_done else [])
        a1 = S.op("s", lambda e, sl=sl: e.activation(sp_r[:, :], lrp[:, sl], AF.Relu), deps=war)
        a2 = S.op("s", lambda e, sl=sl: e.activation(sp_a[:, :], lrp[:, sl], AF.Abs), deps=war)
        a3 = S.op("s", lambda e, sl=sl: e.activation(sp_a[:, :], sp_a[:, :], AF.Exp, scale=-1.0), deps=[a2])
        a4 = S.op("s", lambda e, sl=sl: e.activation(sp_a[:, :], sp_a[:, :], AF.Ln, bias=1.0), deps=[a3])
        sp_done.append(S.op("v", lambda e, sl=sl: e.tensor_add(lrp[:, sl], sp_a[:, :], sp_r[:, :]),
                            deps=[a4, a1]))
    lrt_last = None
    for g4 in range(4):
        trs = None
        for u in range(4):
            tt16 = g4 * 4 + u
            def mk(e, tt16=tt16, u=u):
                return e.transpose(pb[5][:, u * 24:(u + 1) * 24],
                                   lrp[0:24, tt16 * 128:(tt16 + 1) * 128],
                                   ident_f[0:24, 0:24])
            dd = list(sp_done) + ([lrt_last] if lrt_last else [])
            trs = S.op("t", mk, deps=dd if u == 0 else ())
        lrt_last = S.op("v", lambda e, g4=g4: e.tensor_copy(lr_tok[:, g4 * 96:(g4 + 1) * 96],
                                                            pb[5][:, 0:96]), deps=[trs])
    p1_done = dict(qkv=ev_tok[(23, 3)], lr=lrt_last, d_xt=(d_xt0, d_xt1))

    def dbg(name, ap_, deps):
        if name not in dbg_names:
            return None
        o = nc.dram_tensor("dbg_" + name, list(ap_.shape), ap_.tensor.dtype,
                           kind="ExternalOutput").ap()
        dbg_outs[name] = True
        return S.dma("y", lambda e, o=o, a=ap_: e.dma_start(out=o, in_=a), deps)

    dbg("qkv", qkv[:, :], [ev_tok[(23, 3)]])
    dbg("lr", lr_tok[:, :], [lrt_last])

    ctx = dict(nc=nc, S=S, sb=sb, pb=pb, pt0=pt0, pt1=pt1, dr=dr, dbg=dbg, load=load,
               p1=p1_done, out=out, cc_in=cc_in, cc_out=cc_out, OFF_S=OFF_S, sba=sba,
               BASE=BASE, OFF_A=OFF_A, OFF_A2=OFF_A + 32768, OFF_Q=OFF_Q,
               d_wstate=d_wstate, d_wfw=d_wfw, d_consts=d_consts)
    if upto >= 2:
        build_scan(ctx)
    if upto >= 3:
        build_tail(ctx)

    S.op("g", lambda e: e.engine_nop(),
         deps=[("dy", S.cnt["dy"]), ("dg", S.cnt["dg"]), ("ds", S.cnt["ds"])])

    S.finalize()
    sems = {}
    for name in ("t", "v", "s", "g", "dy", "ds", "dg", "cc"):
        for k in range(S.nsems(name)):
            sems[(name, k)] = nc.alloc_semaphore(f"{name}{k}")
    with nc.Block() as blk:
        @blk.tensor
        def _(e):
            S.replay(e, "t", sems)
        @blk.vector
        def _(e):
            S.replay(e, "v", sems)
        @blk.scalar
        def _(e):
            S.replay(e, "s", sems)
        @blk.gpsimd
        def _(e):
            S.replay(e, "g", sems)
        @blk.sync
        def _(e):
            S.replay(e, "y", sems)
    return nc, dbg_outs


def build_scan(ctx):
    nc, S, sb, pb, dr = ctx["nc"], ctx["S"], ctx["sb"], ctx["pb"], ctx["dr"]
    pt0, pt1, dbg = ctx["pt0"], ctx["pt1"], ctx["dbg"]
    sba, OFF_S = ctx["sba"], ctx["OFF_S"]
    qkv, oiN = sb["qkv"], sb["oiN"]
    w0T, w1T, w2T = sb["w0T"], sb["w1T"], sb["w2T"]
    w0fw, w1fw, w1Tfw, w2fw = sb["w0fw"], sb["w1fw"], sb["w1Tfw"], sb["w2fw"]
    ones_b, ident_b = sb["ones_b_s"], sb["ident_b_s"]
    ones_f, ident_f = sb["ones_f_s"], sb["ident_f_s"]
    onorm_s, lr_tok, wn_s = sb["onorm_s"], sb["lr_tok"], sb["wn_s"]
    wT = [w0T, w1T, w2T]
    p1 = ctx["p1"]

    RH = ctx["OFF_A2"]
    rA = sba("rA", [128, 4096], BF16, RH)
    rB = sba("rB", [128, 4096], BF16, RH + 8192)
    rC = sba("rC", [128, 4096], BF16, RH + 16384)
    rD = sba("rD", [128, 4096], BF16, RH + 24576)
    scrf = [sba(f"scf{i}", [128, 1024], F32, OFF_S + i * 4096) for i in range(4)]
    pk = sba("pk", [128, 384], F32, OFF_S + 16384)
    scrb = sba("scrb", [128, 1024], BF16, OFF_S + 17920)
    scrb2 = sba("scrb2", [128, 1024], BF16, OFF_S + 19968)
    KIN, QIN, FRB, RNM, A0C, A2C = 0, 128, 256, 280, 304, 336
    PKS = 384
    LRS = 16 * 24
    A_, B_, C_, D_ = scrf

    st = dict()

    d_ws = ctx["d_wstate"]
    last = [p1["qkv"], p1["lr"], d_ws, ctx["d_wfw"], ctx["d_consts"]]
    for w in range(3):
        t1 = S.op("s", lambda e, w=w: e.activation(A_[:, :], wT[w][:, :], AF.Square), deps=last)
        r1 = S.op("v", lambda e, w=w: e.reduce_sum(
            wn_s[:, w * 8:w * 8 + 8], A_[:, :].rearrange("p (j d) -> p j d", j=8), axis=AX.X),
            deps=[t1])
        last = [r1]
    t1 = S.op("s", lambda e: e.activation(wn_s[:, :], wn_s[:, :], AF.Ln), deps=last)
    wn_done = S.op("s", lambda e: e.activation(wn_s[:, :], wn_s[:, :], AF.Exp, scale=0.5), deps=[t1])

    st["w_upd"] = [wn_done] * 3
    st["w1t_cast"] = wn_done
    st["rfree"] = [wn_done]
    st["pt0"] = [wn_done]
    st["pt1"] = [wn_done]
    bank_free = {i: wn_done for i in range(6)}

    h_evk = h_evq = h_sqk = h_sqq = None
    h_last = []
    for c0 in range(NCH):
        for j in range(8):
            ksl = qkv[:, 8 * 2048 + j * 2048 + c0 * 512: 8 * 2048 + j * 2048 + (c0 + 1) * 512]
            qsl = qkv[:, j * 2048 + c0 * 512: j * 2048 + (c0 + 1) * 512]
            trk = trq = None
            for u in range(4):
                def mk(e, u=u, ksl=ksl):
                    return e.transpose(pt0[:, u * 128:(u + 1) * 128], ksl[:, u * 128:(u + 1) * 128],
                                       ident_b[:, :])
                trk = S.op("t", mk, deps=st["pt0"] + [p1["qkv"]] if u == 0 else ())
            for u in range(4):
                def mq(e, u=u, qsl=qsl):
                    return e.transpose(pt1[:, u * 128:(u + 1) * 128], qsl[:, u * 128:(u + 1) * 128],
                                       ident_b[:, :])
                trq = S.op("t", mq, deps=st["pt1"] if u == 0 else ())
            h_evk = S.op("v", lambda e: e.tensor_copy(scrb[:, 0:512], pt0[:, 0:512]),
                         deps=[trk] + ([h_sqk] if h_sqk else []))
            h_evq = S.op("s", lambda e: e.activation(scrb[:, 512:1024], pt1[:, 0:512], AF.Copy),
                         deps=[trq] + ([h_sqq] if h_sqq else []))
            st["pt0"] = [h_evk]
            st["pt1"] = [h_evq]
            h_sqk = S.op("s", lambda e: e.activation(A_[:, 0:512], scrb[:, 0:512], AF.Square),
                         deps=[h_evk] + ([h_last[-2]] if len(h_last) >= 2 else []))
            h_sqq = S.op("v", lambda e: e.activation(A_[:, 512:1024], scrb[:, 512:1024], AF.Square)
                         if False else e.tensor_mul(A_[:, 512:1024], scrb[:, 512:1024], scrb[:, 512:1024]),
                         deps=[h_evq] + ([h_last[-1]] if len(h_last) >= 2 else []))
            rk = S.op("v", lambda e, c0=c0, j=j: e.reduce_sum(
                pk[:, KIN + c0 * 32 + j * 4: KIN + c0 * 32 + j * 4 + 4],
                A_[:, 0:512].rearrange("p (u d) -> p u d", u=4), axis=AX.X), deps=[h_sqk])
            rq = S.op("v", lambda e, c0=c0, j=j: e.reduce_sum(
                pk[:, QIN + c0 * 32 + j * 4: QIN + c0 * 32 + j * 4 + 4],
                A_[:, 512:1024].rearrange("p (u d) -> p u d", u=4), axis=AX.X), deps=[h_sqq])
            h_last += [rk, rq]
    h1 = S.op("s", lambda e: e.activation(pk[:, 0:256], pk[:, 0:256], AF.Ln), deps=h_last[-2:])
    h2 = S.op("s", lambda e: e.activation(pk[:, 0:256], pk[:, 0:256], AF.Exp, scale=0.5), deps=[h1])
    h3 = S.op("v", lambda e: e.tensor_scalar_add(pk[:, 0:256], pk[:, 0:256], 1e-5), deps=[h2])
    hq_done = S.op("v", lambda e: e.reciprocal(pk[:, 0:256], pk[:, 0:256]), deps=[h3])
    st["scrA"] = [hq_done]
    st["scrB"] = [wn_done]
    st["scrC"] = [wn_done]
    st["scrD"] = [wn_done]
    st["scrb"] = [h_sqk, h_sqq]
    st["scrb2"] = [wn_done]
    st["alpha_free"] = [wn_done]
    st["oi_q"] = {}

    QB, KB, VB = 0, 8 * 2048, 16 * 2048
    oiN_unscaled = {}
    rms_done = {}

    def emit_qpath(cm1):
        toks = []
        prev = {}
        for j in range(8):
            h = (j % 2) * 512
            bq, bh, bo = (0, 1, 2) if j % 2 == 0 else (3, 4, 5)
            qsl = qkv[:, QB + j * 2048 + cm1 * 512: QB + j * 2048 + (cm1 + 1) * 512]
            mmg = mmh = None
            for u in range(4):
                def mg(e, j=j, u=u, qsl=qsl, bq=bq):
                    return e.matmul(pb[bq][:, u * 128:(u + 1) * 128], qsl[:, u * 128:(u + 1) * 128],
                                    w0fw[:, j * 128:(j + 1) * 128], start=True, stop=True)
                mmg = S.op("t", mg, deps=[bank_free[bq], st["w_upd"][0], p1["qkv"]] if u == 0 else ())
            for u in range(4):
                def mh(e, j=j, u=u, qsl=qsl, bh=bh):
                    return e.matmul(pb[bh][:, u * 128:(u + 1) * 128], qsl[:, u * 128:(u + 1) * 128],
                                    w2fw[:, j * 128:(j + 1) * 128], start=True, stop=True)
                mmh = S.op("t", mh, deps=[bank_free[bh], st["w_upd"][2]] if u == 0 else ())
            war = prev.get((j % 2, "A"), st["scrA"][-1] if j < 2 else None)
            t_sl = None
            for u in range(4):
                def msl(e, u=u, h=h, bq=bq, cm1=cm1, j=j):
                    return e.activation(A_[:, h + u * 128: h + (u + 1) * 128],
                                        pb[bq][:, u * 128:(u + 1) * 128], AF.Silu,
                                        scale=pk[:, QIN + cm1 * 32 + j * 4 + u: QIN + cm1 * 32 + j * 4 + u + 1])
                t_sl = S.op("s", msl,
                            deps=([mmg, hq_done] + ([war] if war else []) +
                                  (st["scrA"] if j < 2 else [])) if u == 0 else ())
            sq_ap = bass.AP(pk, QIN + cm1 * 32 + j * 4, [[PKS, 128], [1, 4], [0, 128]])
            t_qhn = S.op("v", lambda e, sq_ap=sq_ap, h=h, bh=bh: e.tensor_tensor(
                B_[:, h:h + 512].rearrange("p (u c) -> p u c", u=4),
                pb[bh][:, :].rearrange("p (u c) -> p u c", u=4), sq_ap, ALU.mult),
                deps=[mmh] + ([prev[(j % 2, "B")]] if (j % 2, "B") in prev else st["scrB"]))
            bank_free[bq], bank_free[bh] = t_sl, t_qhn
            t_hq = S.op("g", lambda e, h=h: e.tensor_mul(scrb[:, h:h + 512], A_[:, h:h + 512],
                                                         B_[:, h:h + 512]),
                        deps=[t_sl, t_qhn] + ([prev[(j % 2, "hq")]] if (j % 2, "hq") in prev else st["scrb"]))
            trs = None
            ptx = pt0 if j % 2 == 0 else pt1
            for u in range(4):
                def mt(e, u=u, h=h, ptx=ptx):
                    return e.transpose(ptx[:, u * 128:(u + 1) * 128],
                                       scrb[:, h + u * 128: h + (u + 1) * 128], ident_b[:, :])
                trs = S.op("t", mt, deps=[t_hq] + (st["pt0"] if j % 2 == 0 else st["pt1"]) if u == 0 else ())
            if j % 2 == 0:
                t_hqT = S.op("s", lambda e, h=h, ptx=ptx: e.activation(
                    scrb2[:, h:h + 512], ptx[:, 0:512], AF.Copy),
                    deps=[trs] + ([prev[(j % 2, "hqT")]] if (j % 2, "hqT") in prev else st["scrb2"]))
            else:
                t_hqT = S.op("v", lambda e, h=h, ptx=ptx: e.tensor_copy(
                    scrb2[:, h:h + 512], ptx[:, 0:512]),
                    deps=[trs] + ([prev[(j % 2, "hqT")]] if (j % 2, "hqT") in prev else st["scrb2"]))
            if j % 2 == 0:
                st["pt0"] = [t_hqT]
            else:
                st["pt1"] = [t_hqT]
            def moi(e, j=j, h=h, bo=bo):
                return e.matmul(pb[bo][:, :], w1fw[:, j * 128:(j + 1) * 128], scrb2[:, h:h + 512],
                                start=True, stop=True)
            mmoi = S.op("t", moi, deps=[t_hqT, bank_free[bo], st["w_upd"][1]])
            t_ev = S.op("v", lambda e, j=j, cm1=cm1, bo=bo: e.tensor_copy(
                oiN[:, j * 2048 + cm1 * 512: j * 2048 + (cm1 + 1) * 512], pb[bo][:, :]),
                deps=[mmoi, st.get("rms_war", wn_done)])
            bank_free[bo] = t_ev
            prev[(j % 2, "A")] = t_hq
            prev[(j % 2, "B")] = t_hq
            prev[(j % 2, "hq")] = trs
            prev[(j % 2, "hqT")] = mmoi
            toks.append(t_ev)
        oiN_unscaled[cm1] = toks
        st["scrA"] = [prev[(0, "A")], prev[(1, "A")]]
        st["scrB"] = [prev[(0, "B")], prev[(1, "B")]]
        st["scrb"] = [prev[(0, "hq")], prev[(1, "hq")]]
        st["scrb2"] = [prev[(0, "hqT")], prev[(1, "hqT")]]

    def emit_rms(cm1):
        toks = []
        prev = {}
        for j in range(8):
            h = (j % 2) * 512
            bo = 3 + (j % 2)
            sl = oiN[:, j * 2048 + cm1 * 512: j * 2048 + (cm1 + 1) * 512]
            t_sq = S.op("s", lambda e, sl=sl, h=h: e.activation(scrb[:, h:h + 512], sl, AF.Square),
                        deps=[oiN_unscaled[cm1][j]] + ([prev.get((j % 2, "sq"))] if (j % 2, "sq") in prev
                                                       else st["scrb"]))
            def msq(e, h=h, bo=bo):
                return e.matmul(pb[bo][:, :], ones_b[:, :], scrb[:, h:h + 512], start=True, stop=True)
            mm = S.op("t", msq, deps=[t_sq, bank_free[bo]])
            t_u = S.op("s", lambda e, h=h, bo=bo: e.activation(C_[:, h:h + 512], pb[bo][:, :],
                                                               AF.Copy, scale=1.0 / 128, bias=1e-5),
                       deps=[mm] + ([prev.get((j % 2, "C"))] if (j % 2, "C") in prev else st["scrC"]))
            bank_free[bo] = t_u
            t_l = S.op("s", lambda e, h=h: e.activation(C_[:, h:h + 512], C_[:, h:h + 512], AF.Ln),
                       deps=[t_u])
            t_e = S.op("s", lambda e, h=h: e.activation(C_[:, h:h + 512], C_[:, h:h + 512], AF.Exp,
                                                        scale=-0.5), deps=[t_l])
            t_m = S.op("g", lambda e, sl=sl, h=h: e.tensor_mul(sl, sl, C_[:, h:h + 512]),
                       deps=[t_e])
            t_g = S.op("v", lambda e, sl=sl: e.tensor_scalar_mul(sl, sl, onorm_s[:, 0:1]),
                       deps=[t_m])
            prev[(j % 2, "sq")] = mm
            prev[(j % 2, "C")] = t_g
            toks.append(t_g)
        st["scrb"] = [prev[(0, "sq")], prev[(1, "sq")]]
        st["scrC"] = [prev[(0, "C")], prev[(1, "C")]]
        st["rms_war"] = toks[-1]
        return toks

    for c in range(NCH):
        if c > 0:
            emit_qpath(c - 1)
        lr0_ap = bass.AP(lr_tok, (c * 4) * 24 + 0, [[LRS, 128], [1, 8], [24, 4]])
        lr2_ap = bass.AP(lr_tok, (c * 4) * 24 + 16, [[LRS, 128], [1, 8], [24, 4]])
        ki_ap = bass.AP(pk, KIN + c * 32, [[PKS, 128], [4, 8], [1, 4]])
        b5 = S.op("v", lambda e, lr0_ap=lr0_ap, ki_ap=ki_ap: e.tensor_tensor(
            pk[:, A0C:A0C + 32].rearrange("p (j u) -> p j u", j=8), lr0_ap, ki_ap, ALU.mult),
            deps=[hq_done, p1["lr"]] + st["alpha_free"])
        b6 = S.op("v", lambda e, lr2_ap=lr2_ap, ki_ap=ki_ap: e.tensor_tensor(
            pk[:, A2C:A2C + 32].rearrange("p (j u) -> p j u", j=8), lr2_ap, ki_ap, ALU.mult),
            deps=[hq_done])
        gev = []
        pm = {}
        for p in range(4):
            ja, jb = 2 * p, 2 * p + 1
            war = st["rfree"] if (c > 0 and p == 0) else []
            trk = trv = None
            for j2, j in enumerate((ja, jb)):
                ksl = qkv[:, KB + j * 2048 + c * 512: KB + j * 2048 + (c + 1) * 512]
                vsl = qkv[:, VB + j * 2048 + c * 512: VB + j * 2048 + (c + 1) * 512]
                for u in range(4):
                    def mk(e, u=u, ksl=ksl, j2=j2):
                        return e.transpose(pt0[:, j2 * 512 + u * 128: j2 * 512 + (u + 1) * 128],
                                           ksl[:, u * 128:(u + 1) * 128], ident_b[:, :])
                    trk = S.op("t", mk, deps=(st["pt0"] + [p1["qkv"]]) if (j2 == 0 and u == 0) else ())
            for j2, j in enumerate((ja, jb)):
                vsl = qkv[:, VB + j * 2048 + c * 512: VB + j * 2048 + (c + 1) * 512]
                for u in range(4):
                    def mv(e, u=u, vsl=vsl, j2=j2):
                        return e.transpose(pt1[:, j2 * 512 + u * 128: j2 * 512 + (u + 1) * 128],
                                           vsl[:, u * 128:(u + 1) * 128], ident_b[:, :])
                    trv = S.op("t", mv, deps=st["pt1"] if (j2 == 0 and u == 0) else ())
            t_kt = S.op("v", lambda e: e.tensor_copy(rB[:, 0:1024], pt0[:, :]),
                        deps=[trk] + war + ([pm.get("ktfree")] if "ktfree" in pm else []))
            t_vt = S.op("v", lambda e: e.tensor_copy(rB[:, 1024:2048], pt1[:, :]),
                        deps=[trv] + war + ([pm.get("vtfree")] if "vtfree" in pm else []))
            st["pt0"] = [t_kt]
            st["pt1"] = [t_vt]
            mm_g = {}
            for j2, j in enumerate((ja, jb)):
                ksl = qkv[:, KB + j * 2048 + c * 512: KB + j * 2048 + (c + 1) * 512]
                vsl = qkv[:, VB + j * 2048 + c * 512: VB + j * 2048 + (c + 1) * 512]
                bg, bh, bd = (3, 4, 5) if j2 == 0 else (0, 1, 2)
                for u in range(4):
                    def m1(e, u=u, ksl=ksl, j=j, bg=bg):
                        return e.matmul(pb[bg][:, u * 128:(u + 1) * 128], ksl[:, u * 128:(u + 1) * 128],
                                        w0fw[:, j * 128:(j + 1) * 128], start=True, stop=True)
                    mm_g[(j2, "g")] = S.op("t", m1, deps=[bank_free[bg], st["w_upd"][0]] if u == 0 else ())
                for u in range(4):
                    def m2(e, u=u, ksl=ksl, j=j, bh=bh):
                        return e.matmul(pb[bh][:, u * 128:(u + 1) * 128], ksl[:, u * 128:(u + 1) * 128],
                                        w2fw[:, j * 128:(j + 1) * 128], start=True, stop=True)
                    mm_g[(j2, "h")] = S.op("t", m2, deps=[bank_free[bh], st["w_upd"][2]] if u == 0 else ())
                for u in range(4):
                    def m3(e, u=u, vsl=vsl, j=j, bd=bd):
                        return e.matmul(pb[bd][:, u * 128:(u + 1) * 128], vsl[:, u * 128:(u + 1) * 128],
                                        w1Tfw[:, j * 128:(j + 1) * 128], start=True, stop=True)
                    mm_g[(j2, "d")] = S.op("t", m3, deps=[bank_free[bd], st.get("w1t_cast", st["w_upd"][1])] if u == 0 else ())
            kin_a = bass.AP(pk, KIN + c * 32 + ja * 4, [[PKS, 128], [1, 4], [0, 128]])
            kin_b = bass.AP(pk, KIN + c * 32 + jb * 4, [[PKS, 128], [1, 4], [0, 128]])
            n1a = S.op("v", lambda e, kin_a=kin_a: e.tensor_tensor(
                A_[:, 0:512].rearrange("p (u d) -> p u d", u=4),
                pb[3][:, :].rearrange("p (u d) -> p u d", u=4), kin_a, ALU.mult),
                deps=[mm_g[(0, "g")], hq_done] + st["scrA"])
            n1b = S.op("v", lambda e, kin_b=kin_b: e.tensor_tensor(
                A_[:, 512:1024].rearrange("p (u d) -> p u d", u=4),
                pb[0][:, :].rearrange("p (u d) -> p u d", u=4), kin_b, ALU.mult),
                deps=[mm_g[(1, "g")]] + st["scrA"])
            n2 = S.op("s", lambda e: e.activation(B_[:, :], A_[:, :], AF.Sigmoid),
                      deps=[n1a, n1b] + st["scrB"])
            n3a = S.op("v", lambda e, kin_a=kin_a: e.tensor_tensor(
                C_[:, 0:512].rearrange("p (u d) -> p u d", u=4),
                pb[4][:, :].rearrange("p (u d) -> p u d", u=4), kin_a, ALU.mult),
                deps=[mm_g[(0, "h")]] + st["scrC"])
            n3b = S.op("v", lambda e, kin_b=kin_b: e.tensor_tensor(
                C_[:, 512:1024].rearrange("p (u d) -> p u d", u=4),
                pb[1][:, :].rearrange("p (u d) -> p u d", u=4), kin_b, ALU.mult),
                deps=[mm_g[(1, "h")]] + st["scrC"])
            n4 = S.op("v", lambda e: e.tensor_mul(D_[:, :], A_[:, :], B_[:, :]),
                      deps=[n2] + st["scrD"])
            n5 = S.op("v", lambda e: e.scalar_tensor_tensor(A_[:, :], A_[:, :], 1.0, D_[:, :],
                                                            ALU.add, ALU.subtract), deps=[n4])
            n6 = S.op("v", lambda e: e.tensor_mul(B_[:, :], B_[:, :], A_[:, :]), deps=[n5])
            n7a = S.op("v", lambda e: e.tensor_tensor(A_[:, 0:512], pb[5][:, :], C_[:, 0:512],
                                                      ALU.mult), deps=[mm_g[(0, "d")], n6, n3a, n5])
            n7b = S.op("v", lambda e: e.tensor_tensor(A_[:, 512:1024], pb[2][:, :], C_[:, 512:1024],
                                                      ALU.mult), deps=[mm_g[(1, "d")], n3b, n6])
            n8 = S.op("v", lambda e: e.tensor_mul(rB[:, 2048:3072], A_[:, :], B_[:, :]),
                      deps=[n7a, n7b] + ([pm.get("dgba")] if "dgba" in pm else []) + war)
            n9a = S.op("v", lambda e: e.tensor_tensor(rB[:, 3072:3584], pb[5][:, :], D_[:, 0:512],
                                                      ALU.mult),
                       deps=[n7a] + ([pm.get("dhbm")] if "dhbm" in pm else []) + war)
            n9b = S.op("v", lambda e: e.tensor_tensor(rB[:, 3584:4096], pb[2][:, :], D_[:, 512:1024],
                                                      ALU.mult), deps=[n7b])
            n10 = S.op("v", lambda e: e.tensor_mul(A_[:, :], D_[:, :], C_[:, :]), deps=[n9a, n9b, n8])
            lr1_ap = bass.AP(lr_tok, (c * 4) * 24 + 8 + ja, [[LRS, 128], [1, 2], [24, 4], [0, 128]])
            n11 = S.op("v", lambda e, lr1_ap=lr1_ap: e.tensor_tensor(
                rD[:, 0:1024].rearrange("p (j u d) -> p j u d", j=2, u=4),
                A_[:, :].rearrange("p (j u d) -> p j u d", j=2, u=4), lr1_ap, ALU.mult),
                deps=[n10, p1["lr"]] + ([pm.get("hlr1")] if "hlr1" in pm else []))
            a0_ap = bass.AP(pk, A0C + ja * 4, [[PKS, 128], [1, 2], [4, 4] if False else [1, 4], [0, 128]])
            a0_ap = bass.AP(pk, A0C + ja * 4, [[PKS, 128], [4, 2], [1, 4], [0, 128]])
            a2_ap = bass.AP(pk, A2C + ja * 4, [[PKS, 128], [4, 2], [1, 4], [0, 128]])
            f1 = S.op("v", lambda e, a0_ap=a0_ap: e.tensor_tensor(
                rB[:, 2048:3072].rearrange("p (j u d) -> p j u d", j=2, u=4),
                rB[:, 2048:3072].rearrange("p (j u d) -> p j u d", j=2, u=4), a0_ap, ALU.mult),
                deps=[b5, n8])
            f2 = S.op("v", lambda e, a2_ap=a2_ap: e.tensor_tensor(
                rB[:, 3072:4096].rearrange("p (j u d) -> p j u d", j=2, u=4),
                rB[:, 3072:4096].rearrange("p (j u d) -> p j u d", j=2, u=4), a2_ap, ALU.mult),
                deps=[b6, n9a, n9b])
            for j2, j in enumerate((ja, jb)):
                bk = pb[3] if j2 == 0 else pb[0]
                ko, vo, do_, ho = j2 * 512, 1024 + j2 * 512, 2048 + j2 * 512, j2 * 512
                last_mm = None
                dep0 = [f1, f2, n11, n1a if j2 == 0 else n1b, t_kt, t_vt]
                for u in range(4):
                    def g0(e, bk=bk, do_=do_, ko=ko, u=u):
                        return e.matmul(bk[:, 0:128], rB[:, do_ + u * 128: do_ + (u + 1) * 128],
                                        rB[:, ko + u * 128: ko + (u + 1) * 128],
                                        start=(u == 0), stop=(u == 3))
                    last_mm = S.op("t", g0, deps=dep0 if u == 0 else ())
                for u in range(4):
                    def g1(e, bk=bk, vo=vo, ho=ho, u=u):
                        return e.matmul(bk[:, 128:256], rB[:, vo + u * 128: vo + (u + 1) * 128],
                                        rD[:, ho + u * 128: ho + (u + 1) * 128],
                                        start=(u == 0), stop=(u == 3))
                    last_mm = S.op("t", g1, deps=())
                for u in range(4):
                    def g2(e, bk=bk, ko=ko, j2=j2, u=u):
                        return e.matmul(bk[:, 256:384], rB[:, 3072 + j2 * 512 + u * 128: 3072 + j2 * 512 + (u + 1) * 128],
                                        rB[:, ko + u * 128: ko + (u + 1) * 128],
                                        start=(u == 0), stop=(u == 3))
                    last_mm = S.op("t", g2, deps=())
                ev = S.op("v", lambda e, j=j, bk=bk: e.tensor_copy(rA[:, j * 384: j * 384 + 384],
                                                                   bk[:, 0:384]),
                          deps=[last_mm] + (st["rfree"] if (c > 0 and p == 0 and j2 == 0) else []))
                bank_free[3 if j2 == 0 else 0] = ev
                gev.append(ev)
            bank_free[4] = n3a
            bank_free[1] = n3b
            bank_free[5] = n9a
            bank_free[2] = n9b
            pm = dict(ktfree=last_mm, vtfree=last_mm, dgba=f1, dhbm=f2, hlr1=last_mm)
            st["scrA"] = [n11]
            st["scrB"] = [n8]
            st["scrC"] = [n10]
            st["scrD"] = [n10]
        st["alpha_free"] = [f1, f2]

        if c > 0:
            rms_done[c - 1] = emit_rms(c - 1)
        if c == 3:
            ctx["wpre_toks"] = []
            for i in range(2):
                ctx["wpre_toks"].append(S.dma("g", lambda e, i=i: e.dma_start(
                    out=bass.AP(qkv, 16384 + (4 * i) * 2048 + 1024,
                                [[49152, 128], [2048, 4], [128, 4], [1, 128]]),
                    in_=ctx["dr"]["wproj"][i].rearrange("p k c -> p (k c)")),
                    deps=rms_done[2]))
        x2_tok, xread = ns_block(S, pb, pt0, pt1, ident_b, ones_f, rA, rB, rC, rD,
                                 scrf, scrb, scrb2, pk, FRB, PKS, gev, bank_free, st)
        w_update(S, pb, wT, wn_s, pk, RNM, PKS, scrf, rA, rB, rC, rD, x2_tok, xread,
                 bank_free, w0fw, w1fw, w1Tfw, w2fw, ident_f, st)

    emit_qpath(3)
    rms_done[3] = emit_rms(3)
    ctx["scan_done"] = [t for ts in rms_done.values() for t in ts]
    ctx["rms_toks"] = rms_done
    ctx["r_free"] = st["rfree"] + st["w_upd"]
    dbg("oiN", oiN[:, :], ctx["scan_done"][-1:])


def ns_block(S, pb, pt0, pt1, ident_b, ones_f, rA, rB, rC, rD, scrf, scrb, scrb2,
             pk, FRB, PKS, gev, bank_free, st):
    A_, B_, C_, D_ = scrf

    def xap(r, w, nj=8, j0=0):
        return bass.AP(r, j0 * 384 + w * 128, [[4096, 128], [384, nj], [1, 128]])

    fro = []
    t2 = None
    for w in range(3):
        for half in range(2):
            sqv = A_[:, half * 512:(half + 1) * 512]
            t1 = S.op("s", lambda e, w=w, half=half, sqv=sqv: e.activation(
                sqv.rearrange("p (j d) -> p j d", j=4), xap(rA, w, 4, half * 4), AF.Square),
                deps=(gev + st["scrA"] + st.get("dbg_war", [])) if (w == 0 and half == 0)
                     else ([t2] if t2 else []))
            t2 = S.op("v", lambda e, w=w, half=half, sqv=sqv: e.reduce_sum(
                pk[:, FRB + w * 8 + half * 4: FRB + w * 8 + half * 4 + 4],
                sqv.rearrange("p (j d) -> p j d", j=4), axis=AX.X), deps=[t1])
            fro.append(t2)
    st["scrA"] = [t2]
    def mfr(e):
        return e.matmul(pb[2][:, 0:24], ones_f[:, 0:128], pk[:, FRB:FRB + 24],
                        start=True, stop=True)
    mm = S.op("t", mfr, deps=fro + [bank_free[2]])
    c1 = S.op("v", lambda e: e.tensor_copy(pk[:, FRB:FRB + 24], pb[2][:, 0:24]), deps=[mm])
    bank_free[2] = c1
    c2 = S.op("s", lambda e: e.activation(pk[:, FRB:FRB + 24], pk[:, FRB:FRB + 24], AF.Ln), deps=[c1])
    c3 = S.op("s", lambda e: e.activation(pk[:, FRB:FRB + 24], pk[:, FRB:FRB + 24], AF.Exp, scale=0.5),
              deps=[c2])
    c4 = S.op("v", lambda e: e.tensor_scalar_add(pk[:, FRB:FRB + 24], pk[:, FRB:FRB + 24], 1e-7),
              deps=[c3])
    c5 = S.op("v", lambda e: e.reciprocal(pk[:, FRB:FRB + 24], pk[:, FRB:FRB + 24]), deps=[c4])

    xt_ev = []
    for w in range(3):
        ptx = (pt0, pt1, pt0)[w]
        key = "pt0" if ptx is pt0 else "pt1"
        trs = None
        for j in range(8):
            def mt(e, w=w, j=j, ptx=ptx):
                return e.transpose(ptx[:, j * 128:(j + 1) * 128],
                                   rA[:, j * 384 + w * 128: j * 384 + (w + 1) * 128], ident_b[:, :])
            trs = S.op("t", mt, deps=(st[key] + [gev[-1]]) if j == 0 else ())
        ev = S.op("v", lambda e, w=w, ptx=ptx: e.tensor_copy(
            xap(rC, w), ptx[:, :].rearrange("p (j d) -> p j d", j=8)), deps=[trs])
        st[key] = [ev]
        xt_ev.append(ev)

    x0t, xt0t = [], []
    for w in range(3):
        iv_ap = bass.AP(pk, FRB + w * 8, [[PKS, 128], [1, 8], [0, 128]])
        x0t.append(S.op("v", lambda e, w=w, iv_ap=iv_ap: e.tensor_tensor(
            xap(rB, w), xap(rA, w), iv_ap, ALU.mult), deps=[c5, gev[-1]] + (st["scrB"] if w == 0 else [])))
        xt0t.append(S.op("g", lambda e, w=w, iv_ap=iv_ap: e.tensor_tensor(
            xap(rD, w), xap(rC, w), iv_ap, ALU.mult), deps=[c5, xt_ev[w]] + (st["scrD"] if w == 0 else [])))

    x_src, xT_src = rB, rD
    x_dst = rA
    dep_x, dep_xT = x0t, xt0t
    x2_tok = []
    prevh = {}
    for step in range(2):
        new_dep_x, new_dep_xT = [], []
        for w in range(3):
            for grp in range(2):
                j0 = grp * 4
                hh = ((w * 2 + grp) % 2) * 512
                lastA = None
                for j in range(j0, j0 + 4):
                    def mA(e, w=w, j=j, j0=j0, xT_src=xT_src):
                        return e.matmul(pb[2][:, (j - j0) * 128:(j - j0 + 1) * 128],
                                        xT_src[:, j * 384 + w * 128: j * 384 + (w + 1) * 128],
                                        xT_src[:, j * 384 + w * 128: j * 384 + (w + 1) * 128],
                                        start=True, stop=True)
                    lastA = S.op("t", mA, deps=([bank_free[2]] + dep_xT) if j == j0 else ())
                evA = S.op("s", lambda e, hh=hh: e.activation(scrb[:, hh:hh + 512], pb[2][:, :],
                                                              AF.Copy),
                           deps=[lastA] + ([prevh[(hh, "As")]] if (hh, "As") in prevh else st["scrb"]))
                bank_free[2] = evA
                lastA2 = None
                for j4 in range(4):
                    def mA2(e, j4=j4, hh=hh):
                        return e.matmul(pb[3][:, j4 * 128:(j4 + 1) * 128],
                                        scrb[:, hh + j4 * 128: hh + (j4 + 1) * 128],
                                        scrb[:, hh + j4 * 128: hh + (j4 + 1) * 128],
                                        start=True, stop=True)
                    lastA2 = S.op("t", mA2, deps=[evA, bank_free[3]] if j4 == 0 else ())
                t1 = S.op("s", lambda e, hh=hh: e.activation(C_[:, hh:hh + 512], pb[3][:, :],
                                                             AF.Copy, scale=NS_C),
                          deps=[lastA2] + ([prevh[(hh, "t1")]] if (hh, "t1") in prevh else st["scrC"]))
                bank_free[3] = t1
                tBm = S.op("v", lambda e, hh=hh: e.scalar_tensor_tensor(
                    scrb2[:, hh:hh + 512], scrb[:, hh:hh + 512], NS_B, C_[:, hh:hh + 512],
                    ALU.mult, ALU.add),
                    deps=[t1] + ([prevh[(hh, "Bm")]] if (hh, "Bm") in prevh else st["scrb2"]))
                lastB = lastXT = None
                for j in range(j0, j0 + 4):
                    def mBX(e, w=w, j=j, j0=j0, x_src=x_src, hh=hh):
                        return e.matmul(pb[4][:, (j - j0) * 128:(j - j0 + 1) * 128],
                                        scrb2[:, hh + (j - j0) * 128: hh + (j - j0 + 1) * 128],
                                        x_src[:, j * 384 + w * 128: j * 384 + (w + 1) * 128],
                                        start=True, stop=True)
                    lastB = S.op("t", mBX, deps=([tBm, bank_free[4]] + dep_x) if j == j0 else ())
                if step == 0:
                    for j in range(j0, j0 + 4):
                        def mXTB(e, w=w, j=j, j0=j0, x_src=x_src, hh=hh):
                            return e.matmul(pb[5][:, (j - j0) * 128:(j - j0 + 1) * 128],
                                            x_src[:, j * 384 + w * 128: j * 384 + (w + 1) * 128],
                                            scrb2[:, hh + (j - j0) * 128: hh + (j - j0 + 1) * 128],
                                            start=True, stop=True)
                        lastXT = S.op("t", mXTB, deps=[bank_free[5]] if j == j0 else ())
                def upd(e, w=w, j0=j0, x_src=x_src, x_dst=x_dst):
                    return e.scalar_tensor_tensor(xap(x_dst, w, 4, j0), xap(x_src, w, 4, j0),
                                                  NS_A, pb[4][:, :].rearrange("p (j d) -> p j d", j=4),
                                                  ALU.mult, ALU.add)
                tX = S.op("v", upd, deps=[lastB] + (x2_tok if step == 1 and w == 0 and grp == 0 else []))
                bank_free[4] = tX
                new_dep_x.append(tX)
                if step == 0:
                    def updT(e, w=w, j0=j0, xT_src=xT_src):
                        return e.scalar_tensor_tensor(xap(rC, w, 4, j0), xap(xT_src, w, 4, j0),
                                                      NS_A, pb[5][:, :].rearrange("p (j d) -> p j d", j=4),
                                                      ALU.mult, ALU.add)
                    tXT = S.op("v", updT, deps=[lastXT])
                    bank_free[5] = tXT
                    new_dep_xT.append(tXT)
                prevh[(hh, "As")] = lastA2
                prevh[(hh, "t1")] = tBm
                prevh[(hh, "Bm")] = lastB if step == 1 else (lastXT or lastB)
        if step == 0:
            x_src, xT_src = rA, rC
            x_dst = rB
            dep_x, dep_xT = new_dep_x, new_dep_xT
        else:
            x2_tok = new_dep_x
    st["scrb"] = [prevh[(0, "As")], prevh[(512, "As")]]
    st["scrb2"] = [prevh[(0, "Bm")], prevh[(512, "Bm")]]
    st["scrC"] = [prevh[(0, "t1")], prevh[(512, "t1")]]
    xread = dep_x + dep_xT
    return x2_tok, xread


def w_update(S, pb, wT, wn_s, pk, RNM, PKS, scrf, rA, rB, rC, rD, x2_tok, xread,
             bank_free, w0fw, w1fw, w1Tfw, w2fw, ident_f, st):
    A_, B_, C_, D_ = scrf
    pre = {}
    red = []
    sq_war = st["scrB"]
    for w in range(3):
        for half in range(2):
            bk = pb[w * 2 + half]
            bi = w * 2 + half
            def mpre(e, w=w, half=half, bk=bk):
                return e.tensor_tensor(bk[:, :], wT[w][:, half * 512:(half + 1) * 512],
                                       bass.AP(rB, (half * 4) * 384 + w * 128,
                                               [[4096, 128], [384, 4], [1, 128]]), ALU.add)
            tp = S.op("v", mpre, deps=[bank_free[bi]] + x2_tok)
            pre[(w, half)] = (bk, tp)
            hh = ((w * 2 + half) % 2) * 512
            t1 = S.op("s", lambda e, bk=bk, hh=hh: e.activation(B_[:, hh:hh + 512], bk[:, :], AF.Square),
                      deps=[tp] + (sq_war if w == 0 and half == 0 else []))
            t2 = S.op("v", lambda e, w=w, half=half, hh=hh: e.reduce_sum(
                pk[:, RNM + w * 8 + half * 4: RNM + w * 8 + half * 4 + 4],
                B_[:, hh:hh + 512].rearrange("p (j d) -> p j d", j=4), axis=AX.X), deps=[t1])
            red.append(t2)
    st["scrB"] = red[-2:]
    c2 = S.op("s", lambda e: e.activation(pk[:, RNM:RNM + 24], pk[:, RNM:RNM + 24], AF.Ln), deps=red)
    c3 = S.op("s", lambda e: e.activation(pk[:, RNM:RNM + 24], pk[:, RNM:RNM + 24], AF.Exp, scale=0.5),
              deps=[c2])
    c4 = S.op("v", lambda e: e.tensor_scalar_add(pk[:, RNM:RNM + 24], pk[:, RNM:RNM + 24], 1e-5),
              deps=[c3])
    c5 = S.op("v", lambda e: e.reciprocal(pk[:, RNM:RNM + 24], pk[:, RNM:RNM + 24]), deps=[c4])
    c6 = S.op("v", lambda e: e.tensor_mul(pk[:, RNM:RNM + 24], pk[:, RNM:RNM + 24], wn_s[:, :]),
              deps=[c5])
    wnew = {}
    for w in range(3):
        for half in range(2):
            bk, tp = pre[(w, half)]
            sc_ap = bass.AP(pk, RNM + w * 8 + half * 4, [[PKS, 128], [1, 4], [0, 128]])
            def mnew(e, w=w, half=half, bk=bk, sc_ap=sc_ap):
                return e.tensor_tensor(wT[w][:, half * 512:(half + 1) * 512].rearrange(
                    "p (j d) -> p j d", j=4), bk[:, :].rearrange("p (j d) -> p j d", j=4),
                    sc_ap, ALU.mult)
            wnew[(w, half)] = S.op("v", mnew, deps=[c6, tp])
    for w in range(3):
        for half in range(2):
            bank_free[w * 2 + half] = wnew[(w, half)]
    fwmap = {0: w0fw, 1: w1fw, 2: w2fw}
    upd_tok = [None, None, None]
    for w in range(3):
        evs = []
        for half in range(2):
            bk = pb[half]
            trs = None
            for j4 in range(4):
                j = half * 4 + j4
                def mt(e, w=w, j=j, j4=j4, bk=bk):
                    return e.transpose(bass.AP(bk, j4 * 128, [[512, 128], [1, 128]]),
                                       wT[w][:, j * 128:(j + 1) * 128], ident_f[:, :])
                trs = S.op("t", mt, deps=[wnew[(w, 0)], wnew[(w, 1)], bank_free[half]] if j4 == 0 else ())
            ev = S.op("s", lambda e, w=w, half=half, bk=bk: e.activation(
                fwmap[w][:, half * 512:(half + 1) * 512], bk[:, :], AF.Copy), deps=[trs])
            bank_free[half] = ev
            evs.append(ev)
        upd_tok[w] = evs[-1]
    tcp = S.op("v", lambda e: e.tensor_copy(w1Tfw[:, :], wT[1][:, :]), deps=[wnew[(1, 0)], wnew[(1, 1)]])
    st["w_upd"] = [upd_tok[0], upd_tok[1], upd_tok[2]]
    st["w1t_cast"] = tcp
    st["rfree"] = x2_tok + xread + [wnew[(w, h)] for w in range(3) for h in range(2)] + [tcp]


def build_tail(ctx):
    nc, S, sb, pb, dr = ctx["nc"], ctx["S"], ctx["sb"], ctx["pb"], ctx["dr"]
    pt0, pt1, dbg, load = ctx["pt0"], ctx["pt1"], ctx["dbg"], ctx["load"]
    sba, OFF_S = ctx["sba"], ctx["OFF_S"]
    out, cc_in, cc_out = ctx["out"], ctx["cc_in"], ctx["cc_out"]
    oiN, attn, x1, x1nb, x1n, x2p = (sb["oiN"], sb["attn"], sb["x1"], sb["x1nb"],
                                     sb["x1n"], sb["x2p"])
    ones_b, ident_f, e64_s = sb["ones_b_s"], sb["ident_f_s"], sb["e64_s"]
    scan_done = ctx["scan_done"]


    wstr2_b = [sba("wstr2A", [128, 512], BF16, OFF_S + 4096),
               sba("wstr2B", [128, 512], BF16, OFF_S)]
    wg_b = sba("wg_b", [128, 128], BF16, OFF_S + 5120)
    xres = [sba(f"xres{i}", [128, 1024], F32, OFF_S + 8192 + i * 4096) for i in range(2)]
    hm_sb = sba("hm_sb", [128, 4096], BF16, OFF_S + 8192)
    p5k = sba("p5k", [128, 256], F32, OFF_S + 16384)
    mT_b = sba("mT_b", [128, 1024], BF16, OFF_S + 17408)
    mvb = sba("mvb", [128, 512], BF16, OFF_S + 19456)
    sqb = sba("sqb", [128, 1024], BF16, OFF_S + 20480)

    bank = {i: scan_done[-1] for i in range(6)}
    A2A = [[0, 1], [2, 3], [4, 5], [6, 7]]

    rms_toks, r_free = ctx["rms_toks"], ctx["r_free"]
    cc_toks = []
    for cch in range(4):
        d_c = S.dma("y", lambda e, cch=cch: e.dma_start(
            out=cc_in[cch].rearrange("(j p) t -> p j t", p=128),
            in_=bass.AP(oiN, cch * 512, [[16384, 128], [2048, 8], [1, 512]])),
            deps=rms_toks[cch])
        cc_toks.append(S.coll(lambda e, cch=cch: e.collective_compute(
            "AllGather", ALU.bypass, replica_groups=A2A, ins=[cc_in[cch]], outs=[cc_out[cch]]),
            deps=[d_c]))
    pid = nc.partition_id()
    t0 = (pid % 2) * 1024
    qkv = sb["qkv"]
    d_attn_t = []
    for tt in range(2):
        src = bass.AP(cc_out.tensor, (pid % 2) * 2097152 + tt * 1048576,
                      [[512, 128], [128 * 512, 16], [1, 512]])
        dst = bass.AP(qkv, 32768 + tt * 512, [[49152, 128], [1024, 16], [1, 512]])
        d_attn_t.append(S.dma("y" if tt == 0 else "s",
                              lambda e, src=src, dst=dst: e.dma_start(out=dst, in_=src),
                              deps=[cc_toks[tt], cc_toks[2 + tt]]))

    msb = sba("msb", [128, 1024], F32, OFF_S + 9728)
    rsb = sba("rsb", [128, 1024], F32, OFF_S + 9728 + 4096)
    last_wp = None
    last_x = None
    ev_toks = []
    st_mm = None
    sq_mm = None
    def wgrp_base(wi):
        return 16384 + (4 * (wi % 2)) * 2048 + 1024 + (wi // 2) * 512
    wp_mm = [None] * 4
    for oc in range(16):
        wi = oc % 4
        if oc < 2:
            d_wp = ctx["wpre_toks"][oc]
        else:
            d_wp = S.dma("g", lambda e, oc=oc, wi=wi: e.dma_start(
                out=bass.AP(qkv, wgrp_base(wi),
                            [[49152, 128], [2048, 4], [128, 4], [1, 128]]),
                in_=dr["wproj"][oc].rearrange("p k c -> p (k c)")),
                deps=rms_toks[3] if oc < 4 else [wp_mm[wi]])
        d_xr = S.dma("y", lambda e, oc=oc, t0=t0: e.dma_start(
            out=xres[oc % 2][:, :],
            in_=bass.AP(dr["xT"].tensor, oc * 128 * 2048 + t0, [[2048, 128], [1, 1024]])),
            deps=[ev_toks[-2]] if oc >= 2 else [])
        mms = {}
        for tt in range(2):
            mm = None
            for k in range(16):
                def mp(e, tt=tt, k=k, wi=wi):
                    wc = wgrp_base(wi) + (k // 4) * 2048 + (k % 4) * 128
                    ac = 32768 + k * 1024 + tt * 512
                    return e.matmul(pb[tt][:, :], qkv[:, wc:wc + 128],
                                    qkv[:, ac:ac + 512],
                                    start=(k == 0), stop=(k == 15))
                mm = S.op("t", mp, deps=[d_wp, d_attn_t[tt], bank[tt]] if k == 0 else ())
            mms[tt] = mm
        wp_mm[wi] = mms[1]
        evs = []
        for tt in range(2):
            def ev(e, oc=oc, tt=tt):
                return e.tensor_tensor(x1[:, oc * 1024 + tt * 512: oc * 1024 + (tt + 1) * 512],
                                       pb[tt][:, :], xres[oc % 2][:, tt * 512:(tt + 1) * 512],
                                       ALU.add)
            t_ = S.op("v", ev, deps=[mms[tt], d_xr])
            bank[tt] = t_
            evs.append(t_)
        t_b = S.op("s", lambda e, oc=oc: e.activation(
            mT_b[:, :], x1[:, oc * 1024:(oc + 1) * 1024], AF.Copy), deps=evs + ([st_mm] if st_mm else []))
        t_sq = S.op("s", lambda e: e.activation(sqb[:, :], mT_b[:, :], AF.Square),
                    deps=[t_b] + ([sq_mm] if sq_mm else []))
        for tt in range(2):
            def ms(e, tt=tt, oc=oc):
                return e.matmul(pb[2 + tt][:, :], ones_b[:, :], mT_b[:, tt * 512:(tt + 1) * 512],
                                start=(oc == 0), stop=(oc == 15))
            st_mm = S.op("t", ms, deps=[t_b] if tt == 0 else ())
            def mq(e, tt=tt, oc=oc):
                return e.matmul(pb[4 + tt][:, :], ones_b[:, :], sqb[:, tt * 512:(tt + 1) * 512],
                                start=(oc == 0), stop=(oc == 15))
            sq_mm = S.op("t", mq, deps=[t_sq] if tt == 0 else ())
        ev_toks.append(evs[-1])
    l1 = S.op("v", lambda e: e.tensor_scalar_mul(msb[:, 0:512], pb[2][:, :], 1.0 / 2048),
              deps=[st_mm, scan_done[-1]])
    l2 = S.op("v", lambda e: e.tensor_scalar_mul(msb[:, 512:1024], pb[3][:, :], 1.0 / 2048),
              deps=[st_mm])
    l3 = S.op("v", lambda e: e.tensor_scalar_mul(rsb[:, 0:512], pb[4][:, :], 1.0 / 2048),
              deps=[sq_mm])
    l4 = S.op("v", lambda e: e.tensor_scalar_mul(rsb[:, 512:1024], pb[5][:, :], 1.0 / 2048),
              deps=[sq_mm])
    l5 = S.op("v", lambda e: e.tensor_tensor(pb[2][:, :], msb[:, 0:512], msb[:, 0:512], ALU.mult),
              deps=[l1, l3])
    l6 = S.op("v", lambda e: e.tensor_tensor(pb[3][:, :], msb[:, 512:1024], msb[:, 512:1024], ALU.mult),
              deps=[l2, l4])
    l7 = S.op("v", lambda e: e.tensor_tensor(rsb[:, 0:512], rsb[:, 0:512], pb[2][:, :], ALU.subtract),
              deps=[l5])
    l8 = S.op("v", lambda e: e.tensor_tensor(rsb[:, 512:1024], rsb[:, 512:1024], pb[3][:, :], ALU.subtract),
              deps=[l6])
    l9 = S.op("v", lambda e: e.tensor_scalar_add(rsb[:, :], rsb[:, :], 1e-5), deps=[l7, l8])
    l10 = S.op("s", lambda e: e.activation(rsb[:, :], rsb[:, :], AF.Ln), deps=[l9])
    l11 = S.op("s", lambda e: e.activation(rsb[:, :], rsb[:, :], AF.Exp, scale=-0.5), deps=[l10])
    bank[2], bank[3], bank[4], bank[5] = l7, l8, l3, l4
    ap_toks = []
    for oc in range(16):
        sl = slice(oc * 1024, (oc + 1) * 1024)
        a1 = S.op("v", lambda e, sl=sl: e.tensor_tensor(x1n[:, sl], x1[:, sl], msb[:, :],
                                                        ALU.subtract), deps=[l11, ev_toks[oc]])
        a2 = S.op("g" if oc % 2 else "v", lambda e, sl=sl: e.tensor_tensor(
            x1n[:, sl], x1n[:, sl], rsb[:, :], ALU.mult), deps=[a1])
        a3 = S.op("s", lambda e, sl=sl: e.activation(x1nb[:, sl], x1n[:, sl], AF.Copy), deps=[a2])
        ap_toks.append((a2, a3))
    dbg("x1n", x1n[:, :], [ap_toks[-1][0]])

    d_wg = load("g", wg_b[:, :], dr["wgt"].rearrange("p k e -> p (k e)"))
    lg_mm = {}
    for ti in range(8):
        mm = None
        for k in range(16):
            def ml(e, ti=ti, k=k):
                return e.matmul(pb[0][:, ti * 8:(ti + 1) * 8],
                                x1nb[:, k * 1024 + ti * 128: k * 1024 + (ti + 1) * 128],
                                wg_b[:, k * 8:(k + 1) * 8], start=(k == 0), stop=(k == 15))
            mm = S.op("t", ml, deps=[d_wg, ap_toks[15][1], bank[0]] if (ti == 0 and k == 0) else ())
        lg_mm[ti] = mm
    LG, MX, NEG, ES, SE, SC, MS = 0, 64, 72, 80, 144, 152, 160
    s1 = S.op("v", lambda e: e.tensor_copy(p5k[:, LG:LG + 64], pb[0][:, 0:64]), deps=[lg_mm[7]])
    bank[0] = s1
    s2 = S.op("v", lambda e: e.reduce_max(p5k[:, MX:MX + 8],
                                          p5k[:, LG:LG + 64].rearrange("p (t e) -> p t e", t=8),
                                          axis=AX.X), deps=[s1])
    s3 = S.op("v", lambda e: e.tensor_scalar_mul(p5k[:, NEG:NEG + 8], p5k[:, MX:MX + 8], -1.0),
              deps=[s2])
    es_toks = []
    for ti in range(8):
        def se_(e, ti=ti):
            return e.activation(p5k[:, ES + ti * 8: ES + ti * 8 + 8], p5k[:, LG + ti * 8: LG + ti * 8 + 8],
                                AF.Exp, bias=p5k[:, NEG + ti: NEG + ti + 1])
        es_toks.append(S.op("s", se_, deps=[s3] if ti == 0 else ()))
    s4 = S.op("v", lambda e: e.reduce_sum(p5k[:, SE:SE + 8],
                                          p5k[:, ES:ES + 64].rearrange("p (t e) -> p t e", t=8),
                                          axis=AX.X), deps=es_toks)
    s5 = S.op("v", lambda e: e.reciprocal(p5k[:, SC:SC + 8], p5k[:, SE:SE + 8]), deps=[s4])
    s6 = S.op("v", lambda e: e.tensor_scalar_mul(p5k[:, SC:SC + 8], p5k[:, SC:SC + 8], 4.0), deps=[s5])
    ms_toks = []
    for ti in range(8):
        m1 = S.op("v", lambda e, ti=ti: e.tensor_scalar(
            p5k[:, MS + ti * 8: MS + ti * 8 + 8], p5k[:, LG + ti * 8: LG + ti * 8 + 8],
            p5k[:, MX + ti: MX + ti + 1], None, ALU.is_ge), deps=[s6] if ti == 0 else ())
        m2 = S.op("v", lambda e, ti=ti: e.tensor_scalar(
            p5k[:, MS + ti * 8: MS + ti * 8 + 8], p5k[:, MS + ti * 8: MS + ti * 8 + 8],
            p5k[:, SC + ti: SC + ti + 1], None, ALU.mult), deps=[m1])
        ms_toks.append(m2)
    z1 = S.op("g", lambda e: e.memset(mT_b[:, :], 0.0), deps=ms_toks + [lg_mm[7], st_mm, sq_mm])
    tr_toks = []
    for ti in range(8):
        bk = pb[2 + ti // 4]
        def mt(e, ti=ti, bk=bk):
            return e.transpose(bk[0:8, (ti % 4) * 128: (ti % 4 + 1) * 128],
                               p5k[:, MS + ti * 8: MS + ti * 8 + 8], ident_f[:, :])
        tr_toks.append(S.op("t", mt, deps=[ms_toks[ti]] + ([bank[2 + ti // 4]] if ti % 4 == 0 else [])))
    e1 = S.op("s", lambda e: e.activation(mT_b[0:8, 0:512], pb[2][0:8, :], AF.Copy),
              deps=[tr_toks[3], z1])
    e2 = S.op("s", lambda e: e.activation(mT_b[0:8, 512:1024], pb[3][0:8, :], AF.Copy),
              deps=[tr_toks[7], z1])
    bank[2], bank[3] = e1, e2
    hm_toks = []
    OFF_W = OFF_S + 23040
    wstr_b = [sba("wstrA", [128, 2048], BF16, OFF_W),
              sba("wstrB", [128, 2048], BF16, OFF_W + 4096)]
    wa_mm = [wp_mm[0], wp_mm[1]]
    for mt_ in range(4):
        wstr = wstr_b[mt_ % 2]
        d_wa = S.dma("g", lambda e, mt_=mt_, wstr=wstr: e.dma_start(
            out=wstr[:, :], in_=dr["wa"][mt_].rearrange("p k c -> p (k c)")),
            deps=[wa_mm[mt_ % 2]])
        for tt2 in range(2):
            mm = None
            for k in range(16):
                def mh(e, mt_=mt_, tt2=tt2, k=k, wstr=wstr):
                    return e.matmul(pb[4 + tt2][:, :], wstr[:, k * 128:(k + 1) * 128],
                                    x1nb[:, k * 1024 + tt2 * 512: k * 1024 + (tt2 + 1) * 512],
                                    start=(k == 0), stop=(k == 15))
                mm = S.op("t", mh, deps=[d_wa, bank[4 + tt2], ap_toks[15][1]] if k == 0 else ())
            wa_mm[mt_ % 2] = mm
            def mb(e, mt_=mt_, tt2=tt2):
                return e.matmul(pb[2][:, :], e64_s[:, mt_ * 128:(mt_ + 1) * 128],
                                mT_b[:, tt2 * 512:(tt2 + 1) * 512], start=True, stop=True)
            mmb = S.op("t", mb, deps=[bank[2], e1, e2])
            t_mv = S.op("s", lambda e: e.activation(mvb[:, :], pb[2][:, :], AF.Copy),
                        deps=[mmb] + (hm_toks[-1:] if hm_toks else []))
            bank[2] = t_mv
            def hmop(e, mt_=mt_, tt2=tt2):
                return e.tensor_tensor(hm_sb[:, mt_ * 1024 + tt2 * 512: mt_ * 1024 + (tt2 + 1) * 512],
                                       pb[4 + tt2][:, :], mvb[:, :], ALU.mult)
            t_hm = S.op("v", hmop, deps=[mm, t_mv])
            bank[4 + tt2] = t_hm
            hm_toks.append(t_hm)
    st2_mm = sq2_mm = None
    ev2_toks = []
    bw_mm = [hm_toks[0], hm_toks[0]]
    for oc in range(16):
        wstr2 = wstr2_b[oc % 2]
        d_bw = S.dma("g", lambda e, oc=oc, wstr2=wstr2: e.dma_start(
            out=wstr2[:, :], in_=dr["wbw"][oc].rearrange("p k c -> p (k c)")),
            deps=[bw_mm[oc % 2]])
        mms = {}
        for tt2 in range(2):
            mm = None
            for kk in range(4):
                def my(e, oc=oc, tt2=tt2, kk=kk, wstr2=wstr2):
                    return e.matmul(pb[tt2][:, :], wstr2[:, kk * 128:(kk + 1) * 128],
                                    hm_sb[:, kk * 1024 + tt2 * 512: kk * 1024 + (tt2 + 1) * 512],
                                    start=(kk == 0), stop=(kk == 3))
                mm = S.op("t", my, deps=[d_bw, bank[tt2]] + hm_toks if (kk == 0) else ())
            mms[tt2] = mm
        bw_mm[oc % 2] = mms[1]
        evs = []
        for tt2 in range(2):
            def ev(e, oc=oc, tt2=tt2):
                return e.tensor_tensor(x2p[:, oc * 1024 + tt2 * 512: oc * 1024 + (tt2 + 1) * 512],
                                       pb[tt2][:, :],
                                       x1n[:, oc * 1024 + tt2 * 512: oc * 1024 + (tt2 + 1) * 512],
                                       ALU.add)
            t_ = S.op("v", ev, deps=[mms[tt2], ap_toks[oc][0]])
            bank[tt2] = t_
            evs.append(t_)
        t_b = S.op("s", lambda e, oc=oc: e.activation(
            mT_b[:, :], x2p[:, oc * 1024:(oc + 1) * 1024], AF.Copy),
            deps=evs + ([st2_mm] if st2_mm else [e1, e2]))
        t_sq = S.op("s", lambda e: e.activation(sqb[:, :], mT_b[:, :], AF.Square),
                    deps=[t_b] + ([sq2_mm] if sq2_mm else []))
        for tt2 in range(2):
            def ms2(e, tt2=tt2, oc=oc):
                return e.matmul(pb[2 + tt2][:, :], ones_b[:, :], mT_b[:, tt2 * 512:(tt2 + 1) * 512],
                                start=(oc == 0), stop=(oc == 15))
            st2_mm = S.op("t", ms2, deps=[t_b, bank[2 + tt2]] if (tt2 == 0 or oc == 0) else ())
            def mq2(e, tt2=tt2, oc=oc):
                return e.matmul(pb[4 + tt2][:, :], ones_b[:, :], sqb[:, tt2 * 512:(tt2 + 1) * 512],
                                start=(oc == 0), stop=(oc == 15))
            sq2_mm = S.op("t", mq2, deps=[t_sq, bank[4 + tt2]] if (tt2 == 0 or oc == 0) else ())
        ev2_toks.append(evs[-1])
    l1 = S.op("v", lambda e: e.tensor_scalar_mul(msb[:, 0:512], pb[2][:, :], 1.0 / 2048),
              deps=[st2_mm, ap_toks[15][0]])
    l2 = S.op("v", lambda e: e.tensor_scalar_mul(msb[:, 512:1024], pb[3][:, :], 1.0 / 2048),
              deps=[st2_mm])
    l3 = S.op("v", lambda e: e.tensor_scalar_mul(rsb[:, 0:512], pb[4][:, :], 1.0 / 2048),
              deps=[sq2_mm])
    l4 = S.op("v", lambda e: e.tensor_scalar_mul(rsb[:, 512:1024], pb[5][:, :], 1.0 / 2048),
              deps=[sq2_mm])
    l5 = S.op("v", lambda e: e.tensor_tensor(pb[2][:, :], msb[:, 0:512], msb[:, 0:512], ALU.mult),
              deps=[l1, l3])
    l6 = S.op("v", lambda e: e.tensor_tensor(pb[3][:, :], msb[:, 512:1024], msb[:, 512:1024], ALU.mult),
              deps=[l2, l4])
    l7 = S.op("v", lambda e: e.tensor_tensor(rsb[:, 0:512], rsb[:, 0:512], pb[2][:, :], ALU.subtract),
              deps=[l5])
    l8 = S.op("v", lambda e: e.tensor_tensor(rsb[:, 512:1024], rsb[:, 512:1024], pb[3][:, :], ALU.subtract),
              deps=[l6])
    l9 = S.op("v", lambda e: e.tensor_scalar_add(rsb[:, :], rsb[:, :], 1e-5), deps=[l7, l8])
    l10 = S.op("s", lambda e: e.activation(rsb[:, :], rsb[:, :], AF.Ln), deps=[l9])
    l11 = S.op("s", lambda e: e.activation(rsb[:, :], rsb[:, :], AF.Exp, scale=-0.5), deps=[l10])
    for oc in range(16):
        sl = slice(oc * 1024, (oc + 1) * 1024)
        a1 = S.op("v", lambda e, sl=sl: e.tensor_tensor(x1n[:, sl], x2p[:, sl], msb[:, :],
                                                        ALU.subtract), deps=[l11, ev2_toks[oc]])
        a2 = S.op("g" if oc % 2 else "v", lambda e, sl=sl: e.tensor_tensor(
            x1n[:, sl], x1n[:, sl], rsb[:, :], ALU.mult), deps=[a1])
        S.dma(("y", "s", "g")[oc % 3], lambda e, oc=oc, sl=sl: e.dma_start(
            out=out[oc * 128:(oc + 1) * 128, :], in_=x1n[:, sl]), deps=[a2])


def kernel(**inputs):
    per_core = _prep(inputs)
    nc, _ = build_program()
    res = run_bass_kernel_spmd(nc, per_core, core_ids=list(range(8)))
    out = np.empty((B, L, D), np.float32)
    for c in range(8):
        b, half = c // 2, c % 2
        out[b, half * TOK:(half + 1) * TOK, :] = res.results[c]["out"].T
    return out


# version 09
# speedup vs baseline: 1.0137x; 1.0137x over previous
import numpy as np
import ml_dtypes
import concourse.bass as bass
import concourse.mybir as mybir
from concourse.bass_utils import run_bass_kernel_spmd

F32 = mybir.dt.float32
BF16 = mybir.dt.bfloat16
AF = mybir.ActivationFunctionType
ALU = mybir.AluOpType
AX = mybir.AxisListType

BASE_LR_INV = float(np.log(np.expm1(0.01)))
NS_A, NS_B, NS_C = 3.4445, -4.775, 2.0315

B, L, D = 4, 2048, 2048
SERIAL = 0
SCAN_LIMIT = 99
QP_LIMIT = 99
QH = 99
CH, NCH, KD, TOK = 512, 4, 16, 1024
BF = ml_dtypes.bfloat16


class Sched:
    ENGS = ("t", "v", "s", "g", "y")

    def __init__(self):
        self.q = {e: [] for e in self.ENGS}
        self.cnt = {"t": 0, "v": 0, "s": 0, "g": 0,
                    "dy": 0, "ds": 0, "dg": 0, "cc": 0}

    PAIR = {2: (("t", "v"), ("t", "s"), ("t", "g"), ("v", "t"), ("s", "t"), ("g", "t")),
            3: (("t", "v"), ("v", "t")), 4: (("t", "s"), ("s", "t")),
            5: (("t", "g"), ("g", "t")), 6: (("v", "s"), ("s", "v"))}

    def op(self, eng, fn, deps=()):
        if SERIAL == 1 and getattr(self, "last", None):
            deps = tuple(deps) + (self.last,)
        elif SERIAL >= 2:
            lasts = getattr(self, "lasts", {})
            for (a, b) in self.PAIR[SERIAL]:
                if eng == a and b in lasts:
                    deps = tuple(deps) + (lasts[b],)
        self.cnt[eng] += 1
        self.q[eng].append((fn, tuple(deps), (eng, 1)))
        self.last = (eng, self.cnt[eng])
        if not hasattr(self, "lasts"):
            self.lasts = {}
        self.lasts[eng] = (eng, self.cnt[eng])
        return (eng, self.cnt[eng])

    def dma(self, queue, fn, deps=()):
        sem = {"y": "dy", "s": "ds", "g": "dg"}[queue]
        if SERIAL and getattr(self, "last", None):
            deps = tuple(deps) + (self.last,)
        deps = tuple(deps) + ((sem, self.cnt[sem]),)
        self.cnt[sem] += 16
        self.q[queue].append((fn, deps, (sem, 16)))
        if SERIAL:
            self.last = (sem, self.cnt[sem])
        return (sem, self.cnt[sem])

    def coll(self, fn, deps=()):
        self.cnt["cc"] += 1
        self.q["g"].append((fn, tuple(deps), ("cc", 1)))
        return ("cc", self.cnt["cc"])

    W = 3584

    def finalize(self):
        dep_vals = {k: set() for k in self.cnt}
        for qk in self.q:
            for _, deps, _ in self.q[qk]:
                for (dn, dv) in deps:
                    if dv > 0:
                        dep_vals[dn].add(dv)
        self.rank = {}
        self.rtot = {}
        for name, vals in dep_vals.items():
            r = {}
            step = 16 if name in ("dy", "ds", "dg") else 1
            for i, v in enumerate(sorted(vals)):
                r[v] = (i + 1) * step
            self.rank[name] = r
            self.rtot[name] = len(vals) * step

    def replay(self, engine, eng_key, sems):
        waited = {}
        cur = {k: 0 for k in self.cnt}
        for fn, deps, (sem_name, inc) in self.q[eng_key]:
            for (dn, dv) in deps:
                if dv <= 0:
                    continue
                rv = self.rank[dn][dv]
                k = (rv - 1) // self.W
                v = rv - k * self.W
                if waited.get((dn, k), -1) < v:
                    engine.wait_ge(sems[(dn, k)], v)
                    waited[(dn, k)] = v
            cur[sem_name] += inc
            n = cur[sem_name]
            ins = fn(engine)
            if n in self.rank[sem_name]:
                rv = self.rank[sem_name][n]
                k = (rv - 1) // self.W
                ins.then_inc(sems[(sem_name, k)], inc)

    def nsems(self, name):
        return max(1, (self.rtot.get(name, 0) + self.W - 1) // self.W)


def _prep(inp):
    x = np.ascontiguousarray(np.asarray(inp["L0_attn_x"], np.float32))
    w0 = np.asarray(inp["L0_attn_w0"], np.float32)
    w1 = np.asarray(inp["L0_attn_w1"], np.float32)
    w2 = np.asarray(inp["L0_attn_w2"], np.float32)
    Wqkv = np.asarray(inp["L0_attn_W_qkv"], np.float32)
    Wlr = np.asarray(inp["L0_attn_W_lr"], np.float32)
    blr = np.asarray(inp["L0_attn_b_lr"], np.float32)
    Wproj = np.asarray(inp["L0_attn_W_proj"], np.float32)
    onorm = np.asarray(inp["L0_attn_onorm_w"], np.float32)
    A = np.asarray(inp["L0_ffn_A"], np.float32)
    Bw = np.asarray(inp["L0_ffn_Bw"], np.float32)
    Wg = np.asarray(inp["L0_ffn_Wg"], np.float32)

    wproj_r = np.ascontiguousarray(Wproj.reshape(16, 128, 16, 128).transpose(2, 1, 0, 3))
    AT = A.transpose(2, 0, 1).reshape(D, 512)
    wa_r = np.ascontiguousarray(AT.reshape(16, 128, 4, 128).transpose(2, 1, 0, 3))
    Bw_l = Bw.transpose(0, 2, 1).reshape(512, D)
    wbw_r = np.ascontiguousarray(Bw_l.reshape(4, 128, 16, 128).transpose(2, 1, 0, 3))
    wgt_r = np.ascontiguousarray(Wg.T.reshape(16, 128, 8).transpose(1, 0, 2))
    onorm_r = np.ascontiguousarray(onorm.reshape(128, 1))

    ones_b = np.ones((128, 128), dtype=BF)
    ones_f = np.ones((128, 128), dtype=np.float32)
    ident_b = np.eye(128, dtype=np.float32).astype(BF)
    ident_f = np.eye(128, dtype=np.float32)
    e64 = np.zeros((128, 512), np.float32)
    for col in range(512):
        e64[col // 64, col] = 1.0
    e64_b = e64.astype(BF)

    xT = [np.ascontiguousarray(x[b].T) for b in range(B)]

    per_core = []
    for c in range(8):
        b, half = c // 2, c % 2
        hs = slice(half * 8, half * 8 + 8)
        Wcols = np.ascontiguousarray(Wqkv.reshape(D, 3, 16, 128)[:, :, hs, :].reshape(D, 24 * 128))
        wq_r = np.ascontiguousarray(Wcols.reshape(16, 128, 24, 128).transpose(2, 1, 0, 3))
        Wlrc = Wlr.reshape(D, 3, 16)[:, :, hs].reshape(D, 24)
        wlr_r = np.ascontiguousarray(Wlrc.reshape(16, 128, 24).transpose(1, 0, 2))
        blr_r = np.ascontiguousarray(blr.reshape(3, 16)[:, hs].reshape(24, 1))
        w0h, w1h, w2h = w0[hs], w1[hs], w2[hs]
        pc = {
            "xT": xT[b], "wq": wq_r, "wlr": wlr_r, "blr": blr_r,
            "w0t": np.ascontiguousarray(w0h.transpose(2, 0, 1)),
            "w1t": np.ascontiguousarray(w1h.transpose(2, 0, 1)),
            "w2t": np.ascontiguousarray(w2h.transpose(2, 0, 1)),
            "w0o": np.ascontiguousarray(w0h.transpose(1, 0, 2)),
            "w1o": np.ascontiguousarray(w1h.transpose(1, 0, 2)),
            "w2o": np.ascontiguousarray(w2h.transpose(1, 0, 2)),
            "wproj": wproj_r, "wa": wa_r, "wbw": wbw_r, "wgt": wgt_r,
            "onorm": onorm_r, "ones_b": ones_b, "ones_f": ones_f,
            "ident_b": ident_b, "ident_f": ident_f, "e64_b": e64_b,
        }
        per_core.append(pc)
    return per_core


def build_program(upto=99, dbg_names=()):
    nc = bass.Bass("TRN2", target_bir_lowering=False)
    S = Sched()
    dbg_outs = {}

    dr = {}
    def di(name, shape, dt=F32):
        dr[name] = nc.dram_tensor(name, list(shape), dt, kind="ExternalInput").ap()
    di("xT", (D, L))
    di("wq", (24, 128, 16, 128))
    di("wlr", (128, 16, 24))
    di("blr", (24, 1))
    for n in ("w0t", "w1t", "w2t", "w0o", "w1o", "w2o"):
        di(n, (128, 8, 128))
    di("wproj", (16, 128, 16, 128))
    di("wa", (4, 128, 16, 128))
    di("wbw", (16, 128, 4, 128))
    di("wgt", (128, 16, 8))
    di("onorm", (128, 1))
    di("ones_b", (128, 128), BF16); di("ones_f", (128, 128))
    di("ident_b", (128, 128), BF16); di("ident_f", (128, 128))
    di("e64_b", (128, 512), BF16)
    out = nc.dram_tensor("out", [D, TOK], F32, kind="ExternalOutput").ap()
    cc_in = nc.dram_tensor("cc_in", [4, 1024, 512], BF16, kind="Internal").ap()
    cc_out = nc.dram_tensor("cc_out", [4, 2048, 512], BF16, kind="Internal").ap()

    BASE = (nc.sbuf_base + 31) // 32 * 32
    OFF_Q, OFF_A, OFF_S = BASE, BASE + 98304, BASE + 163840
    SCR = 23040
    OFF_FIX = OFF_S + SCR
    ARENA = 211680
    nc.alloc_sbuf_tensor("arena", [128, ARENA], mybir.dt.int8)

    sb = {}
    def sba(name, shape, dt, off):
        assert off % 32 == 0, (name, off)
        assert off >= BASE
        sb[name] = nc.alloc_sbuf_tensor_at(name, list(shape), dt, offset=off)
        return sb[name]

    qkv = sba("qkv", [128, 24 * 2048], BF16, OFF_Q)
    x1 = sba("x1", [128, 16 * 1024], F32, OFF_A)
    x2p = sba("x2p", [128, 16 * 1024], F32, OFF_A)
    x1nb = sba("x1nb", [128, 16 * 1024], BF16, OFF_Q + 65536)
    xT_sb = sba("xT_sb", [128, 16 * 2048], BF16, OFF_A)
    oiN = sba("oiN", [128, 8 * 2048], BF16, OFF_A)
    attn = sba("attn", [128, 16 * 1024], BF16, OFF_A + 32768)
    x1n = sba("x1n", [128, 16 * 1024], F32, OFF_Q)

    w0T = sba("w0T", [128, 8 * 128], F32, OFF_FIX)
    w1T = sba("w1T", [128, 8 * 128], F32, OFF_FIX + 4096)
    w2T = sba("w2T", [128, 8 * 128], F32, OFF_FIX + 8192)
    w0fw = sba("w0fw", [128, 8 * 128], BF16, OFF_FIX + 12288)
    w1fw = sba("w1fw", [128, 8 * 128], BF16, OFF_FIX + 14336)
    w1Tfw = sba("w1Tfw", [128, 8 * 128], BF16, OFF_FIX + 16384)
    w2fw = sba("w2fw", [128, 8 * 128], BF16, OFF_FIX + 18432)
    CON = OFF_FIX + 20480
    ones_b = sba("ones_b_s", [128, 128], BF16, CON)
    ident_b = sba("ident_b_s", [128, 128], BF16, CON + 256)
    ones_f = sba("ones_f_s", [128, 128], F32, CON + 512)
    ident_f = sba("ident_f_s", [128, 128], F32, CON + 1024)
    e64_s = sba("e64_s", [128, 512], BF16, CON + 1536)
    onorm_s = sba("onorm_s", [128, 1], F32, CON + 2560)
    blr_s = sba("blr_s", [24, 1], F32, CON + 2592)
    lr_tok = sba("lr_tok", [128, 16 * 24], F32, CON + 2624)
    wn_s = sba("wn_s", [128, 24], F32, CON + 4160)

    wq_b = [sba(f"wq_b{i}", [128, 16 * 128], BF16, OFF_S + i * 4096) for i in range(2)]
    lrp = sba("lrp", [24, 2048], F32, OFF_S + 8192)
    sp_r = sba("sp_r", [24, 512], F32, OFF_S + 16384)
    sp_a = sba("sp_a", [24, 512], F32, OFF_S + 18432)
    wlr_b = sba("wlr_b", [128, 16 * 24], BF16, OFF_S + 20480)

    pb = [nc.alloc_psum_tensor(f"pb{i}", [128, 512], F32) for i in range(6)]
    pt0 = nc.alloc_psum_tensor("pt0", [128, 1024], BF16)
    pt1 = nc.alloc_psum_tensor("pt1", [128, 1024], BF16)

    sems = None

    def load(qu, dst, src, deps=()):
        return S.dma(qu, lambda e, d=dst, s=src: e.dma_start(out=d, in_=s), deps)

    load("y", ones_b[:, :], dr["ones_b"])
    load("y", ident_b[:, :], dr["ident_b"])
    load("y", ones_f[:, :], dr["ones_f"])
    load("y", ident_f[:, :], dr["ident_f"])
    load("y", e64_s[:, :], dr["e64_b"])
    load("y", onorm_s[:, :], dr["onorm"])
    d_blr = load("y", blr_s[:, :], dr["blr"])
    load("y", w0T[:, :], dr["w0t"].rearrange("p j d -> p (j d)"))
    load("y", w1T[:, :], dr["w1t"].rearrange("p j d -> p (j d)"))
    d_wstate = load("y", w2T[:, :], dr["w2t"].rearrange("p j d -> p (j d)"))
    load("g", w0fw[:, :], dr["w0o"].rearrange("p j d -> p (j d)"))
    load("g", w1fw[:, :], dr["w1o"].rearrange("p j d -> p (j d)"))
    load("g", w1Tfw[:, :], dr["w1t"].rearrange("p j d -> p (j d)"))
    d_wfw = load("g", w2fw[:, :], dr["w2o"].rearrange("p j d -> p (j d)"))
    d_consts = ("dy", S.cnt["dy"])

    xT_dst = xT_sb[:, :].rearrange("p (k t) -> p k t", k=16)
    xT_src = dr["xT"].rearrange("(k p) t -> p k t", p=128)
    d_xt0 = load("g", xT_dst[:, :, 0:1024], xT_src[:, :, 0:1024])
    d_xt1 = load("g", xT_dst[:, :, 1024:2048], xT_src[:, :, 1024:2048])
    d_wlr = load("g", wlr_b[:, :], dr["wlr"].rearrange("p k c -> p (k c)"))

    t_blradj = S.op("v", lambda e: e.tensor_scalar_add(blr_s[:, :], blr_s[:, :], BASE_LR_INV),
                    deps=[d_blr])
    ev_tok = {}
    last_evac = {tt: None for tt in range(4)}
    last_mm_of_cc = {}
    for cc in range(24):
        par = cc % 2
        wdeps = [] if cc < 2 else [last_mm_of_cc[cc - 2]]
        d_wq = load("g", wq_b[par][:, :], dr["wq"][cc].rearrange("p k c -> p (k c)"), wdeps)
        for tt in range(4):
            deps = [d_wq, d_xt0 if tt < 2 else d_xt1]
            if last_evac[tt] is not None:
                deps.append(last_evac[tt])
            mm_tok = None
            for k in range(16):
                def mk(e, par=par, tt=tt, k=k):
                    return e.matmul(pb[tt][:, :], wq_b[par][:, k * 128:(k + 1) * 128],
                                    xT_sb[:, k * 2048 + tt * 512: k * 2048 + (tt + 1) * 512],
                                    start=(k == 0), stop=(k == 15))
                mm_tok = S.op("t", mk, deps=deps if k == 0 else ())
            last_mm_of_cc[cc] = mm_tok
            def ev(e, cc=cc, tt=tt):
                return e.activation(qkv[:, cc * 2048 + tt * 512: cc * 2048 + (tt + 1) * 512],
                                    pb[tt][:, :], AF.Silu)
            tok = S.op("s", ev, deps=[mm_tok])
            ev_tok[(cc, tt)] = tok
            last_evac[tt] = tok

    lr_ev = []
    for tt in range(4):
        mm_tok = None
        for k in range(16):
            def mk(e, tt=tt, k=k):
                return e.matmul(pb[4][0:24, :], wlr_b[:, k * 24:(k + 1) * 24],
                                xT_sb[:, k * 2048 + tt * 512: k * 2048 + (tt + 1) * 512],
                                start=(k == 0), stop=(k == 15))
            deps = [d_wlr, d_xt0 if tt < 2 else d_xt1] + ([lr_ev[-1]] if lr_ev else [])
            mm_tok = S.op("t", mk, deps=deps if k == 0 else ())
        def ev(e, tt=tt):
            return e.tensor_scalar(lrp[:, tt * 512:(tt + 1) * 512], pb[4][0:24, :],
                                   blr_s[:, 0:1], None, ALU.add)
        lr_ev.append(S.op("v", ev, deps=[mm_tok, t_blradj]))
    sp_done = []
    for hh in range(4):
        sl = slice(hh * 512, (hh + 1) * 512)
        war = list(lr_ev) + ([sp_done[-1]] if sp_done else [])
        a1 = S.op("s", lambda e, sl=sl: e.activation(sp_r[:, :], lrp[:, sl], AF.Relu), deps=war)
        a2 = S.op("s", lambda e, sl=sl: e.activation(sp_a[:, :], lrp[:, sl], AF.Abs), deps=war)
        a3 = S.op("s", lambda e, sl=sl: e.activation(sp_a[:, :], sp_a[:, :], AF.Exp, scale=-1.0), deps=[a2])
        a4 = S.op("s", lambda e, sl=sl: e.activation(sp_a[:, :], sp_a[:, :], AF.Ln, bias=1.0), deps=[a3])
        sp_done.append(S.op("v", lambda e, sl=sl: e.tensor_add(lrp[:, sl], sp_a[:, :], sp_r[:, :]),
                            deps=[a4, a1]))
    lrt_last = None
    for g4 in range(4):
        trs = None
        for u in range(4):
            tt16 = g4 * 4 + u
            def mk(e, tt16=tt16, u=u):
                return e.transpose(pb[5][:, u * 24:(u + 1) * 24],
                                   lrp[0:24, tt16 * 128:(tt16 + 1) * 128],
                                   ident_f[0:24, 0:24])
            dd = list(sp_done) + ([lrt_last] if lrt_last else [])
            trs = S.op("t", mk, deps=dd if u == 0 else ())
        lrt_last = S.op("v", lambda e, g4=g4: e.tensor_copy(lr_tok[:, g4 * 96:(g4 + 1) * 96],
                                                            pb[5][:, 0:96]), deps=[trs])
    p1_done = dict(qkv=ev_tok[(23, 3)], lr=lrt_last, d_xt=(d_xt0, d_xt1))

    def dbg(name, ap_, deps):
        if name not in dbg_names:
            return None
        o = nc.dram_tensor("dbg_" + name, list(ap_.shape), ap_.tensor.dtype,
                           kind="ExternalOutput").ap()
        dbg_outs[name] = True
        return S.dma("y", lambda e, o=o, a=ap_: e.dma_start(out=o, in_=a), deps)

    dbg("qkv", qkv[:, :], [ev_tok[(23, 3)]])
    dbg("lr", lr_tok[:, :], [lrt_last])

    ctx = dict(nc=nc, S=S, sb=sb, pb=pb, pt0=pt0, pt1=pt1, dr=dr, dbg=dbg, load=load,
               p1=p1_done, out=out, cc_in=cc_in, cc_out=cc_out, OFF_S=OFF_S, sba=sba,
               BASE=BASE, OFF_A=OFF_A, OFF_A2=OFF_A + 32768, OFF_Q=OFF_Q,
               d_wstate=d_wstate, d_wfw=d_wfw, d_consts=d_consts)
    if upto >= 2:
        build_scan(ctx)
    if upto >= 3:
        build_tail(ctx)

    S.op("g", lambda e: e.engine_nop(),
         deps=[("dy", S.cnt["dy"]), ("dg", S.cnt["dg"]), ("ds", S.cnt["ds"])])

    S.finalize()
    sems = {}
    for name in ("t", "v", "s", "g", "dy", "ds", "dg", "cc"):
        for k in range(S.nsems(name)):
            sems[(name, k)] = nc.alloc_semaphore(f"{name}{k}")
    with nc.Block() as blk:
        @blk.tensor
        def _(e):
            S.replay(e, "t", sems)
        @blk.vector
        def _(e):
            S.replay(e, "v", sems)
        @blk.scalar
        def _(e):
            S.replay(e, "s", sems)
        @blk.gpsimd
        def _(e):
            S.replay(e, "g", sems)
        @blk.sync
        def _(e):
            S.replay(e, "y", sems)
    return nc, dbg_outs


def build_scan(ctx):
    nc, S, sb, pb, dr = ctx["nc"], ctx["S"], ctx["sb"], ctx["pb"], ctx["dr"]
    pt0, pt1, dbg = ctx["pt0"], ctx["pt1"], ctx["dbg"]
    sba, OFF_S = ctx["sba"], ctx["OFF_S"]
    qkv, oiN = sb["qkv"], sb["oiN"]
    w0T, w1T, w2T = sb["w0T"], sb["w1T"], sb["w2T"]
    w0fw, w1fw, w1Tfw, w2fw = sb["w0fw"], sb["w1fw"], sb["w1Tfw"], sb["w2fw"]
    ones_b, ident_b = sb["ones_b_s"], sb["ident_b_s"]
    ones_f, ident_f = sb["ones_f_s"], sb["ident_f_s"]
    onorm_s, lr_tok, wn_s = sb["onorm_s"], sb["lr_tok"], sb["wn_s"]
    wT = [w0T, w1T, w2T]
    p1 = ctx["p1"]

    RH = ctx["OFF_A2"]
    rA = sba("rA", [128, 4096], BF16, RH)
    rB = sba("rB", [128, 4096], BF16, RH + 8192)
    rC = sba("rC", [128, 4096], BF16, RH + 16384)
    rD = sba("rD", [128, 4096], BF16, RH + 24576)
    scrf = [sba(f"scf{i}", [128, 1024], F32, OFF_S + i * 4096) for i in range(4)]
    pk = sba("pk", [128, 384], F32, OFF_S + 16384)
    scrb = sba("scrb", [128, 1024], BF16, OFF_S + 17920)
    scrb2 = sba("scrb2", [128, 1024], BF16, OFF_S + 19968)
    KIN, QIN, FRB, RNM, A0C, A2C = 0, 128, 256, 280, 304, 336
    PKS = 384
    LRS = 16 * 24
    A_, B_, C_, D_ = scrf

    st = dict()

    d_ws = ctx["d_wstate"]
    last = [p1["qkv"], p1["lr"], d_ws, ctx["d_wfw"], ctx["d_consts"]]
    for w in range(3):
        t1 = S.op("s", lambda e, w=w: e.activation(A_[:, :], wT[w][:, :], AF.Square), deps=last)
        r1 = S.op("v", lambda e, w=w: e.reduce_sum(
            wn_s[:, w * 8:w * 8 + 8], A_[:, :].rearrange("p (j d) -> p j d", j=8), axis=AX.X),
            deps=[t1])
        last = [r1]
    t1 = S.op("s", lambda e: e.activation(wn_s[:, :], wn_s[:, :], AF.Ln), deps=last)
    wn_done = S.op("s", lambda e: e.activation(wn_s[:, :], wn_s[:, :], AF.Exp, scale=0.5), deps=[t1])

    st["w_upd"] = [wn_done] * 3
    st["w1t_cast"] = wn_done
    st["rfree"] = [wn_done]
    st["pt0"] = [wn_done]
    st["pt1"] = [wn_done]
    bank_free = {i: wn_done for i in range(6)}

    h_evk = h_evq = h_sqk = h_sqq = None
    h_last = []
    for c0 in range(NCH):
        for j in range(8):
            ksl = qkv[:, 8 * 2048 + j * 2048 + c0 * 512: 8 * 2048 + j * 2048 + (c0 + 1) * 512]
            qsl = qkv[:, j * 2048 + c0 * 512: j * 2048 + (c0 + 1) * 512]
            trk = trq = None
            for u in range(4):
                def mk(e, u=u, ksl=ksl):
                    return e.transpose(pt0[:, u * 128:(u + 1) * 128], ksl[:, u * 128:(u + 1) * 128],
                                       ident_b[:, :])
                trk = S.op("t", mk, deps=st["pt0"] + [p1["qkv"]] if u == 0 else ())
            for u in range(4):
                def mq(e, u=u, qsl=qsl):
                    return e.transpose(pt1[:, u * 128:(u + 1) * 128], qsl[:, u * 128:(u + 1) * 128],
                                       ident_b[:, :])
                trq = S.op("t", mq, deps=st["pt1"] if u == 0 else ())
            h_evk = S.op("v", lambda e: e.tensor_copy(scrb[:, 0:512], pt0[:, 0:512]),
                         deps=[trk] + ([h_sqk] if h_sqk else []))
            h_evq = S.op("s", lambda e: e.activation(scrb[:, 512:1024], pt1[:, 0:512], AF.Copy),
                         deps=[trq] + ([h_sqq] if h_sqq else []))
            st["pt0"] = [h_evk]
            st["pt1"] = [h_evq]
            h_sqk = S.op("s", lambda e: e.activation(A_[:, 0:512], scrb[:, 0:512], AF.Square),
                         deps=[h_evk] + ([h_last[-2]] if len(h_last) >= 2 else []))
            h_sqq = S.op("v", lambda e: e.activation(A_[:, 512:1024], scrb[:, 512:1024], AF.Square)
                         if False else e.tensor_mul(A_[:, 512:1024], scrb[:, 512:1024], scrb[:, 512:1024]),
                         deps=[h_evq] + ([h_last[-1]] if len(h_last) >= 2 else []))
            rk = S.op("v", lambda e, c0=c0, j=j: e.reduce_sum(
                pk[:, KIN + c0 * 32 + j * 4: KIN + c0 * 32 + j * 4 + 4],
                A_[:, 0:512].rearrange("p (u d) -> p u d", u=4), axis=AX.X), deps=[h_sqk])
            rq = S.op("v", lambda e, c0=c0, j=j: e.reduce_sum(
                pk[:, QIN + c0 * 32 + j * 4: QIN + c0 * 32 + j * 4 + 4],
                A_[:, 512:1024].rearrange("p (u d) -> p u d", u=4), axis=AX.X), deps=[h_sqq])
            h_last += [rk, rq]
    h1 = S.op("s", lambda e: e.activation(pk[:, 0:256], pk[:, 0:256], AF.Ln), deps=h_last[-2:])
    h2 = S.op("s", lambda e: e.activation(pk[:, 0:256], pk[:, 0:256], AF.Exp, scale=0.5), deps=[h1])
    h3 = S.op("v", lambda e: e.tensor_scalar_add(pk[:, 0:256], pk[:, 0:256], 1e-5), deps=[h2])
    hq_done = S.op("v", lambda e: e.reciprocal(pk[:, 0:256], pk[:, 0:256]), deps=[h3])
    st["scrA"] = [hq_done]
    st["scrB"] = [wn_done]
    st["scrC"] = [wn_done]
    st["scrD"] = [wn_done]
    st["scrb"] = [h_sqk, h_sqq]
    st["scrb2"] = [wn_done]
    st["alpha_free"] = [wn_done]
    st["oi_q"] = {}

    QB, KB, VB = 0, 8 * 2048, 16 * 2048
    oiN_unscaled = {}
    rms_done = {}

    def emit_qpath(cm1):
        toks = []
        prev = {}
        for j in range(8):
            h = (j % 2) * 512
            bq, bh, bo = (0, 1, 2) if j % 2 == 0 else (3, 4, 5)
            qsl = qkv[:, QB + j * 2048 + cm1 * 512: QB + j * 2048 + (cm1 + 1) * 512]
            mmg = mmh = None
            for u in range(4):
                def mg(e, j=j, u=u, qsl=qsl, bq=bq):
                    return e.matmul(pb[bq][:, u * 128:(u + 1) * 128], qsl[:, u * 128:(u + 1) * 128],
                                    w0fw[:, j * 128:(j + 1) * 128], start=True, stop=True)
                mmg = S.op("t", mg, deps=[bank_free[bq], st["w_upd"][0], p1["qkv"]] if u == 0 else ())
            for u in range(4):
                def mh(e, j=j, u=u, qsl=qsl, bh=bh):
                    return e.matmul(pb[bh][:, u * 128:(u + 1) * 128], qsl[:, u * 128:(u + 1) * 128],
                                    w2fw[:, j * 128:(j + 1) * 128], start=True, stop=True)
                mmh = S.op("t", mh, deps=[bank_free[bh], st["w_upd"][2]] if u == 0 else ())
            war = prev.get((j % 2, "A"), st["scrA"][-1] if j < 2 else None)
            t_sl = None
            for u in range(4):
                def msl(e, u=u, h=h, bq=bq, cm1=cm1, j=j):
                    return e.activation(A_[:, h + u * 128: h + (u + 1) * 128],
                                        pb[bq][:, u * 128:(u + 1) * 128], AF.Silu,
                                        scale=pk[:, QIN + cm1 * 32 + j * 4 + u: QIN + cm1 * 32 + j * 4 + u + 1])
                t_sl = S.op("s", msl,
                            deps=([mmg, hq_done] + ([war] if war else []) +
                                  (st["scrA"] if j < 2 else [])) if u == 0 else ())
            sq_ap = bass.AP(pk, QIN + cm1 * 32 + j * 4, [[PKS, 128], [1, 4], [0, 128]])
            t_qhn = S.op("v", lambda e, sq_ap=sq_ap, h=h, bh=bh: e.tensor_tensor(
                B_[:, h:h + 512].rearrange("p (u c) -> p u c", u=4),
                pb[bh][:, :].rearrange("p (u c) -> p u c", u=4), sq_ap, ALU.mult),
                deps=[mmh] + ([prev[(j % 2, "B")]] if (j % 2, "B") in prev else st["scrB"]))
            bank_free[bq], bank_free[bh] = t_sl, t_qhn
            t_hq = S.op("g", lambda e, h=h: e.tensor_mul(scrb[:, h:h + 512], A_[:, h:h + 512],
                                                         B_[:, h:h + 512]),
                        deps=[t_sl, t_qhn] + ([prev[(j % 2, "hq")]] if (j % 2, "hq") in prev else st["scrb"]))
            trs = None
            ptx = pt0 if j % 2 == 0 else pt1
            for u in range(4):
                def mt(e, u=u, h=h, ptx=ptx):
                    return e.transpose(ptx[:, u * 128:(u + 1) * 128],
                                       scrb[:, h + u * 128: h + (u + 1) * 128], ident_b[:, :])
                trs = S.op("t", mt, deps=[t_hq] + (st["pt0"] if j % 2 == 0 else st["pt1"]) if u == 0 else ())
            if j % 2 == 0:
                t_hqT = S.op("s", lambda e, h=h, ptx=ptx: e.activation(
                    scrb2[:, h:h + 512], ptx[:, 0:512], AF.Copy),
                    deps=[trs] + ([prev[(j % 2, "hqT")]] if (j % 2, "hqT") in prev else st["scrb2"]))
            else:
                t_hqT = S.op("v", lambda e, h=h, ptx=ptx: e.tensor_copy(
                    scrb2[:, h:h + 512], ptx[:, 0:512]),
                    deps=[trs] + ([prev[(j % 2, "hqT")]] if (j % 2, "hqT") in prev else st["scrb2"]))
            if j % 2 == 0:
                st["pt0"] = [t_hqT]
            else:
                st["pt1"] = [t_hqT]
            def moi(e, j=j, h=h, bo=bo):
                return e.matmul(pb[bo][:, :], w1fw[:, j * 128:(j + 1) * 128], scrb2[:, h:h + 512],
                                start=True, stop=True)
            mmoi = S.op("t", moi, deps=[t_hqT, bank_free[bo], st["w_upd"][1]])
            t_ev = S.op("v", lambda e, j=j, cm1=cm1, bo=bo: e.tensor_copy(
                oiN[:, j * 2048 + cm1 * 512: j * 2048 + (cm1 + 1) * 512], pb[bo][:, :]),
                deps=[mmoi, st.get("rms_war", wn_done)])
            bank_free[bo] = t_ev
            prev[(j % 2, "A")] = t_hq
            prev[(j % 2, "B")] = t_hq
            prev[(j % 2, "hq")] = trs
            prev[(j % 2, "hqT")] = mmoi
            toks.append(t_ev)
        oiN_unscaled[cm1] = toks
        st["scrA"] = [prev[(0, "A")], prev[(1, "A")]]
        st["scrB"] = [prev[(0, "B")], prev[(1, "B")]]
        st["scrb"] = [prev[(0, "hq")], prev[(1, "hq")]]
        st["scrb2"] = [prev[(0, "hqT")], prev[(1, "hqT")]]

    def emit_rms(cm1):
        toks = []
        prev = {}
        for j in range(8):
            h = (j % 2) * 512
            bo = 3 + (j % 2)
            sl = oiN[:, j * 2048 + cm1 * 512: j * 2048 + (cm1 + 1) * 512]
            t_sq = S.op("s", lambda e, sl=sl, h=h: e.activation(scrb[:, h:h + 512], sl, AF.Square),
                        deps=[oiN_unscaled[cm1][j]] + ([prev.get((j % 2, "sq"))] if (j % 2, "sq") in prev
                                                       else st["scrb"]))
            def msq(e, h=h, bo=bo):
                return e.matmul(pb[bo][:, :], ones_b[:, :], scrb[:, h:h + 512], start=True, stop=True)
            mm = S.op("t", msq, deps=[t_sq, bank_free[bo]])
            t_u = S.op("s", lambda e, h=h, bo=bo: e.activation(C_[:, h:h + 512], pb[bo][:, :],
                                                               AF.Copy, scale=1.0 / 128, bias=1e-5),
                       deps=[mm] + ([prev.get((j % 2, "C"))] if (j % 2, "C") in prev else st["scrC"]))
            bank_free[bo] = t_u
            t_l = S.op("s", lambda e, h=h: e.activation(C_[:, h:h + 512], C_[:, h:h + 512], AF.Ln),
                       deps=[t_u])
            t_e = S.op("s", lambda e, h=h: e.activation(C_[:, h:h + 512], C_[:, h:h + 512], AF.Exp,
                                                        scale=-0.5), deps=[t_l])
            t_m = S.op("g", lambda e, sl=sl, h=h: e.tensor_mul(sl, sl, C_[:, h:h + 512]),
                       deps=[t_e])
            t_g = S.op("v", lambda e, sl=sl: e.tensor_scalar_mul(sl, sl, onorm_s[:, 0:1]),
                       deps=[t_m])
            prev[(j % 2, "sq")] = mm
            prev[(j % 2, "C")] = t_g
            toks.append(t_g)
        st["scrb"] = [prev[(0, "sq")], prev[(1, "sq")]]
        st["scrC"] = [prev[(0, "C")], prev[(1, "C")]]
        st["rms_war"] = toks[-1]
        return toks

    cc_in, cc_out = ctx["cc_in"], ctx["cc_out"]
    A2A = [[0, 1], [2, 3], [4, 5], [6, 7]]
    ctx["cc_toks"] = []

    def emit_exchange(cch):
        d_c = S.dma("y", lambda e, cch=cch: e.dma_start(
            out=cc_in[cch].rearrange("(j p) t -> p j t", p=128),
            in_=bass.AP(oiN, cch * 512, [[16384, 128], [2048, 8], [1, 512]])),
            deps=rms_done[cch])
        ctx["cc_toks"].append(S.coll(lambda e, cch=cch: e.collective_compute(
            "AllGather", ALU.bypass, replica_groups=A2A, ins=[cc_in[cch]], outs=[cc_out[cch]]),
            deps=[d_c]))

    for c in range(NCH):
        if c > 0:
            emit_qpath(c - 1)
        lr0_ap = bass.AP(lr_tok, (c * 4) * 24 + 0, [[LRS, 128], [1, 8], [24, 4]])
        lr2_ap = bass.AP(lr_tok, (c * 4) * 24 + 16, [[LRS, 128], [1, 8], [24, 4]])
        ki_ap = bass.AP(pk, KIN + c * 32, [[PKS, 128], [4, 8], [1, 4]])
        b5 = S.op("v", lambda e, lr0_ap=lr0_ap, ki_ap=ki_ap: e.tensor_tensor(
            pk[:, A0C:A0C + 32].rearrange("p (j u) -> p j u", j=8), lr0_ap, ki_ap, ALU.mult),
            deps=[hq_done, p1["lr"]] + st["alpha_free"])
        b6 = S.op("v", lambda e, lr2_ap=lr2_ap, ki_ap=ki_ap: e.tensor_tensor(
            pk[:, A2C:A2C + 32].rearrange("p (j u) -> p j u", j=8), lr2_ap, ki_ap, ALU.mult),
            deps=[hq_done])
        gev = []
        pm = {}
        for p in range(4):
            ja, jb = 2 * p, 2 * p + 1
            war = st["rfree"] if (c > 0 and p == 0) else []
            trk = trv = None
            for j2, j in enumerate((ja, jb)):
                ksl = qkv[:, KB + j * 2048 + c * 512: KB + j * 2048 + (c + 1) * 512]
                vsl = qkv[:, VB + j * 2048 + c * 512: VB + j * 2048 + (c + 1) * 512]
                for u in range(4):
                    def mk(e, u=u, ksl=ksl, j2=j2):
                        return e.transpose(pt0[:, j2 * 512 + u * 128: j2 * 512 + (u + 1) * 128],
                                           ksl[:, u * 128:(u + 1) * 128], ident_b[:, :])
                    trk = S.op("t", mk, deps=(st["pt0"] + [p1["qkv"]]) if (j2 == 0 and u == 0) else ())
            for j2, j in enumerate((ja, jb)):
                vsl = qkv[:, VB + j * 2048 + c * 512: VB + j * 2048 + (c + 1) * 512]
                for u in range(4):
                    def mv(e, u=u, vsl=vsl, j2=j2):
                        return e.transpose(pt1[:, j2 * 512 + u * 128: j2 * 512 + (u + 1) * 128],
                                           vsl[:, u * 128:(u + 1) * 128], ident_b[:, :])
                    trv = S.op("t", mv, deps=st["pt1"] if (j2 == 0 and u == 0) else ())
            t_kt = S.op("v", lambda e: e.tensor_copy(rB[:, 0:1024], pt0[:, :]),
                        deps=[trk] + war + ([pm.get("ktfree")] if "ktfree" in pm else []))
            t_vt = S.op("v", lambda e: e.tensor_copy(rB[:, 1024:2048], pt1[:, :]),
                        deps=[trv] + war + ([pm.get("vtfree")] if "vtfree" in pm else []))
            st["pt0"] = [t_kt]
            st["pt1"] = [t_vt]
            mm_g = {}
            for j2, j in enumerate((ja, jb)):
                ksl = qkv[:, KB + j * 2048 + c * 512: KB + j * 2048 + (c + 1) * 512]
                vsl = qkv[:, VB + j * 2048 + c * 512: VB + j * 2048 + (c + 1) * 512]
                bg, bh, bd = (3, 4, 5) if j2 == 0 else (0, 1, 2)
                for u in range(4):
                    def m1(e, u=u, ksl=ksl, j=j, bg=bg):
                        return e.matmul(pb[bg][:, u * 128:(u + 1) * 128], ksl[:, u * 128:(u + 1) * 128],
                                        w0fw[:, j * 128:(j + 1) * 128], start=True, stop=True)
                    mm_g[(j2, "g")] = S.op("t", m1, deps=[bank_free[bg], st["w_upd"][0]] if u == 0 else ())
                for u in range(4):
                    def m2(e, u=u, ksl=ksl, j=j, bh=bh):
                        return e.matmul(pb[bh][:, u * 128:(u + 1) * 128], ksl[:, u * 128:(u + 1) * 128],
                                        w2fw[:, j * 128:(j + 1) * 128], start=True, stop=True)
                    mm_g[(j2, "h")] = S.op("t", m2, deps=[bank_free[bh], st["w_upd"][2]] if u == 0 else ())
                for u in range(4):
                    def m3(e, u=u, vsl=vsl, j=j, bd=bd):
                        return e.matmul(pb[bd][:, u * 128:(u + 1) * 128], vsl[:, u * 128:(u + 1) * 128],
                                        w1Tfw[:, j * 128:(j + 1) * 128], start=True, stop=True)
                    mm_g[(j2, "d")] = S.op("t", m3, deps=[bank_free[bd], st.get("w1t_cast", st["w_upd"][1])] if u == 0 else ())
            kin_a = bass.AP(pk, KIN + c * 32 + ja * 4, [[PKS, 128], [1, 4], [0, 128]])
            kin_b = bass.AP(pk, KIN + c * 32 + jb * 4, [[PKS, 128], [1, 4], [0, 128]])
            n1a = S.op("v", lambda e, kin_a=kin_a: e.tensor_tensor(
                A_[:, 0:512].rearrange("p (u d) -> p u d", u=4),
                pb[3][:, :].rearrange("p (u d) -> p u d", u=4), kin_a, ALU.mult),
                deps=[mm_g[(0, "g")], hq_done] + st["scrA"])
            n1b = S.op("v", lambda e, kin_b=kin_b: e.tensor_tensor(
                A_[:, 512:1024].rearrange("p (u d) -> p u d", u=4),
                pb[0][:, :].rearrange("p (u d) -> p u d", u=4), kin_b, ALU.mult),
                deps=[mm_g[(1, "g")]] + st["scrA"])
            n2 = S.op("s", lambda e: e.activation(B_[:, :], A_[:, :], AF.Sigmoid),
                      deps=[n1a, n1b] + st["scrB"])
            n3a = S.op("v", lambda e, kin_a=kin_a: e.tensor_tensor(
                C_[:, 0:512].rearrange("p (u d) -> p u d", u=4),
                pb[4][:, :].rearrange("p (u d) -> p u d", u=4), kin_a, ALU.mult),
                deps=[mm_g[(0, "h")]] + st["scrC"])
            n3b = S.op("v", lambda e, kin_b=kin_b: e.tensor_tensor(
                C_[:, 512:1024].rearrange("p (u d) -> p u d", u=4),
                pb[1][:, :].rearrange("p (u d) -> p u d", u=4), kin_b, ALU.mult),
                deps=[mm_g[(1, "h")]] + st["scrC"])
            n4 = S.op("v", lambda e: e.tensor_mul(D_[:, :], A_[:, :], B_[:, :]),
                      deps=[n2] + st["scrD"])
            n5 = S.op("v", lambda e: e.scalar_tensor_tensor(A_[:, :], A_[:, :], 1.0, D_[:, :],
                                                            ALU.add, ALU.subtract), deps=[n4])
            n6 = S.op("v", lambda e: e.tensor_mul(B_[:, :], B_[:, :], A_[:, :]), deps=[n5])
            n7a = S.op("v", lambda e: e.tensor_tensor(A_[:, 0:512], pb[5][:, :], C_[:, 0:512],
                                                      ALU.mult), deps=[mm_g[(0, "d")], n6, n3a, n5])
            n7b = S.op("v", lambda e: e.tensor_tensor(A_[:, 512:1024], pb[2][:, :], C_[:, 512:1024],
                                                      ALU.mult), deps=[mm_g[(1, "d")], n3b, n6])
            n8 = S.op("v", lambda e: e.tensor_mul(rB[:, 2048:3072], A_[:, :], B_[:, :]),
                      deps=[n7a, n7b] + ([pm.get("dgba")] if "dgba" in pm else []) + war)
            n9a = S.op("v", lambda e: e.tensor_tensor(rB[:, 3072:3584], pb[5][:, :], D_[:, 0:512],
                                                      ALU.mult),
                       deps=[n7a] + ([pm.get("dhbm")] if "dhbm" in pm else []) + war)
            n9b = S.op("v", lambda e: e.tensor_tensor(rB[:, 3584:4096], pb[2][:, :], D_[:, 512:1024],
                                                      ALU.mult), deps=[n7b])
            n10 = S.op("v", lambda e: e.tensor_mul(A_[:, :], D_[:, :], C_[:, :]), deps=[n9a, n9b, n8])
            lr1_ap = bass.AP(lr_tok, (c * 4) * 24 + 8 + ja, [[LRS, 128], [1, 2], [24, 4], [0, 128]])
            n11 = S.op("v", lambda e, lr1_ap=lr1_ap: e.tensor_tensor(
                rD[:, 0:1024].rearrange("p (j u d) -> p j u d", j=2, u=4),
                A_[:, :].rearrange("p (j u d) -> p j u d", j=2, u=4), lr1_ap, ALU.mult),
                deps=[n10, p1["lr"]] + ([pm.get("hlr1")] if "hlr1" in pm else []))
            a0_ap = bass.AP(pk, A0C + ja * 4, [[PKS, 128], [1, 2], [4, 4] if False else [1, 4], [0, 128]])
            a0_ap = bass.AP(pk, A0C + ja * 4, [[PKS, 128], [4, 2], [1, 4], [0, 128]])
            a2_ap = bass.AP(pk, A2C + ja * 4, [[PKS, 128], [4, 2], [1, 4], [0, 128]])
            f1 = S.op("v", lambda e, a0_ap=a0_ap: e.tensor_tensor(
                rB[:, 2048:3072].rearrange("p (j u d) -> p j u d", j=2, u=4),
                rB[:, 2048:3072].rearrange("p (j u d) -> p j u d", j=2, u=4), a0_ap, ALU.mult),
                deps=[b5, n8])
            f2 = S.op("v", lambda e, a2_ap=a2_ap: e.tensor_tensor(
                rB[:, 3072:4096].rearrange("p (j u d) -> p j u d", j=2, u=4),
                rB[:, 3072:4096].rearrange("p (j u d) -> p j u d", j=2, u=4), a2_ap, ALU.mult),
                deps=[b6, n9a, n9b])
            for j2, j in enumerate((ja, jb)):
                bk = pb[3] if j2 == 0 else pb[0]
                ko, vo, do_, ho = j2 * 512, 1024 + j2 * 512, 2048 + j2 * 512, j2 * 512
                last_mm = None
                dep0 = [f1, f2, n11, n1a if j2 == 0 else n1b, t_kt, t_vt]
                for u in range(4):
                    def g0(e, bk=bk, do_=do_, ko=ko, u=u):
                        return e.matmul(bk[:, 0:128], rB[:, do_ + u * 128: do_ + (u + 1) * 128],
                                        rB[:, ko + u * 128: ko + (u + 1) * 128],
                                        start=(u == 0), stop=(u == 3))
                    last_mm = S.op("t", g0, deps=dep0 if u == 0 else ())
                for u in range(4):
                    def g1(e, bk=bk, vo=vo, ho=ho, u=u):
                        return e.matmul(bk[:, 128:256], rB[:, vo + u * 128: vo + (u + 1) * 128],
                                        rD[:, ho + u * 128: ho + (u + 1) * 128],
                                        start=(u == 0), stop=(u == 3))
                    last_mm = S.op("t", g1, deps=())
                for u in range(4):
                    def g2(e, bk=bk, ko=ko, j2=j2, u=u):
                        return e.matmul(bk[:, 256:384], rB[:, 3072 + j2 * 512 + u * 128: 3072 + j2 * 512 + (u + 1) * 128],
                                        rB[:, ko + u * 128: ko + (u + 1) * 128],
                                        start=(u == 0), stop=(u == 3))
                    last_mm = S.op("t", g2, deps=())
                ev = S.op("v", lambda e, j=j, bk=bk: e.tensor_copy(rA[:, j * 384: j * 384 + 384],
                                                                   bk[:, 0:384]),
                          deps=[last_mm] + (st["rfree"] if (c > 0 and p == 0 and j2 == 0) else []))
                bank_free[3 if j2 == 0 else 0] = ev
                gev.append(ev)
            bank_free[4] = n3a
            bank_free[1] = n3b
            bank_free[5] = n9a
            bank_free[2] = n9b
            pm = dict(ktfree=last_mm, vtfree=last_mm, dgba=f1, dhbm=f2, hlr1=last_mm)
            st["scrA"] = [n11]
            st["scrB"] = [n8]
            st["scrC"] = [n10]
            st["scrD"] = [n10]
        st["alpha_free"] = [f1, f2]

        if c > 0:
            rms_done[c - 1] = emit_rms(c - 1)
            emit_exchange(c - 1)
        if c == 3:
            pid = nc.partition_id()
            a_src = bass.AP(cc_out.tensor, (pid % 2) * 2097152, [[512, 128], [128 * 512, 16], [1, 512]])
            a_dst = bass.AP(qkv, 32768, [[49152, 128], [1024, 16], [1, 512]])
            ctx["d_attn0"] = S.dma("s", lambda e, a_src=a_src, a_dst=a_dst: e.dma_start(
                out=a_dst, in_=a_src), deps=[ctx["cc_toks"][0], ctx["cc_toks"][2]])
            ctx["wpre_toks"] = []
            for i in range(2):
                ctx["wpre_toks"].append(S.dma("g", lambda e, i=i: e.dma_start(
                    out=bass.AP(qkv, 16384 + (4 * i) * 2048 + 1024,
                                [[49152, 128], [2048, 4], [128, 4], [1, 128]]),
                    in_=ctx["dr"]["wproj"][i].rearrange("p k c -> p (k c)")),
                    deps=rms_done[2]))
        x2_tok, xread = ns_block(S, pb, pt0, pt1, ident_b, ones_f, rA, rB, rC, rD,
                                 scrf, scrb, scrb2, pk, FRB, PKS, gev, bank_free, st)
        w_update(S, pb, wT, wn_s, pk, RNM, PKS, scrf, rA, rB, rC, rD, x2_tok, xread,
                 bank_free, w0fw, w1fw, w1Tfw, w2fw, ident_f, st)

    emit_qpath(3)
    rms_done[3] = emit_rms(3)
    emit_exchange(3)
    ctx["scan_done"] = [t for ts in rms_done.values() for t in ts]
    ctx["rms_toks"] = rms_done
    ctx["r_free"] = st["rfree"] + st["w_upd"]
    dbg("oiN", oiN[:, :], ctx["scan_done"][-1:])


def ns_block(S, pb, pt0, pt1, ident_b, ones_f, rA, rB, rC, rD, scrf, scrb, scrb2,
             pk, FRB, PKS, gev, bank_free, st):
    A_, B_, C_, D_ = scrf

    def xap(r, w, nj=8, j0=0):
        return bass.AP(r, j0 * 384 + w * 128, [[4096, 128], [384, nj], [1, 128]])

    fro = []
    t2 = None
    for w in range(3):
        for half in range(2):
            sqv = A_[:, half * 512:(half + 1) * 512]
            t1 = S.op("s", lambda e, w=w, half=half, sqv=sqv: e.activation(
                sqv.rearrange("p (j d) -> p j d", j=4), xap(rA, w, 4, half * 4), AF.Square),
                deps=(gev + st["scrA"] + st.get("dbg_war", [])) if (w == 0 and half == 0)
                     else ([t2] if t2 else []))
            t2 = S.op("v", lambda e, w=w, half=half, sqv=sqv: e.reduce_sum(
                pk[:, FRB + w * 8 + half * 4: FRB + w * 8 + half * 4 + 4],
                sqv.rearrange("p (j d) -> p j d", j=4), axis=AX.X), deps=[t1])
            fro.append(t2)
    st["scrA"] = [t2]
    def mfr(e):
        return e.matmul(pb[2][:, 0:24], ones_f[:, 0:128], pk[:, FRB:FRB + 24],
                        start=True, stop=True)
    mm = S.op("t", mfr, deps=fro + [bank_free[2]])
    c1 = S.op("v", lambda e: e.tensor_copy(pk[:, FRB:FRB + 24], pb[2][:, 0:24]), deps=[mm])
    bank_free[2] = c1
    c2 = S.op("s", lambda e: e.activation(pk[:, FRB:FRB + 24], pk[:, FRB:FRB + 24], AF.Ln), deps=[c1])
    c3 = S.op("s", lambda e: e.activation(pk[:, FRB:FRB + 24], pk[:, FRB:FRB + 24], AF.Exp, scale=0.5),
              deps=[c2])
    c4 = S.op("v", lambda e: e.tensor_scalar_add(pk[:, FRB:FRB + 24], pk[:, FRB:FRB + 24], 1e-7),
              deps=[c3])
    c5 = S.op("v", lambda e: e.reciprocal(pk[:, FRB:FRB + 24], pk[:, FRB:FRB + 24]), deps=[c4])

    xt_ev = []
    for w in range(3):
        ptx = (pt0, pt1, pt0)[w]
        key = "pt0" if ptx is pt0 else "pt1"
        trs = None
        for j in range(8):
            def mt(e, w=w, j=j, ptx=ptx):
                return e.transpose(ptx[:, j * 128:(j + 1) * 128],
                                   rA[:, j * 384 + w * 128: j * 384 + (w + 1) * 128], ident_b[:, :])
            trs = S.op("t", mt, deps=(st[key] + [gev[-1]]) if j == 0 else ())
        ev = S.op("v", lambda e, w=w, ptx=ptx: e.tensor_copy(
            xap(rC, w), ptx[:, :].rearrange("p (j d) -> p j d", j=8)), deps=[trs])
        st[key] = [ev]
        xt_ev.append(ev)

    x0t, xt0t = [], []
    for w in range(3):
        iv_ap = bass.AP(pk, FRB + w * 8, [[PKS, 128], [1, 8], [0, 128]])
        x0t.append(S.op("v", lambda e, w=w, iv_ap=iv_ap: e.tensor_tensor(
            xap(rB, w), xap(rA, w), iv_ap, ALU.mult), deps=[c5, gev[-1]] + (st["scrB"] if w == 0 else [])))
        xt0t.append(S.op("g", lambda e, w=w, iv_ap=iv_ap: e.tensor_tensor(
            xap(rD, w), xap(rC, w), iv_ap, ALU.mult), deps=[c5, xt_ev[w]] + (st["scrD"] if w == 0 else [])))

    x_src, xT_src = rB, rD
    x_dst = rA
    dep_x, dep_xT = x0t, xt0t
    x2_tok = []
    prevh = {}
    for step in range(2):
        new_dep_x, new_dep_xT = [], []
        for w in range(3):
            for grp in range(2):
                j0 = grp * 4
                hh = ((w * 2 + grp) % 2) * 512
                lastA = None
                for j in range(j0, j0 + 4):
                    def mA(e, w=w, j=j, j0=j0, xT_src=xT_src):
                        return e.matmul(pb[2][:, (j - j0) * 128:(j - j0 + 1) * 128],
                                        xT_src[:, j * 384 + w * 128: j * 384 + (w + 1) * 128],
                                        xT_src[:, j * 384 + w * 128: j * 384 + (w + 1) * 128],
                                        start=True, stop=True)
                    lastA = S.op("t", mA, deps=([bank_free[2]] + dep_xT) if j == j0 else ())
                evA = S.op("s", lambda e, hh=hh: e.activation(scrb[:, hh:hh + 512], pb[2][:, :],
                                                              AF.Copy),
                           deps=[lastA] + ([prevh[(hh, "As")]] if (hh, "As") in prevh else st["scrb"]))
                bank_free[2] = evA
                lastA2 = None
                for j4 in range(4):
                    def mA2(e, j4=j4, hh=hh):
                        return e.matmul(pb[3][:, j4 * 128:(j4 + 1) * 128],
                                        scrb[:, hh + j4 * 128: hh + (j4 + 1) * 128],
                                        scrb[:, hh + j4 * 128: hh + (j4 + 1) * 128],
                                        start=True, stop=True)
                    lastA2 = S.op("t", mA2, deps=[evA, bank_free[3]] if j4 == 0 else ())
                t1 = S.op("s", lambda e, hh=hh: e.activation(C_[:, hh:hh + 512], pb[3][:, :],
                                                             AF.Copy, scale=NS_C),
                          deps=[lastA2] + ([prevh[(hh, "t1")]] if (hh, "t1") in prevh else st["scrC"]))
                bank_free[3] = t1
                tBm = S.op("v", lambda e, hh=hh: e.scalar_tensor_tensor(
                    scrb2[:, hh:hh + 512], scrb[:, hh:hh + 512], NS_B, C_[:, hh:hh + 512],
                    ALU.mult, ALU.add),
                    deps=[t1] + ([prevh[(hh, "Bm")]] if (hh, "Bm") in prevh else st["scrb2"]))
                lastB = lastXT = None
                for j in range(j0, j0 + 4):
                    def mBX(e, w=w, j=j, j0=j0, x_src=x_src, hh=hh):
                        return e.matmul(pb[4][:, (j - j0) * 128:(j - j0 + 1) * 128],
                                        scrb2[:, hh + (j - j0) * 128: hh + (j - j0 + 1) * 128],
                                        x_src[:, j * 384 + w * 128: j * 384 + (w + 1) * 128],
                                        start=True, stop=True)
                    lastB = S.op("t", mBX, deps=([tBm, bank_free[4]] + dep_x) if j == j0 else ())
                if step == 0:
                    for j in range(j0, j0 + 4):
                        def mXTB(e, w=w, j=j, j0=j0, x_src=x_src, hh=hh):
                            return e.matmul(pb[5][:, (j - j0) * 128:(j - j0 + 1) * 128],
                                            x_src[:, j * 384 + w * 128: j * 384 + (w + 1) * 128],
                                            scrb2[:, hh + (j - j0) * 128: hh + (j - j0 + 1) * 128],
                                            start=True, stop=True)
                        lastXT = S.op("t", mXTB, deps=[bank_free[5]] if j == j0 else ())
                def upd(e, w=w, j0=j0, x_src=x_src, x_dst=x_dst):
                    return e.scalar_tensor_tensor(xap(x_dst, w, 4, j0), xap(x_src, w, 4, j0),
                                                  NS_A, pb[4][:, :].rearrange("p (j d) -> p j d", j=4),
                                                  ALU.mult, ALU.add)
                tX = S.op("v", upd, deps=[lastB] + (x2_tok if step == 1 and w == 0 and grp == 0 else []))
                bank_free[4] = tX
                new_dep_x.append(tX)
                if step == 0:
                    def updT(e, w=w, j0=j0, xT_src=xT_src):
                        return e.scalar_tensor_tensor(xap(rC, w, 4, j0), xap(xT_src, w, 4, j0),
                                                      NS_A, pb[5][:, :].rearrange("p (j d) -> p j d", j=4),
                                                      ALU.mult, ALU.add)
                    tXT = S.op("v", updT, deps=[lastXT])
                    bank_free[5] = tXT
                    new_dep_xT.append(tXT)
                prevh[(hh, "As")] = lastA2
                prevh[(hh, "t1")] = tBm
                prevh[(hh, "Bm")] = lastB if step == 1 else (lastXT or lastB)
        if step == 0:
            x_src, xT_src = rA, rC
            x_dst = rB
            dep_x, dep_xT = new_dep_x, new_dep_xT
        else:
            x2_tok = new_dep_x
    st["scrb"] = [prevh[(0, "As")], prevh[(512, "As")]]
    st["scrb2"] = [prevh[(0, "Bm")], prevh[(512, "Bm")]]
    st["scrC"] = [prevh[(0, "t1")], prevh[(512, "t1")]]
    xread = dep_x + dep_xT
    return x2_tok, xread


def w_update(S, pb, wT, wn_s, pk, RNM, PKS, scrf, rA, rB, rC, rD, x2_tok, xread,
             bank_free, w0fw, w1fw, w1Tfw, w2fw, ident_f, st):
    A_, B_, C_, D_ = scrf
    pre = {}
    red = []
    sq_war = st["scrB"]
    for w in range(3):
        for half in range(2):
            bk = pb[w * 2 + half]
            bi = w * 2 + half
            def mpre(e, w=w, half=half, bk=bk):
                return e.tensor_tensor(bk[:, :], wT[w][:, half * 512:(half + 1) * 512],
                                       bass.AP(rB, (half * 4) * 384 + w * 128,
                                               [[4096, 128], [384, 4], [1, 128]]), ALU.add)
            tp = S.op("v", mpre, deps=[bank_free[bi]] + x2_tok)
            pre[(w, half)] = (bk, tp)
            hh = ((w * 2 + half) % 2) * 512
            t1 = S.op("s", lambda e, bk=bk, hh=hh: e.activation(B_[:, hh:hh + 512], bk[:, :], AF.Square),
                      deps=[tp] + (sq_war if w == 0 and half == 0 else []))
            t2 = S.op("v", lambda e, w=w, half=half, hh=hh: e.reduce_sum(
                pk[:, RNM + w * 8 + half * 4: RNM + w * 8 + half * 4 + 4],
                B_[:, hh:hh + 512].rearrange("p (j d) -> p j d", j=4), axis=AX.X), deps=[t1])
            red.append(t2)
    st["scrB"] = red[-2:]
    c2 = S.op("s", lambda e: e.activation(pk[:, RNM:RNM + 24], pk[:, RNM:RNM + 24], AF.Ln), deps=red)
    c3 = S.op("s", lambda e: e.activation(pk[:, RNM:RNM + 24], pk[:, RNM:RNM + 24], AF.Exp, scale=0.5),
              deps=[c2])
    c4 = S.op("v", lambda e: e.tensor_scalar_add(pk[:, RNM:RNM + 24], pk[:, RNM:RNM + 24], 1e-5),
              deps=[c3])
    c5 = S.op("v", lambda e: e.reciprocal(pk[:, RNM:RNM + 24], pk[:, RNM:RNM + 24]), deps=[c4])
    c6 = S.op("v", lambda e: e.tensor_mul(pk[:, RNM:RNM + 24], pk[:, RNM:RNM + 24], wn_s[:, :]),
              deps=[c5])
    wnew = {}
    for w in range(3):
        for half in range(2):
            bk, tp = pre[(w, half)]
            sc_ap = bass.AP(pk, RNM + w * 8 + half * 4, [[PKS, 128], [1, 4], [0, 128]])
            def mnew(e, w=w, half=half, bk=bk, sc_ap=sc_ap):
                return e.tensor_tensor(wT[w][:, half * 512:(half + 1) * 512].rearrange(
                    "p (j d) -> p j d", j=4), bk[:, :].rearrange("p (j d) -> p j d", j=4),
                    sc_ap, ALU.mult)
            wnew[(w, half)] = S.op("v", mnew, deps=[c6, tp])
    for w in range(3):
        for half in range(2):
            bank_free[w * 2 + half] = wnew[(w, half)]
    fwmap = {0: w0fw, 1: w1fw, 2: w2fw}
    upd_tok = [None, None, None]
    for w in range(3):
        evs = []
        for half in range(2):
            bk = pb[half]
            trs = None
            for j4 in range(4):
                j = half * 4 + j4
                def mt(e, w=w, j=j, j4=j4, bk=bk):
                    return e.transpose(bass.AP(bk, j4 * 128, [[512, 128], [1, 128]]),
                                       wT[w][:, j * 128:(j + 1) * 128], ident_f[:, :])
                trs = S.op("t", mt, deps=[wnew[(w, 0)], wnew[(w, 1)], bank_free[half]] if j4 == 0 else ())
            ev = S.op("s", lambda e, w=w, half=half, bk=bk: e.activation(
                fwmap[w][:, half * 512:(half + 1) * 512], bk[:, :], AF.Copy), deps=[trs])
            bank_free[half] = ev
            evs.append(ev)
        upd_tok[w] = evs[-1]
    tcp = S.op("v", lambda e: e.tensor_copy(w1Tfw[:, :], wT[1][:, :]), deps=[wnew[(1, 0)], wnew[(1, 1)]])
    st["w_upd"] = [upd_tok[0], upd_tok[1], upd_tok[2]]
    st["w1t_cast"] = tcp
    st["rfree"] = x2_tok + xread + [wnew[(w, h)] for w in range(3) for h in range(2)] + [tcp]


def build_tail(ctx):
    nc, S, sb, pb, dr = ctx["nc"], ctx["S"], ctx["sb"], ctx["pb"], ctx["dr"]
    pt0, pt1, dbg, load = ctx["pt0"], ctx["pt1"], ctx["dbg"], ctx["load"]
    sba, OFF_S = ctx["sba"], ctx["OFF_S"]
    out, cc_in, cc_out = ctx["out"], ctx["cc_in"], ctx["cc_out"]
    oiN, attn, x1, x1nb, x1n, x2p = (sb["oiN"], sb["attn"], sb["x1"], sb["x1nb"],
                                     sb["x1n"], sb["x2p"])
    ones_b, ident_f, e64_s = sb["ones_b_s"], sb["ident_f_s"], sb["e64_s"]
    scan_done = ctx["scan_done"]


    wstr2_b = [sba("wstr2A", [128, 512], BF16, OFF_S + 4096),
               sba("wstr2B", [128, 512], BF16, OFF_S)]
    wg_b = sba("wg_b", [128, 128], BF16, OFF_S + 5120)
    xres = [sba(f"xres{i}", [128, 1024], F32, OFF_S + 8192 + i * 4096) for i in range(2)]
    hm_sb = sba("hm_sb", [128, 4096], BF16, OFF_S + 8192)
    p5k = sba("p5k", [128, 256], F32, OFF_S + 16384)
    mT_b = sba("mT_b", [128, 1024], BF16, OFF_S + 17408)
    mvb = sba("mvb", [128, 512], BF16, OFF_S + 19456)
    sqb = sba("sqb", [128, 1024], BF16, OFF_S + 20480)

    bank = {i: scan_done[-1] for i in range(6)}
    A2A = [[0, 1], [2, 3], [4, 5], [6, 7]]

    rms_toks, r_free = ctx["rms_toks"], ctx["r_free"]
    cc_toks = ctx["cc_toks"]
    pid = nc.partition_id()
    t0 = (pid % 2) * 1024
    qkv = sb["qkv"]
    at1 = []
    for kh in range(2):
        src = bass.AP(cc_out.tensor, (pid % 2) * 2097152 + 1048576 + kh * 524288,
                      [[512, 128], [128 * 512, 8], [1, 512]])
        dst = bass.AP(qkv, 32768 + 512 + kh * 8192, [[49152, 128], [1024, 8], [1, 512]])
        at1.append(S.dma("y" if kh == 0 else "g",
                         lambda e, src=src, dst=dst: e.dma_start(out=dst, in_=src),
                         deps=[cc_toks[1], cc_toks[3]]))
    d_attn_t = [ctx["d_attn0"], at1]
    dbg("attn", qkv[:, 32768:49152], [ctx["d_attn0"]] + at1)

    msb = sba("msb", [128, 1024], F32, OFF_S + 9728)
    rsb = sba("rsb", [128, 1024], F32, OFF_S + 9728 + 4096)
    last_wp = None
    last_x = None
    ev_toks = []
    st_mm = None
    sq_mm = None
    def wgrp_base(wi):
        return 16384 + (4 * (wi % 2)) * 2048 + 1024 + (wi // 2) * 512
    wp_mm = [None] * 4
    for oc in range(16):
        wi = oc % 4
        if oc < 2:
            d_wp = ctx["wpre_toks"][oc]
        else:
            d_wp = S.dma("g", lambda e, oc=oc, wi=wi: e.dma_start(
                out=bass.AP(qkv, wgrp_base(wi),
                            [[49152, 128], [2048, 4], [128, 4], [1, 128]]),
                in_=dr["wproj"][oc].rearrange("p k c -> p (k c)")),
                deps=rms_toks[3] if oc < 4 else [wp_mm[wi]])
        d_xr = S.dma("y", lambda e, oc=oc, t0=t0: e.dma_start(
            out=xres[oc % 2][:, :],
            in_=bass.AP(dr["xT"].tensor, oc * 128 * 2048 + t0, [[2048, 128], [1, 1024]])),
            deps=[ev_toks[-2]] if oc >= 2 else [])
        mms = {}
        for tt in range(2):
            mm = None
            for k in range(16):
                def mp(e, tt=tt, k=k, wi=wi):
                    wc = wgrp_base(wi) + (k // 4) * 2048 + (k % 4) * 128
                    ac = 32768 + k * 1024 + tt * 512
                    return e.matmul(pb[tt][:, :], qkv[:, wc:wc + 128],
                                    qkv[:, ac:ac + 512],
                                    start=(k == 0), stop=(k == 15))
                ad = d_attn_t[tt] if tt == 0 else d_attn_t[1]
                mm = S.op("t", mp, deps=([d_wp, bank[tt]] + (ad if isinstance(ad, list) else [ad]))
                          if k == 0 else ())
            mms[tt] = mm
        wp_mm[wi] = mms[1]
        evs = []
        for tt in range(2):
            def ev(e, oc=oc, tt=tt):
                return e.tensor_tensor(x1[:, oc * 1024 + tt * 512: oc * 1024 + (tt + 1) * 512],
                                       pb[tt][:, :], xres[oc % 2][:, tt * 512:(tt + 1) * 512],
                                       ALU.add)
            t_ = S.op("v", ev, deps=[mms[tt], d_xr])
            bank[tt] = t_
            evs.append(t_)
        t_b = S.op("s", lambda e, oc=oc: e.activation(
            mT_b[:, :], x1[:, oc * 1024:(oc + 1) * 1024], AF.Copy), deps=evs + ([st_mm] if st_mm else []))
        t_sq = S.op("s", lambda e: e.activation(sqb[:, :], mT_b[:, :], AF.Square),
                    deps=[t_b] + ([sq_mm] if sq_mm else []))
        for tt in range(2):
            def ms(e, tt=tt, oc=oc):
                return e.matmul(pb[2 + tt][:, :], ones_b[:, :], mT_b[:, tt * 512:(tt + 1) * 512],
                                start=(oc == 0), stop=(oc == 15))
            st_mm = S.op("t", ms, deps=[t_b] if tt == 0 else ())
            def mq(e, tt=tt, oc=oc):
                return e.matmul(pb[4 + tt][:, :], ones_b[:, :], sqb[:, tt * 512:(tt + 1) * 512],
                                start=(oc == 0), stop=(oc == 15))
            sq_mm = S.op("t", mq, deps=[t_sq] if tt == 0 else ())
        ev_toks.append(evs[-1])
    l1 = S.op("v", lambda e: e.tensor_scalar_mul(msb[:, 0:512], pb[2][:, :], 1.0 / 2048),
              deps=[st_mm, scan_done[-1]])
    l2 = S.op("v", lambda e: e.tensor_scalar_mul(msb[:, 512:1024], pb[3][:, :], 1.0 / 2048),
              deps=[st_mm])
    l3 = S.op("v", lambda e: e.tensor_scalar_mul(rsb[:, 0:512], pb[4][:, :], 1.0 / 2048),
              deps=[sq_mm])
    l4 = S.op("v", lambda e: e.tensor_scalar_mul(rsb[:, 512:1024], pb[5][:, :], 1.0 / 2048),
              deps=[sq_mm])
    l5 = S.op("v", lambda e: e.tensor_tensor(pb[2][:, :], msb[:, 0:512], msb[:, 0:512], ALU.mult),
              deps=[l1, l3])
    l6 = S.op("v", lambda e: e.tensor_tensor(pb[3][:, :], msb[:, 512:1024], msb[:, 512:1024], ALU.mult),
              deps=[l2, l4])
    l7 = S.op("v", lambda e: e.tensor_tensor(rsb[:, 0:512], rsb[:, 0:512], pb[2][:, :], ALU.subtract),
              deps=[l5])
    l8 = S.op("v", lambda e: e.tensor_tensor(rsb[:, 512:1024], rsb[:, 512:1024], pb[3][:, :], ALU.subtract),
              deps=[l6])
    l9 = S.op("v", lambda e: e.tensor_scalar_add(rsb[:, :], rsb[:, :], 1e-5), deps=[l7, l8])
    l10 = S.op("s", lambda e: e.activation(rsb[:, :], rsb[:, :], AF.Ln), deps=[l9])
    l11 = S.op("s", lambda e: e.activation(rsb[:, :], rsb[:, :], AF.Exp, scale=-0.5), deps=[l10])
    bank[2], bank[3], bank[4], bank[5] = l7, l8, l3, l4
    ap_toks = []
    for oc in range(16):
        sl = slice(oc * 1024, (oc + 1) * 1024)
        a1 = S.op("v", lambda e, sl=sl: e.tensor_tensor(x1n[:, sl], x1[:, sl], msb[:, :],
                                                        ALU.subtract), deps=[l11, ev_toks[oc]])
        a2 = S.op("g" if oc % 2 else "v", lambda e, sl=sl: e.tensor_tensor(
            x1n[:, sl], x1n[:, sl], rsb[:, :], ALU.mult), deps=[a1])
        a3 = S.op("s", lambda e, sl=sl: e.activation(x1nb[:, sl], x1n[:, sl], AF.Copy), deps=[a2])
        ap_toks.append((a2, a3))
    dbg("x1n", x1n[:, :], [ap_toks[-1][0]])

    d_wg = load("g", wg_b[:, :], dr["wgt"].rearrange("p k e -> p (k e)"))
    lg_mm = {}
    for ti in range(8):
        mm = None
        for k in range(16):
            def ml(e, ti=ti, k=k):
                return e.matmul(pb[0][:, ti * 8:(ti + 1) * 8],
                                x1nb[:, k * 1024 + ti * 128: k * 1024 + (ti + 1) * 128],
                                wg_b[:, k * 8:(k + 1) * 8], start=(k == 0), stop=(k == 15))
            mm = S.op("t", ml, deps=[d_wg, ap_toks[15][1], bank[0]] if (ti == 0 and k == 0) else ())
        lg_mm[ti] = mm
    LG, MX, NEG, ES, SE, SC, MS = 0, 64, 72, 80, 144, 152, 160
    s1 = S.op("v", lambda e: e.tensor_copy(p5k[:, LG:LG + 64], pb[0][:, 0:64]), deps=[lg_mm[7]])
    bank[0] = s1
    s2 = S.op("v", lambda e: e.reduce_max(p5k[:, MX:MX + 8],
                                          p5k[:, LG:LG + 64].rearrange("p (t e) -> p t e", t=8),
                                          axis=AX.X), deps=[s1])
    s3 = S.op("v", lambda e: e.tensor_scalar_mul(p5k[:, NEG:NEG + 8], p5k[:, MX:MX + 8], -1.0),
              deps=[s2])
    es_toks = []
    for ti in range(8):
        def se_(e, ti=ti):
            return e.activation(p5k[:, ES + ti * 8: ES + ti * 8 + 8], p5k[:, LG + ti * 8: LG + ti * 8 + 8],
                                AF.Exp, bias=p5k[:, NEG + ti: NEG + ti + 1])
        es_toks.append(S.op("s", se_, deps=[s3] if ti == 0 else ()))
    s4 = S.op("v", lambda e: e.reduce_sum(p5k[:, SE:SE + 8],
                                          p5k[:, ES:ES + 64].rearrange("p (t e) -> p t e", t=8),
                                          axis=AX.X), deps=es_toks)
    s5 = S.op("v", lambda e: e.reciprocal(p5k[:, SC:SC + 8], p5k[:, SE:SE + 8]), deps=[s4])
    s6 = S.op("v", lambda e: e.tensor_scalar_mul(p5k[:, SC:SC + 8], p5k[:, SC:SC + 8], 4.0), deps=[s5])
    ms_toks = []
    for ti in range(8):
        m1 = S.op("v", lambda e, ti=ti: e.tensor_scalar(
            p5k[:, MS + ti * 8: MS + ti * 8 + 8], p5k[:, LG + ti * 8: LG + ti * 8 + 8],
            p5k[:, MX + ti: MX + ti + 1], None, ALU.is_ge), deps=[s6] if ti == 0 else ())
        m2 = S.op("v", lambda e, ti=ti: e.tensor_scalar(
            p5k[:, MS + ti * 8: MS + ti * 8 + 8], p5k[:, MS + ti * 8: MS + ti * 8 + 8],
            p5k[:, SC + ti: SC + ti + 1], None, ALU.mult), deps=[m1])
        ms_toks.append(m2)
    z1 = S.op("g", lambda e: e.memset(mT_b[:, :], 0.0), deps=ms_toks + [lg_mm[7], st_mm, sq_mm])
    tr_toks = []
    for ti in range(8):
        bk = pb[2 + ti // 4]
        def mt(e, ti=ti, bk=bk):
            return e.transpose(bk[0:8, (ti % 4) * 128: (ti % 4 + 1) * 128],
                               p5k[:, MS + ti * 8: MS + ti * 8 + 8], ident_f[:, :])
        tr_toks.append(S.op("t", mt, deps=[ms_toks[ti]] + ([bank[2 + ti // 4]] if ti % 4 == 0 else [])))
    e1 = S.op("s", lambda e: e.activation(mT_b[0:8, 0:512], pb[2][0:8, :], AF.Copy),
              deps=[tr_toks[3], z1])
    e2 = S.op("s", lambda e: e.activation(mT_b[0:8, 512:1024], pb[3][0:8, :], AF.Copy),
              deps=[tr_toks[7], z1])
    bank[2], bank[3] = e1, e2
    hm_toks = []
    OFF_W = OFF_S + 23040
    wstr_b = [sba("wstrA", [128, 2048], BF16, OFF_W),
              sba("wstrB", [128, 2048], BF16, OFF_W + 4096)]
    wa_mm = [wp_mm[0], wp_mm[1]]
    for mt_ in range(4):
        wstr = wstr_b[mt_ % 2]
        d_wa = S.dma("g", lambda e, mt_=mt_, wstr=wstr: e.dma_start(
            out=wstr[:, :], in_=dr["wa"][mt_].rearrange("p k c -> p (k c)")),
            deps=[wa_mm[mt_ % 2]])
        for tt2 in range(2):
            mm = None
            for k in range(16):
                def mh(e, mt_=mt_, tt2=tt2, k=k, wstr=wstr):
                    return e.matmul(pb[4 + tt2][:, :], wstr[:, k * 128:(k + 1) * 128],
                                    x1nb[:, k * 1024 + tt2 * 512: k * 1024 + (tt2 + 1) * 512],
                                    start=(k == 0), stop=(k == 15))
                mm = S.op("t", mh, deps=[d_wa, bank[4 + tt2], ap_toks[15][1]] if k == 0 else ())
            wa_mm[mt_ % 2] = mm
            def mb(e, mt_=mt_, tt2=tt2):
                return e.matmul(pb[2][:, :], e64_s[:, mt_ * 128:(mt_ + 1) * 128],
                                mT_b[:, tt2 * 512:(tt2 + 1) * 512], start=True, stop=True)
            mmb = S.op("t", mb, deps=[bank[2], e1, e2])
            t_mv = S.op("s", lambda e: e.activation(mvb[:, :], pb[2][:, :], AF.Copy),
                        deps=[mmb] + (hm_toks[-1:] if hm_toks else []))
            bank[2] = t_mv
            def hmop(e, mt_=mt_, tt2=tt2):
                return e.tensor_tensor(hm_sb[:, mt_ * 1024 + tt2 * 512: mt_ * 1024 + (tt2 + 1) * 512],
                                       pb[4 + tt2][:, :], mvb[:, :], ALU.mult)
            t_hm = S.op("v", hmop, deps=[mm, t_mv])
            bank[4 + tt2] = t_hm
            hm_toks.append(t_hm)
    st2_mm = sq2_mm = None
    ev2_toks = []
    bw_mm = [hm_toks[0], hm_toks[0]]
    for oc in range(16):
        wstr2 = wstr2_b[oc % 2]
        d_bw = S.dma("g", lambda e, oc=oc, wstr2=wstr2: e.dma_start(
            out=wstr2[:, :], in_=dr["wbw"][oc].rearrange("p k c -> p (k c)")),
            deps=[bw_mm[oc % 2]])
        mms = {}
        for tt2 in range(2):
            mm = None
            for kk in range(4):
                def my(e, oc=oc, tt2=tt2, kk=kk, wstr2=wstr2):
                    return e.matmul(pb[tt2][:, :], wstr2[:, kk * 128:(kk + 1) * 128],
                                    hm_sb[:, kk * 1024 + tt2 * 512: kk * 1024 + (tt2 + 1) * 512],
                                    start=(kk == 0), stop=(kk == 3))
                mm = S.op("t", my, deps=[d_bw, bank[tt2]] + hm_toks if (kk == 0) else ())
            mms[tt2] = mm
        bw_mm[oc % 2] = mms[1]
        evs = []
        for tt2 in range(2):
            def ev(e, oc=oc, tt2=tt2):
                return e.tensor_tensor(x2p[:, oc * 1024 + tt2 * 512: oc * 1024 + (tt2 + 1) * 512],
                                       pb[tt2][:, :],
                                       x1n[:, oc * 1024 + tt2 * 512: oc * 1024 + (tt2 + 1) * 512],
                                       ALU.add)
            t_ = S.op("v", ev, deps=[mms[tt2], ap_toks[oc][0]])
            bank[tt2] = t_
            evs.append(t_)
        t_b = S.op("s", lambda e, oc=oc: e.activation(
            mT_b[:, :], x2p[:, oc * 1024:(oc + 1) * 1024], AF.Copy),
            deps=evs + ([st2_mm] if st2_mm else [e1, e2]))
        t_sq = S.op("s", lambda e: e.activation(sqb[:, :], mT_b[:, :], AF.Square),
                    deps=[t_b] + ([sq2_mm] if sq2_mm else []))
        for tt2 in range(2):
            def ms2(e, tt2=tt2, oc=oc):
                return e.matmul(pb[2 + tt2][:, :], ones_b[:, :], mT_b[:, tt2 * 512:(tt2 + 1) * 512],
                                start=(oc == 0), stop=(oc == 15))
            st2_mm = S.op("t", ms2, deps=[t_b, bank[2 + tt2]] if (tt2 == 0 or oc == 0) else ())
            def mq2(e, tt2=tt2, oc=oc):
                return e.matmul(pb[4 + tt2][:, :], ones_b[:, :], sqb[:, tt2 * 512:(tt2 + 1) * 512],
                                start=(oc == 0), stop=(oc == 15))
            sq2_mm = S.op("t", mq2, deps=[t_sq, bank[4 + tt2]] if (tt2 == 0 or oc == 0) else ())
        ev2_toks.append(evs[-1])
    l1 = S.op("v", lambda e: e.tensor_scalar_mul(msb[:, 0:512], pb[2][:, :], 1.0 / 2048),
              deps=[st2_mm, ap_toks[15][0]])
    l2 = S.op("v", lambda e: e.tensor_scalar_mul(msb[:, 512:1024], pb[3][:, :], 1.0 / 2048),
              deps=[st2_mm])
    l3 = S.op("v", lambda e: e.tensor_scalar_mul(rsb[:, 0:512], pb[4][:, :], 1.0 / 2048),
              deps=[sq2_mm])
    l4 = S.op("v", lambda e: e.tensor_scalar_mul(rsb[:, 512:1024], pb[5][:, :], 1.0 / 2048),
              deps=[sq2_mm])
    l5 = S.op("v", lambda e: e.tensor_tensor(pb[2][:, :], msb[:, 0:512], msb[:, 0:512], ALU.mult),
              deps=[l1, l3])
    l6 = S.op("v", lambda e: e.tensor_tensor(pb[3][:, :], msb[:, 512:1024], msb[:, 512:1024], ALU.mult),
              deps=[l2, l4])
    l7 = S.op("v", lambda e: e.tensor_tensor(rsb[:, 0:512], rsb[:, 0:512], pb[2][:, :], ALU.subtract),
              deps=[l5])
    l8 = S.op("v", lambda e: e.tensor_tensor(rsb[:, 512:1024], rsb[:, 512:1024], pb[3][:, :], ALU.subtract),
              deps=[l6])
    l9 = S.op("v", lambda e: e.tensor_scalar_add(rsb[:, :], rsb[:, :], 1e-5), deps=[l7, l8])
    l10 = S.op("s", lambda e: e.activation(rsb[:, :], rsb[:, :], AF.Ln), deps=[l9])
    l11 = S.op("s", lambda e: e.activation(rsb[:, :], rsb[:, :], AF.Exp, scale=-0.5), deps=[l10])
    for oc in range(16):
        sl = slice(oc * 1024, (oc + 1) * 1024)
        a1 = S.op("v", lambda e, sl=sl: e.tensor_tensor(x1n[:, sl], x2p[:, sl], msb[:, :],
                                                        ALU.subtract), deps=[l11, ev2_toks[oc]])
        a2 = S.op("g" if oc % 2 else "v", lambda e, sl=sl: e.tensor_tensor(
            x1n[:, sl], x1n[:, sl], rsb[:, :], ALU.mult), deps=[a1])
        S.dma(("y", "s", "g")[oc % 3], lambda e, oc=oc, sl=sl: e.dma_start(
            out=out[oc * 128:(oc + 1) * 128, :], in_=x1n[:, sl]), deps=[a2])


def kernel(**inputs):
    per_core = _prep(inputs)
    nc, _ = build_program()
    res = run_bass_kernel_spmd(nc, per_core, core_ids=list(range(8)))
    out = np.empty((B, L, D), np.float32)
    for c in range(8):
        b, half = c // 2, c % 2
        out[b, half * TOK:(half + 1) * TOK, :] = res.results[c]["out"].T
    return out


# version 11 (final)
# speedup vs baseline: 1.2436x; 1.2269x over previous
import numpy as np
import ml_dtypes
import concourse.bass as bass
import concourse.mybir as mybir
from concourse.bass_utils import run_bass_kernel_spmd

F32 = mybir.dt.float32
BF16 = mybir.dt.bfloat16
AF = mybir.ActivationFunctionType
ALU = mybir.AluOpType
AX = mybir.AxisListType

BASE_LR_INV = float(np.log(np.expm1(0.01)))
NS_A, NS_B, NS_C = 3.4445, -4.775, 2.0315

B, L, D = 4, 2048, 2048
SERIAL = 0
LAST_S = None
SCAN_LIMIT = 99
QP_LIMIT = 99
QH = 99
CH, NCH, KD, TOK = 512, 4, 16, 1024
BF = ml_dtypes.bfloat16


class Sched:
    ENGS = ("t", "v", "s", "g", "y")

    def __init__(self):
        self.q = {e: [] for e in self.ENGS}
        self.cnt = {"t": 0, "v": 0, "s": 0, "g": 0,
                    "dy": 0, "ds": 0, "dg": 0, "cc": 0}

    PAIR = {2: (("t", "v"), ("t", "s"), ("t", "g"), ("v", "t"), ("s", "t"), ("g", "t")),
            3: (("t", "v"), ("v", "t")), 4: (("t", "s"), ("s", "t")),
            5: (("t", "g"), ("g", "t")), 6: (("v", "s"), ("s", "v"))}

    def op(self, eng, fn, deps=()):
        if SERIAL == 1 and getattr(self, "last", None):
            deps = tuple(deps) + (self.last,)
        elif SERIAL >= 2:
            lasts = getattr(self, "lasts", {})
            for (a, b) in self.PAIR[SERIAL]:
                if eng == a and b in lasts:
                    deps = tuple(deps) + (lasts[b],)
        self.cnt[eng] += 1
        self.q[eng].append((fn, tuple(deps), (eng, 1)))
        self.last = (eng, self.cnt[eng])
        if not hasattr(self, "lasts"):
            self.lasts = {}
        self.lasts[eng] = (eng, self.cnt[eng])
        return (eng, self.cnt[eng])

    def dma(self, queue, fn, deps=()):
        sem = {"y": "dy", "s": "ds", "g": "dg"}[queue]
        if SERIAL and getattr(self, "last", None):
            deps = tuple(deps) + (self.last,)
        deps = tuple(deps) + ((sem, self.cnt[sem]),)
        self.cnt[sem] += 16
        self.q[queue].append((fn, deps, (sem, 16)))
        if SERIAL:
            self.last = (sem, self.cnt[sem])
        return (sem, self.cnt[sem])

    def coll(self, fn, deps=()):
        self.cnt["cc"] += 1
        self.q["g"].append((fn, tuple(deps), ("cc", 1)))
        return ("cc", self.cnt["cc"])

    W = 3584

    COST = {"t": (0.15, 0.15), "v": (0.7, 0.7), "s": (0.5, 0.5), "g": (1.6, 1.6),
            "dy": (0.1, 8.0), "ds": (0.1, 8.0), "dg": (0.1, 8.0), "cc": (0.1, 18.0)}

    def reschedule(self):
        import heapq
        nodes = []
        tokmap = {}
        cnt = {k: 0 for k in self.cnt}
        for qk in self.q:
            for (fn, deps, (sem, inc)) in self.q[qk]:
                cnt[sem] += inc
                tok = (sem, cnt[sem])
                tokmap[tok] = len(nodes)
                nodes.append([qk, fn, deps, sem, inc, tok])
        N = len(nodes)
        preds = [[] for _ in range(N)]
        succs = [[] for _ in range(N)]
        for nid, nd in enumerate(nodes):
            for d in nd[2]:
                if d in tokmap:
                    p = tokmap[d]
                    preds[nid].append(p)
                    succs[p].append(nid)
        prev_q = {}
        for nid, nd in enumerate(nodes):
            qk = nd[0]
            if not preds[nid] and qk in prev_q:
                preds[nid].append(prev_q[qk])
                succs[prev_q[qk]].append(nid)
            prev_q[qk] = nid
        prev_cc = None
        for nid, nd in enumerate(nodes):
            if nd[3] == "cc":
                if prev_cc is not None:
                    preds[nid].append(prev_cc)
                    succs[prev_cc].append(nid)
                prev_cc = nid
        occ = [self.COST[nodes[i][3]][0] for i in range(N)]
        lat = [self.COST[nodes[i][3]][1] for i in range(N)]
        prio = [0.0] * N
        indeg_r = [len(succs[i]) for i in range(N)]
        stack = [i for i in range(N) if indeg_r[i] == 0]
        while stack:
            n = stack.pop()
            prio[n] = lat[n] + max((prio[s] for s in succs[n]), default=0.0)
            for p in preds[n]:
                indeg_r[p] -= 1
                if indeg_r[p] == 0:
                    stack.append(p)
        indeg = [len(preds[i]) for i in range(N)]
        finish = [0.0] * N
        engine_time = {qk: 0.0 for qk in self.q}
        heap = []
        for i in range(N):
            if indeg[i] == 0:
                heapq.heappush(heap, (0.0, -prio[i], i, 0.0))
        order = {qk: [] for qk in self.q}
        scheduled = [False] * N
        nsched = 0
        while heap:
            key_t, negp, i, rt = heapq.heappop(heap)
            if scheduled[i]:
                continue
            qk = nodes[i][0]
            start = max(engine_time[qk], rt)
            if start > key_t + 1e-9:
                heapq.heappush(heap, (start, negp, i, rt))
                continue
            scheduled[i] = True
            nsched += 1
            engine_time[qk] = start + occ[i]
            finish[i] = start + lat[i]
            order[qk].append(i)
            for s in succs[i]:
                indeg[s] -= 1
                if indeg[s] == 0:
                    srt = max((finish[p] for p in preds[s]), default=0.0)
                    heapq.heappush(heap, (max(srt, engine_time[nodes[s][0]]), -prio[s], s, srt))
        assert nsched == N, (nsched, N)
        cnt2 = {k: 0 for k in self.cnt}
        remap = {}
        for qk in self.q:
            for i in order[qk]:
                sem, inc = nodes[i][3], nodes[i][4]
                cnt2[sem] += inc
                remap[nodes[i][5]] = (sem, cnt2[sem])
        for qk in self.q:
            self.q[qk] = [(nodes[i][1],
                           tuple(remap.get(d, d) for d in nodes[i][2]),
                           (nodes[i][3], nodes[i][4])) for i in order[qk]]
        return max(finish) if N else 0.0

    def finalize(self):
        dep_vals = {k: set() for k in self.cnt}
        for qk in self.q:
            for _, deps, _ in self.q[qk]:
                for (dn, dv) in deps:
                    if dv > 0:
                        dep_vals[dn].add(dv)
        for v in range(1, self.cnt["cc"] + 1):
            dep_vals["cc"].add(v)
        self.rank = {}
        self.rtot = {}
        for name, vals in dep_vals.items():
            r = {}
            step = 16 if name in ("dy", "ds", "dg") else 1
            for i, v in enumerate(sorted(vals)):
                r[v] = (i + 1) * step
            self.rank[name] = r
            self.rtot[name] = len(vals) * step

    def replay(self, engine, eng_key, sems):
        waited = {}
        cur = {k: 0 for k in self.cnt}
        for fn, deps, (sem_name, inc) in self.q[eng_key]:
            for (dn, dv) in deps:
                if dv <= 0:
                    continue
                rv = self.rank[dn][dv]
                k = (rv - 1) // self.W
                v = rv - k * self.W
                if waited.get((dn, k), -1) < v:
                    engine.wait_ge(sems[(dn, k)], v)
                    waited[(dn, k)] = v
            cur[sem_name] += inc
            n = cur[sem_name]
            ins = fn(engine)
            if n in self.rank[sem_name]:
                rv = self.rank[sem_name][n]
                k = (rv - 1) // self.W
                ins.then_inc(sems[(sem_name, k)], inc)

    def nsems(self, name):
        return max(1, (self.rtot.get(name, 0) + self.W - 1) // self.W)


def _prep(inp):
    x = np.ascontiguousarray(np.asarray(inp["L0_attn_x"], np.float32))
    w0 = np.asarray(inp["L0_attn_w0"], np.float32)
    w1 = np.asarray(inp["L0_attn_w1"], np.float32)
    w2 = np.asarray(inp["L0_attn_w2"], np.float32)
    Wqkv = np.asarray(inp["L0_attn_W_qkv"], np.float32)
    Wlr = np.asarray(inp["L0_attn_W_lr"], np.float32)
    blr = np.asarray(inp["L0_attn_b_lr"], np.float32)
    Wproj = np.asarray(inp["L0_attn_W_proj"], np.float32)
    onorm = np.asarray(inp["L0_attn_onorm_w"], np.float32)
    A = np.asarray(inp["L0_ffn_A"], np.float32)
    Bw = np.asarray(inp["L0_ffn_Bw"], np.float32)
    Wg = np.asarray(inp["L0_ffn_Wg"], np.float32)

    wproj_r = np.ascontiguousarray(Wproj.reshape(16, 128, 16, 128).transpose(2, 1, 0, 3))
    AT = A.transpose(2, 0, 1).reshape(D, 512)
    wa_r = np.ascontiguousarray(AT.reshape(16, 128, 4, 128).transpose(2, 1, 0, 3))
    Bw_l = Bw.transpose(0, 2, 1).reshape(512, D)
    wbw_r = np.ascontiguousarray(Bw_l.reshape(4, 128, 16, 128).transpose(2, 1, 0, 3))
    wgt_r = np.ascontiguousarray(Wg.T.reshape(16, 128, 8).transpose(1, 0, 2))
    onorm_r = np.ascontiguousarray(onorm.reshape(128, 1))

    ones_b = np.ones((128, 128), dtype=BF)
    ones_f = np.ones((128, 128), dtype=np.float32)
    ident_b = np.eye(128, dtype=np.float32).astype(BF)
    ident_f = np.eye(128, dtype=np.float32)
    e64 = np.zeros((128, 512), np.float32)
    for col in range(512):
        e64[col // 64, col] = 1.0
    e64_b = e64.astype(BF)

    xT = [np.ascontiguousarray(x[b].T) for b in range(B)]

    per_core = []
    for c in range(8):
        b, half = c // 2, c % 2
        hs = slice(half * 8, half * 8 + 8)
        Wcols = np.ascontiguousarray(Wqkv.reshape(D, 3, 16, 128)[:, :, hs, :].reshape(D, 24 * 128))
        wq_r = np.ascontiguousarray(Wcols.reshape(16, 128, 24, 128).transpose(2, 1, 0, 3))
        Wlrc = Wlr.reshape(D, 3, 16)[:, :, hs].reshape(D, 24)
        wlr_r = np.ascontiguousarray(Wlrc.reshape(16, 128, 24).transpose(1, 0, 2))
        blr_r = np.ascontiguousarray(blr.reshape(3, 16)[:, hs].reshape(24, 1))
        w0h, w1h, w2h = w0[hs], w1[hs], w2[hs]
        pc = {
            "xT": xT[b], "wq": wq_r, "wlr": wlr_r, "blr": blr_r,
            "w0t": np.ascontiguousarray(w0h.transpose(2, 0, 1)),
            "w1t": np.ascontiguousarray(w1h.transpose(2, 0, 1)),
            "w2t": np.ascontiguousarray(w2h.transpose(2, 0, 1)),
            "w0o": np.ascontiguousarray(w0h.transpose(1, 0, 2)),
            "w1o": np.ascontiguousarray(w1h.transpose(1, 0, 2)),
            "w2o": np.ascontiguousarray(w2h.transpose(1, 0, 2)),
            "wproj": wproj_r, "wa": wa_r, "wbw": wbw_r, "wgt": wgt_r,
            "onorm": onorm_r, "ones_b": ones_b, "ones_f": ones_f,
            "ident_b": ident_b, "ident_f": ident_f, "e64_b": e64_b,
        }
        per_core.append(pc)
    return per_core


def build_program(upto=99, dbg_names=()):
    nc = bass.Bass("TRN2", target_bir_lowering=False)
    S = Sched()
    dbg_outs = {}

    dr = {}
    def di(name, shape, dt=F32):
        dr[name] = nc.dram_tensor(name, list(shape), dt, kind="ExternalInput").ap()
    di("xT", (D, L))
    di("wq", (24, 128, 16, 128))
    di("wlr", (128, 16, 24))
    di("blr", (24, 1))
    for n in ("w0t", "w1t", "w2t", "w0o", "w1o", "w2o"):
        di(n, (128, 8, 128))
    di("wproj", (16, 128, 16, 128))
    di("wa", (4, 128, 16, 128))
    di("wbw", (16, 128, 4, 128))
    di("wgt", (128, 16, 8))
    di("onorm", (128, 1))
    di("ones_b", (128, 128), BF16); di("ones_f", (128, 128))
    di("ident_b", (128, 128), BF16); di("ident_f", (128, 128))
    di("e64_b", (128, 512), BF16)
    out = nc.dram_tensor("out", [D, TOK], F32, kind="ExternalOutput").ap()
    cc_in = nc.dram_tensor("cc_in", [4, 1024, 512], BF16, kind="Internal").ap()
    cc_out = nc.dram_tensor("cc_out", [4, 2048, 512], BF16, kind="Internal").ap()

    BASE = (nc.sbuf_base + 31) // 32 * 32
    OFF_Q, OFF_A, OFF_S = BASE, BASE + 98304, BASE + 163840
    SCR = 23040
    OFF_FIX = OFF_S + SCR
    ARENA = 211680
    nc.alloc_sbuf_tensor("arena", [128, ARENA], mybir.dt.int8)

    sb = {}
    def sba(name, shape, dt, off):
        assert off % 32 == 0, (name, off)
        assert off >= BASE
        sb[name] = nc.alloc_sbuf_tensor_at(name, list(shape), dt, offset=off)
        return sb[name]

    qkv = sba("qkv", [128, 24 * 2048], BF16, OFF_Q)
    x1 = sba("x1", [128, 16 * 1024], F32, OFF_A)
    x2p = sba("x2p", [128, 16 * 1024], F32, OFF_A)
    x1nb = sba("x1nb", [128, 16 * 1024], BF16, OFF_Q + 65536)
    xT_sb = sba("xT_sb", [128, 16 * 2048], BF16, OFF_A)
    oiN = sba("oiN", [128, 8 * 2048], BF16, OFF_A)
    attn = sba("attn", [128, 16 * 1024], BF16, OFF_A + 32768)
    x1n = sba("x1n", [128, 16 * 1024], F32, OFF_Q)

    w0T = sba("w0T", [128, 8 * 128], F32, OFF_FIX)
    w1T = sba("w1T", [128, 8 * 128], F32, OFF_FIX + 4096)
    w2T = sba("w2T", [128, 8 * 128], F32, OFF_FIX + 8192)
    w0fw = sba("w0fw", [128, 8 * 128], BF16, OFF_FIX + 12288)
    w1fw = sba("w1fw", [128, 8 * 128], BF16, OFF_FIX + 14336)
    w1Tfw = sba("w1Tfw", [128, 8 * 128], BF16, OFF_FIX + 16384)
    w2fw = sba("w2fw", [128, 8 * 128], BF16, OFF_FIX + 18432)
    CON = OFF_FIX + 20480
    ones_b = sba("ones_b_s", [128, 128], BF16, CON)
    ident_b = sba("ident_b_s", [128, 128], BF16, CON + 256)
    ones_f = sba("ones_f_s", [128, 128], F32, CON + 512)
    ident_f = sba("ident_f_s", [128, 128], F32, CON + 1024)
    e64_s = sba("e64_s", [128, 512], BF16, CON + 1536)
    onorm_s = sba("onorm_s", [128, 1], F32, CON + 2560)
    blr_s = sba("blr_s", [24, 1], F32, CON + 2592)
    lr_tok = sba("lr_tok", [128, 16 * 24], F32, CON + 2624)
    wn_s = sba("wn_s", [128, 24], F32, CON + 4160)

    wq_b = [sba(f"wq_b{i}", [128, 16 * 128], BF16, OFF_S + i * 4096) for i in range(2)]
    lrp = sba("lrp", [24, 2048], F32, OFF_S + 8192)
    sp_r = sba("sp_r", [24, 512], F32, OFF_S + 16384)
    sp_a = sba("sp_a", [24, 512], F32, OFF_S + 18432)
    wlr_b = sba("wlr_b", [128, 16 * 24], BF16, OFF_S + 20480)

    pb = [nc.alloc_psum_tensor(f"pb{i}", [128, 512], F32) for i in range(6)]
    pt0 = nc.alloc_psum_tensor("pt0", [128, 1024], BF16)
    pt1 = nc.alloc_psum_tensor("pt1", [128, 1024], BF16)

    sems = None

    def load(qu, dst, src, deps=()):
        return S.dma(qu, lambda e, d=dst, s=src: e.dma_start(out=d, in_=s), deps)

    load("y", ones_b[:, :], dr["ones_b"])
    load("y", ident_b[:, :], dr["ident_b"])
    load("y", ones_f[:, :], dr["ones_f"])
    load("y", ident_f[:, :], dr["ident_f"])
    load("y", e64_s[:, :], dr["e64_b"])
    load("y", onorm_s[:, :], dr["onorm"])
    d_blr = load("y", blr_s[:, :], dr["blr"])
    load("y", w0T[:, :], dr["w0t"].rearrange("p j d -> p (j d)"))
    load("y", w1T[:, :], dr["w1t"].rearrange("p j d -> p (j d)"))
    d_wstate = load("y", w2T[:, :], dr["w2t"].rearrange("p j d -> p (j d)"))
    load("g", w0fw[:, :], dr["w0o"].rearrange("p j d -> p (j d)"))
    load("g", w1fw[:, :], dr["w1o"].rearrange("p j d -> p (j d)"))
    load("g", w1Tfw[:, :], dr["w1t"].rearrange("p j d -> p (j d)"))
    d_wfw = load("g", w2fw[:, :], dr["w2o"].rearrange("p j d -> p (j d)"))
    d_consts = ("dy", S.cnt["dy"])

    xT_dst = xT_sb[:, :].rearrange("p (k t) -> p k t", k=16)
    xT_src = dr["xT"].rearrange("(k p) t -> p k t", p=128)
    d_xt0 = load("g", xT_dst[:, :, 0:1024], xT_src[:, :, 0:1024])
    d_xt1 = load("g", xT_dst[:, :, 1024:2048], xT_src[:, :, 1024:2048])
    d_wlr = load("g", wlr_b[:, :], dr["wlr"].rearrange("p k c -> p (k c)"))

    t_blradj = S.op("v", lambda e: e.tensor_scalar_add(blr_s[:, :], blr_s[:, :], BASE_LR_INV),
                    deps=[d_blr])
    ev_tok = {}
    last_evac = {tt: None for tt in range(4)}
    mm_tails = {}
    for cc in range(24):
        par = cc % 2
        wdeps = [] if cc < 2 else mm_tails[cc - 2]
        d_wq = load("g", wq_b[par][:, :], dr["wq"][cc].rearrange("p k c -> p (k c)"), wdeps)
        for tt in range(4):
            deps = [d_wq, d_xt0 if tt < 2 else d_xt1]
            if last_evac[tt] is not None:
                deps.append(last_evac[tt])
            mm_tok = None
            for k in range(16):
                def mk(e, par=par, tt=tt, k=k):
                    return e.matmul(pb[tt][:, :], wq_b[par][:, k * 128:(k + 1) * 128],
                                    xT_sb[:, k * 2048 + tt * 512: k * 2048 + (tt + 1) * 512],
                                    start=(k == 0), stop=(k == 15))
                mm_tok = S.op("t", mk, deps=deps if k == 0 else ())
            mm_tails.setdefault(cc, []).append(mm_tok)
            def ev(e, cc=cc, tt=tt):
                return e.activation(qkv[:, cc * 2048 + tt * 512: cc * 2048 + (tt + 1) * 512],
                                    pb[tt][:, :], AF.Silu)
            tok = S.op("s", ev, deps=[mm_tok])
            ev_tok[(cc, tt)] = tok
            last_evac[tt] = tok

    lr_ev = []
    for tt in range(4):
        mm_tok = None
        for k in range(16):
            def mk(e, tt=tt, k=k):
                return e.matmul(pb[4][0:24, :], wlr_b[:, k * 24:(k + 1) * 24],
                                xT_sb[:, k * 2048 + tt * 512: k * 2048 + (tt + 1) * 512],
                                start=(k == 0), stop=(k == 15))
            deps = [d_wlr, d_xt0 if tt < 2 else d_xt1] + ([lr_ev[-1]] if lr_ev else [])
            mm_tok = S.op("t", mk, deps=deps if k == 0 else ())
        def ev(e, tt=tt):
            return e.tensor_scalar(lrp[:, tt * 512:(tt + 1) * 512], pb[4][0:24, :],
                                   blr_s[:, 0:1], None, ALU.add)
        lr_ev.append(S.op("v", ev, deps=[mm_tok, t_blradj]))
    sp_done = []
    for hh in range(4):
        sl = slice(hh * 512, (hh + 1) * 512)
        war = list(lr_ev) + ([sp_done[-1]] if sp_done else [])
        a1 = S.op("s", lambda e, sl=sl: e.activation(sp_r[:, :], lrp[:, sl], AF.Relu), deps=war)
        a2 = S.op("s", lambda e, sl=sl: e.activation(sp_a[:, :], lrp[:, sl], AF.Abs), deps=war)
        a3 = S.op("s", lambda e, sl=sl: e.activation(sp_a[:, :], sp_a[:, :], AF.Exp, scale=-1.0), deps=[a2])
        a4 = S.op("s", lambda e, sl=sl: e.activation(sp_a[:, :], sp_a[:, :], AF.Ln, bias=1.0), deps=[a3])
        sp_done.append(S.op("v", lambda e, sl=sl: e.tensor_add(lrp[:, sl], sp_a[:, :], sp_r[:, :]),
                            deps=[a4, a1]))
    lrt_last = None
    for g4 in range(4):
        trs = None
        for u in range(4):
            tt16 = g4 * 4 + u
            def mk(e, tt16=tt16, u=u):
                return e.transpose(pb[5][:, u * 24:(u + 1) * 24],
                                   lrp[0:24, tt16 * 128:(tt16 + 1) * 128],
                                   ident_f[0:24, 0:24])
            dd = list(sp_done) + ([lrt_last] if lrt_last else [])
            trs = S.op("t", mk, deps=dd if u == 0 else ())
        lrt_last = S.op("v", lambda e, g4=g4: e.tensor_copy(lr_tok[:, g4 * 96:(g4 + 1) * 96],
                                                            pb[5][:, 0:96]), deps=[trs])
    p1_done = dict(qkv=[ev_tok[(23, tt)] for tt in range(4)], lr=lrt_last,
                   d_xt=(d_xt0, d_xt1))

    def dbg(name, ap_, deps):
        if name not in dbg_names:
            return None
        o = nc.dram_tensor("dbg_" + name, list(ap_.shape), ap_.tensor.dtype,
                           kind="ExternalOutput").ap()
        dbg_outs[name] = True
        return S.dma("y", lambda e, o=o, a=ap_: e.dma_start(out=o, in_=a), deps)

    dbg("qkv", qkv[:, :], p1_done["qkv"])
    dbg("lr", lr_tok[:, :], [lrt_last])

    ctx = dict(nc=nc, S=S, sb=sb, pb=pb, pt0=pt0, pt1=pt1, dr=dr, dbg=dbg, load=load,
               p1=p1_done, out=out, cc_in=cc_in, cc_out=cc_out, OFF_S=OFF_S, sba=sba,
               BASE=BASE, OFF_A=OFF_A, OFF_A2=OFF_A + 32768, OFF_Q=OFF_Q,
               d_wstate=d_wstate, d_wfw=d_wfw, d_consts=d_consts)
    ctx["upto"] = upto
    if upto >= 2:
        build_scan(ctx)
    if upto >= 3:
        build_tail(ctx)

    S.op("g", lambda e: e.engine_nop(),
         deps=[("dy", S.cnt["dy"]), ("dg", S.cnt["dg"]), ("ds", S.cnt["ds"])])

    est = S.reschedule()
    global LAST_S
    LAST_S = S
    S.finalize()
    sems = {}
    for name in ("t", "v", "s", "g", "dy", "ds", "dg", "cc"):
        for k in range(S.nsems(name)):
            sems[(name, k)] = nc.alloc_semaphore(f"{name}{k}")
    with nc.Block() as blk:
        @blk.tensor
        def _(e):
            S.replay(e, "t", sems)
        @blk.vector
        def _(e):
            S.replay(e, "v", sems)
        @blk.scalar
        def _(e):
            S.replay(e, "s", sems)
        @blk.gpsimd
        def _(e):
            S.replay(e, "g", sems)
        @blk.sync
        def _(e):
            S.replay(e, "y", sems)
    return nc, dbg_outs


def build_scan(ctx):
    nc, S, sb, pb, dr = ctx["nc"], ctx["S"], ctx["sb"], ctx["pb"], ctx["dr"]
    pt0, pt1, dbg = ctx["pt0"], ctx["pt1"], ctx["dbg"]
    sba, OFF_S = ctx["sba"], ctx["OFF_S"]
    qkv, oiN = sb["qkv"], sb["oiN"]
    w0T, w1T, w2T = sb["w0T"], sb["w1T"], sb["w2T"]
    w0fw, w1fw, w1Tfw, w2fw = sb["w0fw"], sb["w1fw"], sb["w1Tfw"], sb["w2fw"]
    ones_b, ident_b = sb["ones_b_s"], sb["ident_b_s"]
    ones_f, ident_f = sb["ones_f_s"], sb["ident_f_s"]
    onorm_s, lr_tok, wn_s = sb["onorm_s"], sb["lr_tok"], sb["wn_s"]
    wT = [w0T, w1T, w2T]
    p1 = ctx["p1"]

    RH = ctx["OFF_A2"]
    rA = sba("rA", [128, 4096], BF16, RH)
    rB = sba("rB", [128, 4096], BF16, RH + 8192)
    rC = sba("rC", [128, 4096], BF16, RH + 16384)
    rD = sba("rD", [128, 4096], BF16, RH + 24576)
    scrf = [sba(f"scf{i}", [128, 1024], F32, OFF_S + i * 4096) for i in range(4)]
    pk = sba("pk", [128, 384], F32, OFF_S + 16384)
    scrb = sba("scrb", [128, 1024], BF16, OFF_S + 17920)
    scrb2 = sba("scrb2", [128, 1024], BF16, OFF_S + 19968)
    KIN, QIN, FRB, RNM, A0C, A2C = 0, 128, 256, 280, 304, 336
    PKS = 384
    LRS = 16 * 24
    A_, B_, C_, D_ = scrf

    st = dict()

    d_ws = ctx["d_wstate"]
    last = p1["qkv"] + [p1["lr"], d_ws, ctx["d_wfw"], ctx["d_consts"]]
    for w in range(3):
        t1 = S.op("s", lambda e, w=w: e.activation(A_[:, :], wT[w][:, :], AF.Square), deps=last)
        r1 = S.op("v", lambda e, w=w: e.reduce_sum(
            wn_s[:, w * 8:w * 8 + 8], A_[:, :].rearrange("p (j d) -> p j d", j=8), axis=AX.X),
            deps=[t1])
        last = [r1]
    t1 = S.op("s", lambda e: e.activation(wn_s[:, :], wn_s[:, :], AF.Ln), deps=last)
    wn_done = S.op("s", lambda e: e.activation(wn_s[:, :], wn_s[:, :], AF.Exp, scale=0.5), deps=[t1])

    st["w_upd"] = [wn_done] * 3
    st["w1t_cast"] = wn_done
    st["rfree"] = [wn_done]
    st["pt0"] = [wn_done]
    st["pt1"] = [wn_done]
    bank_free = {i: wn_done for i in range(6)}

    h_evk = h_evq = h_sqk = h_sqq = None
    h_last = []
    for c0 in range(NCH):
        for j in range(8):
            ksl = qkv[:, 8 * 2048 + j * 2048 + c0 * 512: 8 * 2048 + j * 2048 + (c0 + 1) * 512]
            qsl = qkv[:, j * 2048 + c0 * 512: j * 2048 + (c0 + 1) * 512]
            trk = trq = None
            for u in range(4):
                def mk(e, u=u, ksl=ksl):
                    return e.transpose(pt0[:, u * 128:(u + 1) * 128], ksl[:, u * 128:(u + 1) * 128],
                                       ident_b[:, :])
                trk = S.op("t", mk, deps=st["pt0"] + p1["qkv"] if u == 0 else ())
            for u in range(4):
                def mq(e, u=u, qsl=qsl):
                    return e.transpose(pt1[:, u * 128:(u + 1) * 128], qsl[:, u * 128:(u + 1) * 128],
                                       ident_b[:, :])
                trq = S.op("t", mq, deps=st["pt1"] if u == 0 else ())
            h_evk = S.op("v", lambda e: e.tensor_copy(scrb[:, 0:512], pt0[:, 0:512]),
                         deps=[trk] + ([h_sqk] if h_sqk else []))
            h_evq = S.op("s", lambda e: e.activation(scrb[:, 512:1024], pt1[:, 0:512], AF.Copy),
                         deps=[trq] + ([h_sqq] if h_sqq else []))
            st["pt0"] = [h_evk]
            st["pt1"] = [h_evq]
            h_sqk = S.op("s", lambda e: e.activation(A_[:, 0:512], scrb[:, 0:512], AF.Square),
                         deps=[h_evk] + ([h_last[-2]] if len(h_last) >= 2 else []))
            h_sqq = S.op("v", lambda e: e.activation(A_[:, 512:1024], scrb[:, 512:1024], AF.Square)
                         if False else e.tensor_mul(A_[:, 512:1024], scrb[:, 512:1024], scrb[:, 512:1024]),
                         deps=[h_evq] + ([h_last[-1]] if len(h_last) >= 2 else []))
            rk = S.op("v", lambda e, c0=c0, j=j: e.reduce_sum(
                pk[:, KIN + c0 * 32 + j * 4: KIN + c0 * 32 + j * 4 + 4],
                A_[:, 0:512].rearrange("p (u d) -> p u d", u=4), axis=AX.X), deps=[h_sqk])
            rq = S.op("v", lambda e, c0=c0, j=j: e.reduce_sum(
                pk[:, QIN + c0 * 32 + j * 4: QIN + c0 * 32 + j * 4 + 4],
                A_[:, 512:1024].rearrange("p (u d) -> p u d", u=4), axis=AX.X), deps=[h_sqq])
            h_last += [rk, rq]
    h1 = S.op("s", lambda e: e.activation(pk[:, 0:256], pk[:, 0:256], AF.Ln), deps=h_last[-2:])
    h2 = S.op("s", lambda e: e.activation(pk[:, 0:256], pk[:, 0:256], AF.Exp, scale=0.5), deps=[h1])
    h3 = S.op("v", lambda e: e.tensor_scalar_add(pk[:, 0:256], pk[:, 0:256], 1e-5), deps=[h2])
    hq_done = S.op("v", lambda e: e.reciprocal(pk[:, 0:256], pk[:, 0:256]), deps=[h3])
    st["scrA"] = [hq_done]
    st["scrB"] = [wn_done]
    st["scrC"] = [wn_done]
    st["scrD"] = [wn_done]
    st["scrb"] = [h_sqk, h_sqq]
    st["scrb2"] = [wn_done]
    st["alpha_free"] = [wn_done]
    st["oi_q"] = {}

    QB, KB, VB = 0, 8 * 2048, 16 * 2048
    oiN_unscaled = {}
    rms_done = {}

    def emit_qpath(cm1):
        toks = []
        prev = {}
        for j in range(8):
            h = (j % 2) * 512
            bq, bh, bo = (0, 1, 2) if j % 2 == 0 else (3, 4, 5)
            qsl = qkv[:, QB + j * 2048 + cm1 * 512: QB + j * 2048 + (cm1 + 1) * 512]
            mmg = mmh = None
            for u in range(4):
                def mg(e, j=j, u=u, qsl=qsl, bq=bq):
                    return e.matmul(pb[bq][:, u * 128:(u + 1) * 128], qsl[:, u * 128:(u + 1) * 128],
                                    w0fw[:, j * 128:(j + 1) * 128], start=True, stop=True)
                mmg = S.op("t", mg, deps=([bank_free[bq], st["w_upd"][0]] + p1["qkv"]) if u == 0 else ())
            for u in range(4):
                def mh(e, j=j, u=u, qsl=qsl, bh=bh):
                    return e.matmul(pb[bh][:, u * 128:(u + 1) * 128], qsl[:, u * 128:(u + 1) * 128],
                                    w2fw[:, j * 128:(j + 1) * 128], start=True, stop=True)
                mmh = S.op("t", mh, deps=[bank_free[bh], st["w_upd"][2]] if u == 0 else ())
            war = prev.get((j % 2, "A"), st["scrA"][-1] if j < 2 else None)
            t_sl = None
            for u in range(4):
                def msl(e, u=u, h=h, bq=bq, cm1=cm1, j=j):
                    return e.activation(A_[:, h + u * 128: h + (u + 1) * 128],
                                        pb[bq][:, u * 128:(u + 1) * 128], AF.Silu,
                                        scale=pk[:, QIN + cm1 * 32 + j * 4 + u: QIN + cm1 * 32 + j * 4 + u + 1])
                t_sl = S.op("s", msl,
                            deps=([mmg, hq_done] + ([war] if war else []) +
                                  (st["scrA"] if j < 2 else [])) if u == 0 else ())
            sq_ap = bass.AP(pk, QIN + cm1 * 32 + j * 4, [[PKS, 128], [1, 4], [0, 128]])
            t_qhn = S.op("v", lambda e, sq_ap=sq_ap, h=h, bh=bh: e.tensor_tensor(
                B_[:, h:h + 512].rearrange("p (u c) -> p u c", u=4),
                pb[bh][:, :].rearrange("p (u c) -> p u c", u=4), sq_ap, ALU.mult),
                deps=[mmh, hq_done] + ([prev[(j % 2, "B")]] if (j % 2, "B") in prev else st["scrB"]))
            bank_free[bq], bank_free[bh] = t_sl, t_qhn
            t_hq = S.op("g", lambda e, h=h: e.tensor_mul(scrb[:, h:h + 512], A_[:, h:h + 512],
                                                         B_[:, h:h + 512]),
                        deps=[t_sl, t_qhn] + ([prev[(j % 2, "hq")]] if (j % 2, "hq") in prev else st["scrb"]))
            trs = None
            ptx = pt0 if j % 2 == 0 else pt1
            for u in range(4):
                def mt(e, u=u, h=h, ptx=ptx):
                    return e.transpose(ptx[:, u * 128:(u + 1) * 128],
                                       scrb[:, h + u * 128: h + (u + 1) * 128], ident_b[:, :])
                trs = S.op("t", mt, deps=[t_hq] + (st["pt0"] if j % 2 == 0 else st["pt1"]) if u == 0 else ())
            if j % 2 == 0:
                t_hqT = S.op("s", lambda e, h=h, ptx=ptx: e.activation(
                    scrb2[:, h:h + 512], ptx[:, 0:512], AF.Copy),
                    deps=[trs] + ([prev[(j % 2, "hqT")]] if (j % 2, "hqT") in prev else st["scrb2"]))
            else:
                t_hqT = S.op("v", lambda e, h=h, ptx=ptx: e.tensor_copy(
                    scrb2[:, h:h + 512], ptx[:, 0:512]),
                    deps=[trs] + ([prev[(j % 2, "hqT")]] if (j % 2, "hqT") in prev else st["scrb2"]))
            if j % 2 == 0:
                st["pt0"] = [t_hqT]
            else:
                st["pt1"] = [t_hqT]
            def moi(e, j=j, h=h, bo=bo):
                return e.matmul(pb[bo][:, :], w1fw[:, j * 128:(j + 1) * 128], scrb2[:, h:h + 512],
                                start=True, stop=True)
            mmoi = S.op("t", moi, deps=[t_hqT, bank_free[bo], st["w_upd"][1]])
            t_ev = S.op("v", lambda e, j=j, cm1=cm1, bo=bo: e.tensor_copy(
                oiN[:, j * 2048 + cm1 * 512: j * 2048 + (cm1 + 1) * 512], pb[bo][:, :]),
                deps=[mmoi, st.get("rms_war", wn_done)])
            bank_free[bo] = t_ev
            prev[(j % 2, "A")] = t_hq
            prev[(j % 2, "B")] = t_hq
            prev[(j % 2, "hq")] = trs
            prev[(j % 2, "hqT")] = mmoi
            toks.append(t_ev)
        oiN_unscaled[cm1] = toks
        st["scrA"] = [prev[(0, "A")], prev[(1, "A")]]
        st["scrB"] = [prev[(0, "B")], prev[(1, "B")]]
        st["scrb"] = [prev[(0, "hq")], prev[(1, "hq")]]
        st["scrb2"] = [prev[(0, "hqT")], prev[(1, "hqT")]]

    def emit_rms(cm1):
        toks = []
        prev = {}
        for j in range(8):
            h = (j % 2) * 512
            bo = 3 + (j % 2)
            sl = oiN[:, j * 2048 + cm1 * 512: j * 2048 + (cm1 + 1) * 512]
            t_sq = S.op("s", lambda e, sl=sl, h=h: e.activation(scrb[:, h:h + 512], sl, AF.Square),
                        deps=[oiN_unscaled[cm1][j]] + ([prev.get((j % 2, "sq"))] if (j % 2, "sq") in prev
                                                       else st["scrb"]))
            def msq(e, h=h, bo=bo):
                return e.matmul(pb[bo][:, :], ones_b[:, :], scrb[:, h:h + 512], start=True, stop=True)
            mm = S.op("t", msq, deps=[t_sq, bank_free[bo]])
            t_u = S.op("s", lambda e, h=h, bo=bo: e.activation(C_[:, h:h + 512], pb[bo][:, :],
                                                               AF.Copy, scale=1.0 / 128, bias=1e-5),
                       deps=[mm] + ([prev.get((j % 2, "C"))] if (j % 2, "C") in prev else st["scrC"]))
            bank_free[bo] = t_u
            t_l = S.op("s", lambda e, h=h: e.activation(C_[:, h:h + 512], C_[:, h:h + 512], AF.Ln),
                       deps=[t_u])
            t_e = S.op("s", lambda e, h=h: e.activation(C_[:, h:h + 512], C_[:, h:h + 512], AF.Exp,
                                                        scale=-0.5), deps=[t_l])
            t_m = S.op("g", lambda e, sl=sl, h=h: e.tensor_mul(sl, sl, C_[:, h:h + 512]),
                       deps=[t_e])
            t_g = S.op("v", lambda e, sl=sl: e.tensor_scalar_mul(sl, sl, onorm_s[:, 0:1]),
                       deps=[t_m])
            prev[(j % 2, "sq")] = mm
            prev[(j % 2, "C")] = t_g
            toks.append(t_g)
        st["scrb"] = [prev[(0, "sq")], prev[(1, "sq")]]
        st["scrC"] = [prev[(0, "C")], prev[(1, "C")]]
        st["rms_war"] = toks[-1]
        return toks

    cc_in, cc_out = ctx["cc_in"], ctx["cc_out"]
    A2A = [[0, 1], [2, 3], [4, 5], [6, 7]]
    ctx["cc_toks"] = []

    dc_toks = {}

    def emit_dc(cch):
        dc_toks[cch] = S.dma("y", lambda e, cch=cch: e.dma_start(
            out=cc_in[cch].rearrange("(j p) t -> p j t", p=128),
            in_=bass.AP(oiN, cch * 512, [[16384, 128], [2048, 8], [1, 512]])),
            deps=rms_done[cch])

    def emit_coll(cch):
        if ctx["upto"] < 3:
            return
        ctx["cc_toks"].append(S.coll(lambda e, cch=cch: e.collective_compute(
            "AllGather", ALU.bypass, replica_groups=A2A, ins=[cc_in[cch]], outs=[cc_out[cch]]),
            deps=[dc_toks[cch]]))
    ctx["emit_coll"] = emit_coll

    for c in range(NCH):
        if c >= 2:
            emit_coll(c - 2)
        if c > 0:
            emit_qpath(c - 1)
        lr0_ap = bass.AP(lr_tok, (c * 4) * 24 + 0, [[LRS, 128], [1, 8], [24, 4]])
        lr2_ap = bass.AP(lr_tok, (c * 4) * 24 + 16, [[LRS, 128], [1, 8], [24, 4]])
        ki_ap = bass.AP(pk, KIN + c * 32, [[PKS, 128], [4, 8], [1, 4]])
        b5 = S.op("v", lambda e, lr0_ap=lr0_ap, ki_ap=ki_ap: e.tensor_tensor(
            pk[:, A0C:A0C + 32].rearrange("p (j u) -> p j u", j=8), lr0_ap, ki_ap, ALU.mult),
            deps=[hq_done, p1["lr"]] + st["alpha_free"])
        b6 = S.op("v", lambda e, lr2_ap=lr2_ap, ki_ap=ki_ap: e.tensor_tensor(
            pk[:, A2C:A2C + 32].rearrange("p (j u) -> p j u", j=8), lr2_ap, ki_ap, ALU.mult),
            deps=[hq_done, p1["lr"]] + st["alpha_free"])
        gev = []
        pm = {}
        for p in range(4):
            ja, jb = 2 * p, 2 * p + 1
            war = st["rfree"] if (c > 0 and p == 0) else []
            trk = trv = None
            for j2, j in enumerate((ja, jb)):
                ksl = qkv[:, KB + j * 2048 + c * 512: KB + j * 2048 + (c + 1) * 512]
                vsl = qkv[:, VB + j * 2048 + c * 512: VB + j * 2048 + (c + 1) * 512]
                for u in range(4):
                    def mk(e, u=u, ksl=ksl, j2=j2):
                        return e.transpose(pt0[:, j2 * 512 + u * 128: j2 * 512 + (u + 1) * 128],
                                           ksl[:, u * 128:(u + 1) * 128], ident_b[:, :])
                    trk = S.op("t", mk, deps=(st["pt0"] + p1["qkv"]) if (j2 == 0 and u == 0) else ())
            for j2, j in enumerate((ja, jb)):
                vsl = qkv[:, VB + j * 2048 + c * 512: VB + j * 2048 + (c + 1) * 512]
                for u in range(4):
                    def mv(e, u=u, vsl=vsl, j2=j2):
                        return e.transpose(pt1[:, j2 * 512 + u * 128: j2 * 512 + (u + 1) * 128],
                                           vsl[:, u * 128:(u + 1) * 128], ident_b[:, :])
                    trv = S.op("t", mv, deps=st["pt1"] if (j2 == 0 and u == 0) else ())
            t_kt = S.op("v", lambda e: e.tensor_copy(rB[:, 0:1024], pt0[:, :]),
                        deps=[trk] + war + ([pm.get("ktfree")] if "ktfree" in pm else []))
            t_vt = S.op("v", lambda e: e.tensor_copy(rB[:, 1024:2048], pt1[:, :]),
                        deps=[trv] + war + ([pm.get("vtfree")] if "vtfree" in pm else []))
            st["pt0"] = [t_kt]
            st["pt1"] = [t_vt]
            mm_g = {}
            for j2, j in enumerate((ja, jb)):
                ksl = qkv[:, KB + j * 2048 + c * 512: KB + j * 2048 + (c + 1) * 512]
                vsl = qkv[:, VB + j * 2048 + c * 512: VB + j * 2048 + (c + 1) * 512]
                bg, bh, bd = (3, 4, 5) if j2 == 0 else (0, 1, 2)
                for u in range(4):
                    def m1(e, u=u, ksl=ksl, j=j, bg=bg):
                        return e.matmul(pb[bg][:, u * 128:(u + 1) * 128], ksl[:, u * 128:(u + 1) * 128],
                                        w0fw[:, j * 128:(j + 1) * 128], start=True, stop=True)
                    mm_g[(j2, "g")] = S.op("t", m1, deps=[bank_free[bg], st["w_upd"][0]] if u == 0 else ())
                for u in range(4):
                    def m2(e, u=u, ksl=ksl, j=j, bh=bh):
                        return e.matmul(pb[bh][:, u * 128:(u + 1) * 128], ksl[:, u * 128:(u + 1) * 128],
                                        w2fw[:, j * 128:(j + 1) * 128], start=True, stop=True)
                    mm_g[(j2, "h")] = S.op("t", m2, deps=[bank_free[bh], st["w_upd"][2]] if u == 0 else ())
                for u in range(4):
                    def m3(e, u=u, vsl=vsl, j=j, bd=bd):
                        return e.matmul(pb[bd][:, u * 128:(u + 1) * 128], vsl[:, u * 128:(u + 1) * 128],
                                        w1Tfw[:, j * 128:(j + 1) * 128], start=True, stop=True)
                    mm_g[(j2, "d")] = S.op("t", m3, deps=[bank_free[bd], st.get("w1t_cast", st["w_upd"][1])] if u == 0 else ())
            kin_a = bass.AP(pk, KIN + c * 32 + ja * 4, [[PKS, 128], [1, 4], [0, 128]])
            kin_b = bass.AP(pk, KIN + c * 32 + jb * 4, [[PKS, 128], [1, 4], [0, 128]])
            n1a = S.op("v", lambda e, kin_a=kin_a: e.tensor_tensor(
                A_[:, 0:512].rearrange("p (u d) -> p u d", u=4),
                pb[3][:, :].rearrange("p (u d) -> p u d", u=4), kin_a, ALU.mult),
                deps=[mm_g[(0, "g")], hq_done] + st["scrA"])
            n1b = S.op("v", lambda e, kin_b=kin_b: e.tensor_tensor(
                A_[:, 512:1024].rearrange("p (u d) -> p u d", u=4),
                pb[0][:, :].rearrange("p (u d) -> p u d", u=4), kin_b, ALU.mult),
                deps=[mm_g[(1, "g")], hq_done] + st["scrA"])
            n2 = S.op("s", lambda e: e.activation(B_[:, :], A_[:, :], AF.Sigmoid),
                      deps=[n1a, n1b] + st["scrB"])
            n3a = S.op("v", lambda e, kin_a=kin_a: e.tensor_tensor(
                C_[:, 0:512].rearrange("p (u d) -> p u d", u=4),
                pb[4][:, :].rearrange("p (u d) -> p u d", u=4), kin_a, ALU.mult),
                deps=[mm_g[(0, "h")], hq_done] + st["scrC"])
            n3b = S.op("v", lambda e, kin_b=kin_b: e.tensor_tensor(
                C_[:, 512:1024].rearrange("p (u d) -> p u d", u=4),
                pb[1][:, :].rearrange("p (u d) -> p u d", u=4), kin_b, ALU.mult),
                deps=[mm_g[(1, "h")], hq_done] + st["scrC"])
            n4 = S.op("v", lambda e: e.tensor_mul(D_[:, :], A_[:, :], B_[:, :]),
                      deps=[n2] + st["scrD"])
            n5 = S.op("v", lambda e: e.scalar_tensor_tensor(A_[:, :], A_[:, :], 1.0, D_[:, :],
                                                            ALU.add, ALU.subtract), deps=[n4])
            n6 = S.op("v", lambda e: e.tensor_mul(B_[:, :], B_[:, :], A_[:, :]), deps=[n5])
            n7a = S.op("v", lambda e: e.tensor_tensor(A_[:, 0:512], pb[5][:, :], C_[:, 0:512],
                                                      ALU.mult), deps=[mm_g[(0, "d")], n6, n3a, n5])
            n7b = S.op("v", lambda e: e.tensor_tensor(A_[:, 512:1024], pb[2][:, :], C_[:, 512:1024],
                                                      ALU.mult), deps=[mm_g[(1, "d")], n3b, n6])
            n8 = S.op("v", lambda e: e.tensor_mul(rB[:, 2048:3072], A_[:, :], B_[:, :]),
                      deps=[n7a, n7b] + ([pm.get("dgba")] if "dgba" in pm else []) + war)
            n9a = S.op("v", lambda e: e.tensor_tensor(rB[:, 3072:3584], pb[5][:, :], D_[:, 0:512],
                                                      ALU.mult),
                       deps=[n7a] + ([pm.get("dhbm")] if "dhbm" in pm else []) + war)
            n9b = S.op("v", lambda e: e.tensor_tensor(rB[:, 3584:4096], pb[2][:, :], D_[:, 512:1024],
                                                      ALU.mult), deps=[n7b])
            n10 = S.op("v", lambda e: e.tensor_mul(A_[:, :], D_[:, :], C_[:, :]), deps=[n9a, n9b, n8])
            lr1_ap = bass.AP(lr_tok, (c * 4) * 24 + 8 + ja, [[LRS, 128], [1, 2], [24, 4], [0, 128]])
            n11 = S.op("v", lambda e, lr1_ap=lr1_ap: e.tensor_tensor(
                rD[:, 0:1024].rearrange("p (j u d) -> p j u d", j=2, u=4),
                A_[:, :].rearrange("p (j u d) -> p j u d", j=2, u=4), lr1_ap, ALU.mult),
                deps=[n10, p1["lr"]] + ([pm.get("hlr1")] if "hlr1" in pm else []))
            a0_ap = bass.AP(pk, A0C + ja * 4, [[PKS, 128], [1, 2], [4, 4] if False else [1, 4], [0, 128]])
            a0_ap = bass.AP(pk, A0C + ja * 4, [[PKS, 128], [4, 2], [1, 4], [0, 128]])
            a2_ap = bass.AP(pk, A2C + ja * 4, [[PKS, 128], [4, 2], [1, 4], [0, 128]])
            f1 = S.op("v", lambda e, a0_ap=a0_ap: e.tensor_tensor(
                rB[:, 2048:3072].rearrange("p (j u d) -> p j u d", j=2, u=4),
                rB[:, 2048:3072].rearrange("p (j u d) -> p j u d", j=2, u=4), a0_ap, ALU.mult),
                deps=[b5, n8])
            f2 = S.op("v", lambda e, a2_ap=a2_ap: e.tensor_tensor(
                rB[:, 3072:4096].rearrange("p (j u d) -> p j u d", j=2, u=4),
                rB[:, 3072:4096].rearrange("p (j u d) -> p j u d", j=2, u=4), a2_ap, ALU.mult),
                deps=[b6, n9a, n9b])
            for j2, j in enumerate((ja, jb)):
                bk = pb[3] if j2 == 0 else pb[0]
                ko, vo, do_, ho = j2 * 512, 1024 + j2 * 512, 2048 + j2 * 512, j2 * 512
                last_mm = None
                dep0 = [f1, f2, n11, n1a if j2 == 0 else n1b, t_kt, t_vt]
                for u in range(4):
                    def g0(e, bk=bk, do_=do_, ko=ko, u=u):
                        return e.matmul(bk[:, 0:128], rB[:, do_ + u * 128: do_ + (u + 1) * 128],
                                        rB[:, ko + u * 128: ko + (u + 1) * 128],
                                        start=(u == 0), stop=(u == 3))
                    last_mm = S.op("t", g0, deps=dep0 if u == 0 else ())
                for u in range(4):
                    def g1(e, bk=bk, vo=vo, ho=ho, u=u):
                        return e.matmul(bk[:, 128:256], rB[:, vo + u * 128: vo + (u + 1) * 128],
                                        rD[:, ho + u * 128: ho + (u + 1) * 128],
                                        start=(u == 0), stop=(u == 3))
                    last_mm = S.op("t", g1, deps=())
                for u in range(4):
                    def g2(e, bk=bk, ko=ko, j2=j2, u=u):
                        return e.matmul(bk[:, 256:384], rB[:, 3072 + j2 * 512 + u * 128: 3072 + j2 * 512 + (u + 1) * 128],
                                        rB[:, ko + u * 128: ko + (u + 1) * 128],
                                        start=(u == 0), stop=(u == 3))
                    last_mm = S.op("t", g2, deps=())
                ev = S.op("v", lambda e, j=j, bk=bk: e.tensor_copy(rA[:, j * 384: j * 384 + 384],
                                                                   bk[:, 0:384]),
                          deps=[last_mm] + (st["rfree"] if (c > 0 and p == 0 and j2 == 0) else []))
                bank_free[3 if j2 == 0 else 0] = ev
                gev.append(ev)
            bank_free[4] = n3a
            bank_free[1] = n3b
            bank_free[5] = n9a
            bank_free[2] = n9b
            pm = dict(ktfree=last_mm, vtfree=last_mm, dgba=f1, dhbm=f2, hlr1=last_mm)
            st["scrA"] = [n11]
            st["scrB"] = [n8]
            st["scrC"] = [n10]
            st["scrD"] = [n10]
        st["alpha_free"] = [f1, f2]

        if c > 0:
            rms_done[c - 1] = emit_rms(c - 1)
            emit_dc(c - 1)
        if c == 3:
            ctx["wpre_toks"] = []
            for i in range(2):
                ctx["wpre_toks"].append(S.dma("g", lambda e, i=i: e.dma_start(
                    out=bass.AP(qkv, 16384 + (4 * i) * 2048 + 1024,
                                [[49152, 128], [2048, 4], [128, 4], [1, 128]]),
                    in_=ctx["dr"]["wproj"][i].rearrange("p k c -> p (k c)")),
                    deps=rms_done[2]))
        x2_tok, xread = ns_block(S, pb, pt0, pt1, ident_b, ones_f, rA, rB, rC, rD,
                                 scrf, scrb, scrb2, pk, FRB, PKS, gev, bank_free, st)
        w_update(S, pb, wT, wn_s, pk, RNM, PKS, scrf, rA, rB, rC, rD, x2_tok, xread,
                 bank_free, w0fw, w1fw, w1Tfw, w2fw, ident_f, st)

    emit_coll(2)
    emit_qpath(3)
    rms_done[3] = emit_rms(3)
    emit_dc(3)
    ctx["scan_done"] = [t for ts in rms_done.values() for t in ts]
    ctx["rms_toks"] = rms_done
    ctx["r_free"] = st["rfree"] + st["w_upd"]
    dbg("oiN", oiN[:, :], ctx["scan_done"])


def ns_block(S, pb, pt0, pt1, ident_b, ones_f, rA, rB, rC, rD, scrf, scrb, scrb2,
             pk, FRB, PKS, gev, bank_free, st):
    A_, B_, C_, D_ = scrf

    def xap(r, w, nj=8, j0=0):
        return bass.AP(r, j0 * 384 + w * 128, [[4096, 128], [384, nj], [1, 128]])

    fro = []
    t2 = None
    for w in range(3):
        for half in range(2):
            sqv = A_[:, half * 512:(half + 1) * 512]
            t1 = S.op("s", lambda e, w=w, half=half, sqv=sqv: e.activation(
                sqv.rearrange("p (j d) -> p j d", j=4), xap(rA, w, 4, half * 4), AF.Square),
                deps=(gev + st["scrA"] + st.get("dbg_war", [])) if (w == 0 and half == 0)
                     else ([t2] if t2 else []))
            t2 = S.op("v", lambda e, w=w, half=half, sqv=sqv: e.reduce_sum(
                pk[:, FRB + w * 8 + half * 4: FRB + w * 8 + half * 4 + 4],
                sqv.rearrange("p (j d) -> p j d", j=4), axis=AX.X), deps=[t1])
            fro.append(t2)
    st["scrA"] = [t2]
    def mfr(e):
        return e.matmul(pb[2][:, 0:24], ones_f[:, 0:128], pk[:, FRB:FRB + 24],
                        start=True, stop=True)
    mm = S.op("t", mfr, deps=fro + [bank_free[2]])
    c1 = S.op("v", lambda e: e.tensor_copy(pk[:, FRB:FRB + 24], pb[2][:, 0:24]), deps=[mm])
    bank_free[2] = c1
    c2 = S.op("s", lambda e: e.activation(pk[:, FRB:FRB + 24], pk[:, FRB:FRB + 24], AF.Ln), deps=[c1])
    c3 = S.op("s", lambda e: e.activation(pk[:, FRB:FRB + 24], pk[:, FRB:FRB + 24], AF.Exp, scale=0.5),
              deps=[c2])
    c4 = S.op("v", lambda e: e.tensor_scalar_add(pk[:, FRB:FRB + 24], pk[:, FRB:FRB + 24], 1e-7),
              deps=[c3])
    c5 = S.op("v", lambda e: e.reciprocal(pk[:, FRB:FRB + 24], pk[:, FRB:FRB + 24]), deps=[c4])

    xt_ev = []
    for w in range(3):
        ptx = (pt0, pt1, pt0)[w]
        key = "pt0" if ptx is pt0 else "pt1"
        trs = None
        for j in range(8):
            def mt(e, w=w, j=j, ptx=ptx):
                return e.transpose(ptx[:, j * 128:(j + 1) * 128],
                                   rA[:, j * 384 + w * 128: j * 384 + (w + 1) * 128], ident_b[:, :])
            trs = S.op("t", mt, deps=(st[key] + gev) if j == 0 else ())
        ev = S.op("v", lambda e, w=w, ptx=ptx: e.tensor_copy(
            xap(rC, w), ptx[:, :].rearrange("p (j d) -> p j d", j=8)), deps=[trs])
        st[key] = [ev]
        xt_ev.append(ev)

    x0t, xt0t = [], []
    for w in range(3):
        iv_ap = bass.AP(pk, FRB + w * 8, [[PKS, 128], [1, 8], [0, 128]])
        x0t.append(S.op("v", lambda e, w=w, iv_ap=iv_ap: e.tensor_tensor(
            xap(rB, w), xap(rA, w), iv_ap, ALU.mult), deps=[c5] + gev + (st["scrB"] if w == 0 else [])))
        xt0t.append(S.op("g", lambda e, w=w, iv_ap=iv_ap: e.tensor_tensor(
            xap(rD, w), xap(rC, w), iv_ap, ALU.mult), deps=[c5, xt_ev[w]] + gev + (st["scrD"] if w == 0 else [])))

    x_src, xT_src = rB, rD
    x_dst = rA
    dep_x, dep_xT = x0t, xt0t
    x2_tok = []
    prevh = {}
    for step in range(2):
        new_dep_x, new_dep_xT = [], []
        for w in range(3):
            for grp in range(2):
                j0 = grp * 4
                hh = ((w * 2 + grp) % 2) * 512
                lastA = None
                for j in range(j0, j0 + 4):
                    def mA(e, w=w, j=j, j0=j0, xT_src=xT_src):
                        return e.matmul(pb[2][:, (j - j0) * 128:(j - j0 + 1) * 128],
                                        xT_src[:, j * 384 + w * 128: j * 384 + (w + 1) * 128],
                                        xT_src[:, j * 384 + w * 128: j * 384 + (w + 1) * 128],
                                        start=True, stop=True)
                    lastA = S.op("t", mA, deps=([bank_free[2]] + dep_xT) if j == j0 else ())
                evA = S.op("s", lambda e, hh=hh: e.activation(scrb[:, hh:hh + 512], pb[2][:, :],
                                                              AF.Copy),
                           deps=[lastA] + (prevh[(hh, "As")] if (hh, "As") in prevh else st["scrb"]))
                bank_free[2] = evA
                lastA2 = None
                for j4 in range(4):
                    def mA2(e, j4=j4, hh=hh):
                        return e.matmul(pb[3][:, j4 * 128:(j4 + 1) * 128],
                                        scrb[:, hh + j4 * 128: hh + (j4 + 1) * 128],
                                        scrb[:, hh + j4 * 128: hh + (j4 + 1) * 128],
                                        start=True, stop=True)
                    lastA2 = S.op("t", mA2, deps=[evA, bank_free[3]] if j4 == 0 else ())
                t1 = S.op("s", lambda e, hh=hh: e.activation(C_[:, hh:hh + 512], pb[3][:, :],
                                                             AF.Copy, scale=NS_C),
                          deps=[lastA2] + ([prevh[(hh, "t1")]] if (hh, "t1") in prevh else st["scrC"]))
                bank_free[3] = t1
                tBm = S.op("v", lambda e, hh=hh: e.scalar_tensor_tensor(
                    scrb2[:, hh:hh + 512], scrb[:, hh:hh + 512], NS_B, C_[:, hh:hh + 512],
                    ALU.mult, ALU.add),
                    deps=[t1] + (prevh[(hh, "Bm")] if (hh, "Bm") in prevh else st["scrb2"]))
                lastB = lastXT = None
                for j in range(j0, j0 + 4):
                    def mBX(e, w=w, j=j, j0=j0, x_src=x_src, hh=hh):
                        return e.matmul(pb[4][:, (j - j0) * 128:(j - j0 + 1) * 128],
                                        scrb2[:, hh + (j - j0) * 128: hh + (j - j0 + 1) * 128],
                                        x_src[:, j * 384 + w * 128: j * 384 + (w + 1) * 128],
                                        start=True, stop=True)
                    lastB = S.op("t", mBX, deps=([tBm, bank_free[4]] + dep_x) if j == j0 else ())
                if step == 0:
                    for j in range(j0, j0 + 4):
                        def mXTB(e, w=w, j=j, j0=j0, x_src=x_src, hh=hh):
                            return e.matmul(pb[5][:, (j - j0) * 128:(j - j0 + 1) * 128],
                                            x_src[:, j * 384 + w * 128: j * 384 + (w + 1) * 128],
                                            scrb2[:, hh + (j - j0) * 128: hh + (j - j0 + 1) * 128],
                                            start=True, stop=True)
                        lastXT = S.op("t", mXTB, deps=([tBm, bank_free[5]] + dep_x) if j == j0 else ())
                def upd(e, w=w, j0=j0, x_src=x_src, x_dst=x_dst):
                    return e.scalar_tensor_tensor(xap(x_dst, w, 4, j0), xap(x_src, w, 4, j0),
                                                  NS_A, pb[4][:, :].rearrange("p (j d) -> p j d", j=4),
                                                  ALU.mult, ALU.add)
                tX = S.op("v", upd, deps=[lastB] + (x2_tok if step == 1 and w == 0 and grp == 0 else []))
                bank_free[4] = tX
                new_dep_x.append(tX)
                if step == 0:
                    def updT(e, w=w, j0=j0, xT_src=xT_src):
                        return e.scalar_tensor_tensor(xap(rC, w, 4, j0), xap(xT_src, w, 4, j0),
                                                      NS_A, pb[5][:, :].rearrange("p (j d) -> p j d", j=4),
                                                      ALU.mult, ALU.add)
                    tXT = S.op("v", updT, deps=[lastXT])
                    bank_free[5] = tXT
                    new_dep_xT.append(tXT)
                prevh[(hh, "As")] = [lastA2, tBm]
                prevh[(hh, "t1")] = tBm
                prevh[(hh, "Bm")] = [lastB] if step == 1 else [lastB, lastXT]
        if step == 0:
            x_src, xT_src = rA, rC
            x_dst = rB
            dep_x, dep_xT = new_dep_x, new_dep_xT
        else:
            x2_tok = new_dep_x
    st["scrb"] = prevh[(0, "As")] + prevh[(512, "As")]
    st["scrb2"] = prevh[(0, "Bm")] + prevh[(512, "Bm")]
    st["scrC"] = [prevh[(0, "t1")], prevh[(512, "t1")]]
    xread = dep_x + dep_xT
    return x2_tok, xread


def w_update(S, pb, wT, wn_s, pk, RNM, PKS, scrf, rA, rB, rC, rD, x2_tok, xread,
             bank_free, w0fw, w1fw, w1Tfw, w2fw, ident_f, st):
    A_, B_, C_, D_ = scrf
    pre = {}
    red = []
    sq_war = st["scrB"]
    prev_red = {0: None, 512: None}
    for w in range(3):
        for half in range(2):
            bk = pb[w * 2 + half]
            bi = w * 2 + half
            def mpre(e, w=w, half=half, bk=bk):
                return e.tensor_tensor(bk[:, :], wT[w][:, half * 512:(half + 1) * 512],
                                       bass.AP(rB, (half * 4) * 384 + w * 128,
                                               [[4096, 128], [384, 4], [1, 128]]), ALU.add)
            tp = S.op("v", mpre, deps=[bank_free[bi]] + x2_tok)
            pre[(w, half)] = (bk, tp)
            hh = ((w * 2 + half) % 2) * 512
            t1 = S.op("s", lambda e, bk=bk, hh=hh: e.activation(B_[:, hh:hh + 512], bk[:, :], AF.Square),
                      deps=[tp] + (sq_war if w == 0 or half == 0 and w == 0 else [])
                           + ([prev_red[hh]] if prev_red[hh] else sq_war))
            t2 = S.op("v", lambda e, w=w, half=half, hh=hh: e.reduce_sum(
                pk[:, RNM + w * 8 + half * 4: RNM + w * 8 + half * 4 + 4],
                B_[:, hh:hh + 512].rearrange("p (j d) -> p j d", j=4), axis=AX.X), deps=[t1])
            red.append(t2)
            prev_red[hh] = t2
    st["scrB"] = red[-2:]
    c2 = S.op("s", lambda e: e.activation(pk[:, RNM:RNM + 24], pk[:, RNM:RNM + 24], AF.Ln), deps=red)
    c3 = S.op("s", lambda e: e.activation(pk[:, RNM:RNM + 24], pk[:, RNM:RNM + 24], AF.Exp, scale=0.5),
              deps=[c2])
    c4 = S.op("v", lambda e: e.tensor_scalar_add(pk[:, RNM:RNM + 24], pk[:, RNM:RNM + 24], 1e-5),
              deps=[c3])
    c5 = S.op("v", lambda e: e.reciprocal(pk[:, RNM:RNM + 24], pk[:, RNM:RNM + 24]), deps=[c4])
    c6 = S.op("v", lambda e: e.tensor_mul(pk[:, RNM:RNM + 24], pk[:, RNM:RNM + 24], wn_s[:, :]),
              deps=[c5])
    wnew = {}
    for w in range(3):
        for half in range(2):
            bk, tp = pre[(w, half)]
            sc_ap = bass.AP(pk, RNM + w * 8 + half * 4, [[PKS, 128], [1, 4], [0, 128]])
            def mnew(e, w=w, half=half, bk=bk, sc_ap=sc_ap):
                return e.tensor_tensor(wT[w][:, half * 512:(half + 1) * 512].rearrange(
                    "p (j d) -> p j d", j=4), bk[:, :].rearrange("p (j d) -> p j d", j=4),
                    sc_ap, ALU.mult)
            wnew[(w, half)] = S.op("v", mnew, deps=[c6, tp])
    for w in range(3):
        for half in range(2):
            bank_free[w * 2 + half] = wnew[(w, half)]
    fwmap = {0: w0fw, 1: w1fw, 2: w2fw}
    upd_tok = [None, None, None]
    for w in range(3):
        evs = []
        for half in range(2):
            bk = pb[half]
            trs = None
            for j4 in range(4):
                j = half * 4 + j4
                def mt(e, w=w, j=j, j4=j4, bk=bk):
                    return e.transpose(bass.AP(bk, j4 * 128, [[512, 128], [1, 128]]),
                                       wT[w][:, j * 128:(j + 1) * 128], ident_f[:, :])
                trs = S.op("t", mt, deps=[wnew[(w, 0)], wnew[(w, 1)], bank_free[half]] if j4 == 0 else ())
            ev = S.op("s", lambda e, w=w, half=half, bk=bk: e.activation(
                fwmap[w][:, half * 512:(half + 1) * 512], bk[:, :], AF.Copy), deps=[trs])
            bank_free[half] = ev
            evs.append(ev)
        upd_tok[w] = evs[-1]
    tcp = S.op("v", lambda e: e.tensor_copy(w1Tfw[:, :], wT[1][:, :]), deps=[wnew[(1, 0)], wnew[(1, 1)]])
    st["w_upd"] = [upd_tok[0], upd_tok[1], upd_tok[2]]
    st["w1t_cast"] = tcp
    st["rfree"] = x2_tok + xread + [wnew[(w, h)] for w in range(3) for h in range(2)] + [tcp]


def build_tail(ctx):
    nc, S, sb, pb, dr = ctx["nc"], ctx["S"], ctx["sb"], ctx["pb"], ctx["dr"]
    pt0, pt1, dbg, load = ctx["pt0"], ctx["pt1"], ctx["dbg"], ctx["load"]
    sba, OFF_S = ctx["sba"], ctx["OFF_S"]
    out, cc_in, cc_out = ctx["out"], ctx["cc_in"], ctx["cc_out"]
    oiN, attn, x1, x1nb, x1n, x2p = (sb["oiN"], sb["attn"], sb["x1"], sb["x1nb"],
                                     sb["x1n"], sb["x2p"])
    ones_b, ident_f, e64_s = sb["ones_b_s"], sb["ident_f_s"], sb["e64_s"]
    scan_done = ctx["scan_done"]


    wstr2_b = [sba("wstr2A", [128, 512], BF16, OFF_S + 4096),
               sba("wstr2B", [128, 512], BF16, OFF_S)]
    wg_b = sba("wg_b", [128, 128], BF16, OFF_S + 5120)
    xres = [sba(f"xres{i}", [128, 1024], F32, OFF_S + 8192 + i * 4096) for i in range(2)]
    hm_sb = sba("hm_sb", [128, 4096], BF16, OFF_S + 8192)
    p5k = sba("p5k", [128, 256], F32, OFF_S + 16384)
    mT_b = sba("mT_b", [128, 1024], BF16, OFF_S + 17408)
    mvb = sba("mvb", [128, 512], BF16, OFF_S + 19456)
    sqb = sba("sqb", [128, 1024], BF16, OFF_S + 20480)

    bank = {i: list(scan_done) for i in range(6)}

    def BL(x):
        return x if isinstance(x, list) else [x]

    rms_toks, r_free = ctx["rms_toks"], ctx["r_free"]
    ctx["emit_coll"](3)
    cc_toks = ctx["cc_toks"]
    pid = nc.partition_id()
    t0 = (pid % 2) * 1024
    qkv = sb["qkv"]
    a_src = bass.AP(cc_out.tensor, (pid % 2) * 2097152, [[512, 128], [128 * 512, 16], [1, 512]])
    a_dst = bass.AP(qkv, 32768, [[49152, 128], [1024, 16], [1, 512]])
    d_attn0 = S.dma("s", lambda e, a_src=a_src, a_dst=a_dst: e.dma_start(
        out=a_dst, in_=a_src), deps=[cc_toks[0], cc_toks[2]])
    at1 = []
    for kh in range(2):
        src = bass.AP(cc_out.tensor, (pid % 2) * 2097152 + 1048576 + kh * 524288,
                      [[512, 128], [128 * 512, 8], [1, 512]])
        dst = bass.AP(qkv, 32768 + 512 + kh * 8192, [[49152, 128], [1024, 8], [1, 512]])
        at1.append(S.dma("y" if kh == 0 else "g",
                         lambda e, src=src, dst=dst: e.dma_start(out=dst, in_=src),
                         deps=[cc_toks[1], cc_toks[3]]))
    d_attn_t = [d_attn0, at1]
    dbg("attn", qkv[:, 32768:49152], [d_attn0] + at1)

    msb = sba("msb", [128, 1024], F32, OFF_S + 9728)
    rsb = sba("rsb", [128, 1024], F32, OFF_S + 9728 + 4096)
    last_wp = None
    last_x = None
    ev_toks = []
    st_mm = None
    sq_mm = None
    def wgrp_base(wi):
        return 16384 + (4 * (wi % 2)) * 2048 + 1024 + (wi // 2) * 512
    wp_mm = [None] * 4
    for oc in range(16):
        wi = oc % 4
        if oc < 2:
            d_wp = ctx["wpre_toks"][oc]
        else:
            d_wp = S.dma("g", lambda e, oc=oc, wi=wi: e.dma_start(
                out=bass.AP(qkv, wgrp_base(wi),
                            [[49152, 128], [2048, 4], [128, 4], [1, 128]]),
                in_=dr["wproj"][oc].rearrange("p k c -> p (k c)")),
                deps=rms_toks[3] if oc < 4 else wp_mm[wi])
        d_xr = S.dma("y", lambda e, oc=oc, t0=t0: e.dma_start(
            out=xres[oc % 2][:, :],
            in_=bass.AP(dr["xT"].tensor, oc * 128 * 2048 + t0, [[2048, 128], [1, 1024]])),
            deps=list(ev_toks[-2]) if oc >= 2 else [])
        mms = {}
        for tt in range(2):
            mm = None
            for k in range(16):
                def mp(e, tt=tt, k=k, wi=wi):
                    wc = wgrp_base(wi) + (k // 4) * 2048 + (k % 4) * 128
                    ac = 32768 + k * 1024 + tt * 512
                    return e.matmul(pb[tt][:, :], qkv[:, wc:wc + 128],
                                    qkv[:, ac:ac + 512],
                                    start=(k == 0), stop=(k == 15))
                ad = d_attn_t[tt] if tt == 0 else d_attn_t[1]
                mm = S.op("t", mp, deps=([d_wp] + BL(bank[tt]) + BL(ad)) if k == 0 else ())
            mms[tt] = mm
        wp_mm[wi] = [mms[0], mms[1]]
        evs = []
        for tt in range(2):
            def ev(e, oc=oc, tt=tt):
                return e.tensor_tensor(x1[:, oc * 1024 + tt * 512: oc * 1024 + (tt + 1) * 512],
                                       pb[tt][:, :], xres[oc % 2][:, tt * 512:(tt + 1) * 512],
                                       ALU.add)
            t_ = S.op("v", ev, deps=[mms[tt], d_xr])
            bank[tt] = t_
            evs.append(t_)
        t_b = S.op("s", lambda e, oc=oc: e.activation(
            mT_b[:, :], x1[:, oc * 1024:(oc + 1) * 1024], AF.Copy), deps=evs + ([st_mm] if st_mm else []))
        t_sq = S.op("s", lambda e: e.activation(sqb[:, :], mT_b[:, :], AF.Square),
                    deps=[t_b] + ([sq_mm] if sq_mm else []))
        for tt in range(2):
            def ms(e, tt=tt, oc=oc):
                return e.matmul(pb[2 + tt][:, :], ones_b[:, :], mT_b[:, tt * 512:(tt + 1) * 512],
                                start=(oc == 0), stop=(oc == 15))
            st_mm = S.op("t", ms, deps=([t_b] + ([st_mm] if st_mm else BL(bank[2]) + BL(bank[3])))
                         if tt == 0 else ())
            def mq(e, tt=tt, oc=oc):
                return e.matmul(pb[4 + tt][:, :], ones_b[:, :], sqb[:, tt * 512:(tt + 1) * 512],
                                start=(oc == 0), stop=(oc == 15))
            sq_mm = S.op("t", mq, deps=([t_sq] + ([sq_mm] if sq_mm else BL(bank[4]) + BL(bank[5])))
                         if tt == 0 else ())
        ev_toks.append(list(evs))
    l1 = S.op("v", lambda e: e.tensor_scalar_mul(msb[:, 0:512], pb[2][:, :], 1.0 / 2048),
              deps=[st_mm] + scan_done)
    l2 = S.op("v", lambda e: e.tensor_scalar_mul(msb[:, 512:1024], pb[3][:, :], 1.0 / 2048),
              deps=[st_mm] + scan_done)
    l3 = S.op("v", lambda e: e.tensor_scalar_mul(rsb[:, 0:512], pb[4][:, :], 1.0 / 2048),
              deps=[sq_mm] + scan_done)
    l4 = S.op("v", lambda e: e.tensor_scalar_mul(rsb[:, 512:1024], pb[5][:, :], 1.0 / 2048),
              deps=[sq_mm] + scan_done)
    l5 = S.op("v", lambda e: e.tensor_tensor(pb[2][:, :], msb[:, 0:512], msb[:, 0:512], ALU.mult),
              deps=[l1, l3])
    l6 = S.op("v", lambda e: e.tensor_tensor(pb[3][:, :], msb[:, 512:1024], msb[:, 512:1024], ALU.mult),
              deps=[l2, l4])
    l7 = S.op("v", lambda e: e.tensor_tensor(rsb[:, 0:512], rsb[:, 0:512], pb[2][:, :], ALU.subtract),
              deps=[l5])
    l8 = S.op("v", lambda e: e.tensor_tensor(rsb[:, 512:1024], rsb[:, 512:1024], pb[3][:, :], ALU.subtract),
              deps=[l6])
    l9 = S.op("v", lambda e: e.tensor_scalar_add(rsb[:, :], rsb[:, :], 1e-5), deps=[l7, l8])
    l10 = S.op("s", lambda e: e.activation(rsb[:, :], rsb[:, :], AF.Ln), deps=[l9])
    l11 = S.op("s", lambda e: e.activation(rsb[:, :], rsb[:, :], AF.Exp, scale=-0.5), deps=[l10])
    bank[2], bank[3], bank[4], bank[5] = l7, l8, l3, l4
    ap_toks = []
    for oc in range(16):
        sl = slice(oc * 1024, (oc + 1) * 1024)
        a1 = S.op("v", lambda e, sl=sl: e.tensor_tensor(x1n[:, sl], x1[:, sl], msb[:, :],
                                                        ALU.subtract), deps=[l11] + ev_toks[oc])
        a2 = S.op("g" if oc % 2 else "v", lambda e, sl=sl: e.tensor_tensor(
            x1n[:, sl], x1n[:, sl], rsb[:, :], ALU.mult), deps=[a1])
        a3 = S.op("s", lambda e, sl=sl: e.activation(x1nb[:, sl], x1n[:, sl], AF.Copy), deps=[a2])
        ap_toks.append((a2, a3))
    dbg("x1n", x1n[:, :], [ap_toks[-1][0]])

    d_wg = load("g", wg_b[:, :], dr["wgt"].rearrange("p k e -> p (k e)"))
    lg_mm = {}
    for ti in range(8):
        mm = None
        for k in range(16):
            def ml(e, ti=ti, k=k):
                return e.matmul(pb[0][:, ti * 8:(ti + 1) * 8],
                                x1nb[:, k * 1024 + ti * 128: k * 1024 + (ti + 1) * 128],
                                wg_b[:, k * 8:(k + 1) * 8], start=(k == 0), stop=(k == 15))
            mm = S.op("t", ml, deps=([d_wg] + [t[1] for t in ap_toks] + BL(bank[0]))
                      if (ti == 0 and k == 0) else ())
        lg_mm[ti] = mm
    LG, MX, NEG, ES, SE, SC, MS = 0, 64, 72, 80, 144, 152, 160
    s1 = S.op("v", lambda e: e.tensor_copy(p5k[:, LG:LG + 64], pb[0][:, 0:64]), deps=[lg_mm[7]])
    bank[0] = s1
    s2 = S.op("v", lambda e: e.reduce_max(p5k[:, MX:MX + 8],
                                          p5k[:, LG:LG + 64].rearrange("p (t e) -> p t e", t=8),
                                          axis=AX.X), deps=[s1])
    s3 = S.op("v", lambda e: e.tensor_scalar_mul(p5k[:, NEG:NEG + 8], p5k[:, MX:MX + 8], -1.0),
              deps=[s2])
    es_toks = []
    for ti in range(8):
        def se_(e, ti=ti):
            return e.activation(p5k[:, ES + ti * 8: ES + ti * 8 + 8], p5k[:, LG + ti * 8: LG + ti * 8 + 8],
                                AF.Exp, bias=p5k[:, NEG + ti: NEG + ti + 1])
        es_toks.append(S.op("s", se_, deps=[s3] if ti == 0 else ()))
    s4 = S.op("v", lambda e: e.reduce_sum(p5k[:, SE:SE + 8],
                                          p5k[:, ES:ES + 64].rearrange("p (t e) -> p t e", t=8),
                                          axis=AX.X), deps=es_toks)
    s5 = S.op("v", lambda e: e.reciprocal(p5k[:, SC:SC + 8], p5k[:, SE:SE + 8]), deps=[s4])
    s6 = S.op("v", lambda e: e.tensor_scalar_mul(p5k[:, SC:SC + 8], p5k[:, SC:SC + 8], 4.0), deps=[s5])
    ms_toks = []
    for ti in range(8):
        m1 = S.op("v", lambda e, ti=ti: e.tensor_scalar(
            p5k[:, MS + ti * 8: MS + ti * 8 + 8], p5k[:, LG + ti * 8: LG + ti * 8 + 8],
            p5k[:, MX + ti: MX + ti + 1], None, ALU.is_ge), deps=[s6] if ti == 0 else ())
        m2 = S.op("v", lambda e, ti=ti: e.tensor_scalar(
            p5k[:, MS + ti * 8: MS + ti * 8 + 8], p5k[:, MS + ti * 8: MS + ti * 8 + 8],
            p5k[:, SC + ti: SC + ti + 1], None, ALU.mult), deps=[m1])
        ms_toks.append(m2)
    z1 = S.op("g", lambda e: e.memset(mT_b[:, :], 0.0), deps=ms_toks + [lg_mm[7], st_mm, sq_mm])
    tr_toks = []
    for ti in range(8):
        bk = pb[2 + ti // 4]
        def mt(e, ti=ti, bk=bk):
            return e.transpose(bk[0:8, (ti % 4) * 128: (ti % 4 + 1) * 128],
                               p5k[:, MS + ti * 8: MS + ti * 8 + 8], ident_f[:, :])
        tr_toks.append(S.op("t", mt, deps=[ms_toks[ti]] + (BL(bank[2 + ti // 4]) if ti % 4 == 0 else [])))
    e1 = S.op("s", lambda e: e.activation(mT_b[0:8, 0:512], pb[2][0:8, :], AF.Copy),
              deps=[tr_toks[3], z1])
    e2 = S.op("s", lambda e: e.activation(mT_b[0:8, 512:1024], pb[3][0:8, :], AF.Copy),
              deps=[tr_toks[7], z1])
    bank[2], bank[3] = e1, e2
    hm_toks = []
    OFF_W = OFF_S + 23040
    wstr_b = [sba("wstrA", [128, 2048], BF16, OFF_W),
              sba("wstrB", [128, 2048], BF16, OFF_W + 4096)]
    wa_mm = [wp_mm[0] + wp_mm[2], wp_mm[1] + wp_mm[3]]
    for mt_ in range(4):
        wstr = wstr_b[mt_ % 2]
        d_wa = S.dma("g", lambda e, mt_=mt_, wstr=wstr: e.dma_start(
            out=wstr[:, :], in_=dr["wa"][mt_].rearrange("p k c -> p (k c)")),
            deps=wa_mm[mt_ % 2])
        mhs = []
        for tt2 in range(2):
            mm = None
            for k in range(16):
                def mh(e, mt_=mt_, tt2=tt2, k=k, wstr=wstr):
                    return e.matmul(pb[4 + tt2][:, :], wstr[:, k * 128:(k + 1) * 128],
                                    x1nb[:, k * 1024 + tt2 * 512: k * 1024 + (tt2 + 1) * 512],
                                    start=(k == 0), stop=(k == 15))
                mm = S.op("t", mh, deps=([d_wa] + BL(bank[4 + tt2]) + [t[1] for t in ap_toks])
                          if k == 0 else ())
            mhs.append(mm)
            def mb(e, mt_=mt_, tt2=tt2):
                return e.matmul(pb[2][:, :], e64_s[:, mt_ * 128:(mt_ + 1) * 128],
                                mT_b[:, tt2 * 512:(tt2 + 1) * 512], start=True, stop=True)
            mmb = S.op("t", mb, deps=BL(bank[2]) + [e1, e2])
            t_mv = S.op("s", lambda e: e.activation(mvb[:, :], pb[2][:, :], AF.Copy),
                        deps=[mmb] + (hm_toks[-1:] if hm_toks else []))
            bank[2] = t_mv
            def hmop(e, mt_=mt_, tt2=tt2):
                return e.tensor_tensor(hm_sb[:, mt_ * 1024 + tt2 * 512: mt_ * 1024 + (tt2 + 1) * 512],
                                       pb[4 + tt2][:, :], mvb[:, :], ALU.mult)
            t_hm = S.op("v", hmop, deps=[mm, t_mv])
            bank[4 + tt2] = t_hm
            hm_toks.append(t_hm)
        wa_mm[mt_ % 2] = mhs
    st2_mm = sq2_mm = None
    ev2_toks = []
    bw_mm = [[hm_toks[0]], [hm_toks[0]]]
    for oc in range(16):
        wstr2 = wstr2_b[oc % 2]
        d_bw = S.dma("g", lambda e, oc=oc, wstr2=wstr2: e.dma_start(
            out=wstr2[:, :], in_=dr["wbw"][oc].rearrange("p k c -> p (k c)")),
            deps=bw_mm[oc % 2])
        mms = {}
        for tt2 in range(2):
            mm = None
            for kk in range(4):
                def my(e, oc=oc, tt2=tt2, kk=kk, wstr2=wstr2):
                    return e.matmul(pb[tt2][:, :], wstr2[:, kk * 128:(kk + 1) * 128],
                                    hm_sb[:, kk * 1024 + tt2 * 512: kk * 1024 + (tt2 + 1) * 512],
                                    start=(kk == 0), stop=(kk == 3))
                mm = S.op("t", my, deps=[d_bw] + BL(bank[tt2]) + hm_toks if (kk == 0) else ())
            mms[tt2] = mm
        bw_mm[oc % 2] = [mms[0], mms[1]]
        evs = []
        for tt2 in range(2):
            def ev(e, oc=oc, tt2=tt2):
                return e.tensor_tensor(x2p[:, oc * 1024 + tt2 * 512: oc * 1024 + (tt2 + 1) * 512],
                                       pb[tt2][:, :],
                                       x1n[:, oc * 1024 + tt2 * 512: oc * 1024 + (tt2 + 1) * 512],
                                       ALU.add)
            t_ = S.op("v", ev, deps=[mms[tt2], ap_toks[oc][0]])
            bank[tt2] = t_
            evs.append(t_)
        t_b = S.op("s", lambda e, oc=oc: e.activation(
            mT_b[:, :], x2p[:, oc * 1024:(oc + 1) * 1024], AF.Copy),
            deps=evs + ([st2_mm] if st2_mm else [e1, e2]))
        t_sq = S.op("s", lambda e: e.activation(sqb[:, :], mT_b[:, :], AF.Square),
                    deps=[t_b] + ([sq2_mm] if sq2_mm else []))
        for tt2 in range(2):
            def ms2(e, tt2=tt2, oc=oc):
                return e.matmul(pb[2 + tt2][:, :], ones_b[:, :], mT_b[:, tt2 * 512:(tt2 + 1) * 512],
                                start=(oc == 0), stop=(oc == 15))
            st2_mm = S.op("t", ms2, deps=([t_b] + ([st2_mm] if st2_mm else BL(bank[2]) + BL(bank[3])))
                          if tt2 == 0 else ())
            def mq2(e, tt2=tt2, oc=oc):
                return e.matmul(pb[4 + tt2][:, :], ones_b[:, :], sqb[:, tt2 * 512:(tt2 + 1) * 512],
                                start=(oc == 0), stop=(oc == 15))
            sq2_mm = S.op("t", mq2, deps=([t_sq] + ([sq2_mm] if sq2_mm else BL(bank[4]) + BL(bank[5])))
                          if tt2 == 0 else ())
        ev2_toks.append(list(evs))
    lnw = [t for pair in ap_toks for t in pair]
    l1 = S.op("v", lambda e: e.tensor_scalar_mul(msb[:, 0:512], pb[2][:, :], 1.0 / 2048),
              deps=[st2_mm] + lnw)
    l2 = S.op("v", lambda e: e.tensor_scalar_mul(msb[:, 512:1024], pb[3][:, :], 1.0 / 2048),
              deps=[st2_mm] + lnw)
    l3 = S.op("v", lambda e: e.tensor_scalar_mul(rsb[:, 0:512], pb[4][:, :], 1.0 / 2048),
              deps=[sq2_mm] + lnw)
    l4 = S.op("v", lambda e: e.tensor_scalar_mul(rsb[:, 512:1024], pb[5][:, :], 1.0 / 2048),
              deps=[sq2_mm] + lnw)
    l5 = S.op("v", lambda e: e.tensor_tensor(pb[2][:, :], msb[:, 0:512], msb[:, 0:512], ALU.mult),
              deps=[l1, l3])
    l6 = S.op("v", lambda e: e.tensor_tensor(pb[3][:, :], msb[:, 512:1024], msb[:, 512:1024], ALU.mult),
              deps=[l2, l4])
    l7 = S.op("v", lambda e: e.tensor_tensor(rsb[:, 0:512], rsb[:, 0:512], pb[2][:, :], ALU.subtract),
              deps=[l5])
    l8 = S.op("v", lambda e: e.tensor_tensor(rsb[:, 512:1024], rsb[:, 512:1024], pb[3][:, :], ALU.subtract),
              deps=[l6])
    l9 = S.op("v", lambda e: e.tensor_scalar_add(rsb[:, :], rsb[:, :], 1e-5), deps=[l7, l8])
    l10 = S.op("s", lambda e: e.activation(rsb[:, :], rsb[:, :], AF.Ln), deps=[l9])
    l11 = S.op("s", lambda e: e.activation(rsb[:, :], rsb[:, :], AF.Exp, scale=-0.5), deps=[l10])
    for oc in range(16):
        sl = slice(oc * 1024, (oc + 1) * 1024)
        a1 = S.op("v", lambda e, sl=sl: e.tensor_tensor(x1n[:, sl], x2p[:, sl], msb[:, :],
                                                        ALU.subtract), deps=[l11] + ev2_toks[oc])
        a2 = S.op("g" if oc % 2 else "v", lambda e, sl=sl: e.tensor_tensor(
            x1n[:, sl], x1n[:, sl], rsb[:, :], ALU.mult), deps=[a1])
        S.dma(("y", "s", "g")[oc % 3], lambda e, oc=oc, sl=sl: e.dma_start(
            out=out[oc * 128:(oc + 1) * 128, :], in_=x1n[:, sl]), deps=[a2])


def kernel(**inputs):
    per_core = _prep(inputs)
    nc, _ = build_program()
    res = run_bass_kernel_spmd(nc, per_core, core_ids=list(range(8)))
    out = np.empty((B, L, D), np.float32)
    for c in range(8):
        b, half = c // 2, c % 2
        out[b, half * TOK:(half + 1) * TOK, :] = res.results[c]["out"].T
    return out
